# version 10
# baseline (speedup 1.0000x reference)
import os
import numpy as np
import ml_dtypes
import concourse.bass as bass
import concourse.mybir as mybir
from concourse.bass_utils import run_bass_kernel_spmd

F32 = mybir.dt.float32
BF16 = mybir.dt.bfloat16
AF = mybir.ActivationFunctionType
ALU = mybir.AluOpType

NF32 = np.float32
NBF = ml_dtypes.bfloat16

B, SEQ, D = 4, 4096, 1024
HALO, OWN = 640, 2048
W = HALO + OWN
PW = 1 + W
NCH = 8
FH, DH, WIN = 8, 64, 512
HID, HIDP = 2730, 2816
NHC = HIDP // 128
BLK = 32
NEG = np.float32(-1e30)
QT0 = 512
NQT = 17
NKC = 21
NTI = 6
STAGE = int(os.environ.get("MK_STAGE", "99"))
DUMP = os.environ.get("MK_DUMP", "0") == "1"
STOP = os.environ.get("MK_STOP", "")
NSIDE = int(os.environ.get("MK_NSIDE", "2"))
NTILE = int(os.environ.get("MK_NTILE", str(NTI)))


def _bfarr(x):
    return np.ascontiguousarray(np.asarray(x, NF32).astype(NBF))


def rotary_tables(par):
    t = (np.arange(W, dtype=np.float64) + (par * OWN - HALO))
    half = np.arange(0, DH, 2, dtype=np.float64)
    inv = 1.0 / 10000.0 ** (half / DH)
    fr = np.concatenate([t[None, :] * inv[:, None]] * 2, 0)
    base = (half + 0.4 * DH) / (1.4 * DH)
    power = (t - SEQ // 2) / float(WIN // 2)
    sc = np.concatenate([base[:, None] ** power[None, :]] * 2, 0)
    cos, sin = np.cos(fr), np.sin(fr)
    csq, ssq = np.asarray(cos * sc, NF32), np.asarray(sin * sc, NF32)
    csk, ssk = np.asarray(cos / sc, NF32), np.asarray(sin / sc, NF32)
    pk = lambda a: np.concatenate([a, a], 0)
    return tuple(_bfarr(pk(a)) for a in (csq, ssq, csk, ssk))


def _allowed(par, qa, ka):
    qg = par * OWN - HALO + qa
    kg = par * OWN - HALO + ka
    return (qg - kg >= 0) & (qg - kg <= WIN) & (kg >= 0)


def maskq_blob(par):
    out = np.zeros((128, 6 * 640), NF32)
    for s in range(6):
        t = s if s < 5 else 8
        q0 = QT0 + 128 * t
        qa = q0 + np.arange(128)[:, None]
        ka = (q0 - 512) + np.arange(640)[None, :]
        out[:, 640 * s:640 * (s + 1)] = np.where(_allowed(par, qa, ka), 0.0, NEG)
    return _bfarr(out)


def kspan(c):
    return max(512, 128 * c), min(W, 128 * c + 640)


def maskk_blob(par):
    pieces, offs, pos = [], {}, 0
    for c in range(5):
        lo, hi = kspan(c)
        ka = 128 * c + np.arange(128)[:, None]
        qa = np.arange(lo, hi)[None, :]
        pieces.append(np.where(_allowed(par, qa, ka), 0.0, NEG))
        offs[c] = (pos, hi - lo)
        pos += hi - lo
    ka = np.arange(128)[:, None]
    qa = np.arange(640)[None, :]
    pieces.append(np.where((qa - ka >= 0) & (qa - ka <= WIN), 0.0, NEG))
    offs["gen"] = (pos, 640)
    pos += 640
    return _bfarr(np.concatenate(pieces, 1)), offs


MK_OFFS = maskk_blob(0)[1]


def maskk_slice(c):
    if c < 5:
        return MK_OFFS[c]
    lo, hi = kspan(c)
    g0, _ = MK_OFFS["gen"]
    return g0 + (lo - 128 * c), hi - lo


def maskv_blob():
    gv = np.arange(128)[:, None]
    i = np.arange(SEQ)[None, :]
    return _bfarr(np.where(gv > (i // BLK), 0.0, NEG))


def a_kaug_rows():
    j = np.arange(SEQ)[None, :]
    g = np.arange(16)[:, None]
    return _bfarr(np.where(((j % 512) // BLK) == g, NEG, 0.0))


def a_qaug_rows():
    i = np.arange(512)[None, :]
    g = np.arange(16)[:, None]
    return _bfarr((g > (i // BLK)).astype(NF32))


def blkind_blob():
    out = np.zeros((128, 32 * 128), NF32)
    t = np.arange(128)[:, None]
    g = np.arange(128)[None, :]
    for tc in range(32):
        out[:, 128 * tc:128 * (tc + 1)] = (g == (4 * tc + t // 32)).astype(NF32)
    return _bfarr(out)


CBF_R2 = 0
CBF_ID = 128
CBF_HS33 = 256
CBF_SEL33 = 289
CBF_ONES1 = 417
CBF_ONESROW = 418
CBF_ONES64 = 546
CBF_W = 674


def const_bf():
    R2 = np.zeros((128, 128), NF32)
    for hh in range(2):
        for d in range(32):
            R2[64 * hh + d, 64 * hh + d + 32] = -1.0
            R2[64 * hh + d + 32, 64 * hh + d] = 1.0
    R2T = R2.T.copy()
    ident = np.eye(128, dtype=NF32)
    hs33 = np.zeros((128, 33), NF32)
    hs33[0:64, 0] = 1.0
    hs33[64:128, 32] = 1.0
    sel33 = np.zeros((128, 128), NF32)
    sel33[0, 0:64] = 1.0
    sel33[32, 64:128] = 1.0
    ones1 = np.ones((128, 1), NF32)
    onesrow = np.zeros((128, 128), NF32)
    onesrow[0, :] = 1.0
    ones64 = np.zeros((128, 128), NF32)
    ones64[64, :] = 1.0
    return _bfarr(np.concatenate([R2T, ident, hs33, sel33, ones1, onesrow, ones64], 1))


CF_ID = 0
CF_NEG8 = 128
CF_ONESROW = 256
CF_EPS = 384
CF_EPS24 = 385
CF_W = 386


def const_f32():
    ident = np.eye(128, dtype=NF32)
    neg8 = (-8.0 * np.eye(128)).astype(NF32)
    onesrow = np.zeros((128, 128), NF32)
    onesrow[0, :] = 1.0
    onesrow[64, :] = 1.0
    eps = np.full((128, 1), 1e-5, NF32)
    eps24 = np.full((128, 1), 1e-24, NF32)
    return np.ascontiguousarray(np.concatenate([ident, neg8, onesrow, eps, eps24], 1))


VL = 70
V_QS, V_KS, V_LG, V_LB, V_B2, V_B1A, V_B1G = 0, 1, 2, 10, 18, 26, 48
V_BO, V_BQ, V_BK, V_BV, V_HV = 140, 148, 152, 156, 160
VW = 161


def vec_blob(inputs, par):
    cols = []

    def pp(v, width):
        return np.asarray(v, NF32).reshape(width, 128).T

    qs = np.asarray(inputs["L0_ffn_q_scale"], NF32)
    ks = np.asarray(inputs["L0_ffn_k_scale"], NF32)
    lg = np.asarray(inputs["L0_ffn_ln_g"], NF32)
    lb = np.asarray(inputs["L0_ffn_ln_b"], NF32)
    b2 = np.asarray(inputs["L0_ffn_b2"], NF32)
    b1 = np.asarray(inputs["L0_ffn_b1"], NF32)
    for l in range(2):
        cols.append(np.concatenate([qs[l], qs[l]]).reshape(128, 1))
        cols.append(np.concatenate([ks[l], ks[l]]).reshape(128, 1))
        cols.append(pp(lg[l], 8))
        cols.append(pp(lb[l], 8))
        cols.append(pp(b2[l], 8))
        b1a = np.zeros(HIDP, NF32)
        b1g = np.zeros(HIDP, NF32)
        b1a[:HID] = b1[l, :HID]
        b1g[:HID] = b1[l, HID:]
        cols.append(pp(b1a, NHC))
        cols.append(pp(b1g, NHC))
    hsl = slice(par * 512, par * 512 + 512)
    cols.append(pp(np.asarray(inputs["L0_attn_bo"], NF32), 8))
    cols.append(pp(np.asarray(inputs["L0_attn_bq"], NF32)[hsl], 4))
    cols.append(pp(np.asarray(inputs["L0_attn_bk"], NF32)[hsl], 4))
    cols.append(pp(np.asarray(inputs["L0_attn_bv"], NF32)[hsl], 4))
    cols.append(np.full((128, 1), 1.0 if par == 1 else 0.0, NF32))
    return np.ascontiguousarray(np.concatenate(cols, 1))


def _b16(a):
    return np.ascontiguousarray(np.asarray(a, NF32).astype(ml_dtypes.bfloat16))


def core_inputs(inputs, core):
    b, par = core // 2, core % 2
    x0 = np.asarray(inputs["L0_attn_hidden_states"], NF32)
    xb = np.ascontiguousarray(x0[b].T)
    x0w = np.zeros((D, PW), NF32)
    lo = par * OWN - HALO
    s0 = max(0, lo)
    x0w[:, 1 + (s0 - lo):] = xb[:, s0:par * OWN + OWN]
    w1 = np.asarray(inputs["L0_ffn_w1"], NF32)
    w1a = np.zeros((2, D, HIDP), NF32)
    w1g = np.zeros((2, D, HIDP), NF32)
    w1a[:, :, :HID] = w1[:, :, :HID]
    w1g[:, :, :HID] = w1[:, :, HID:]
    w2 = np.asarray(inputs["L0_ffn_w2"], NF32)
    w2p = np.zeros((2, HIDP, D), NF32)
    w2p[:, :HID, :] = w2
    hsl = slice(par * 512, par * 512 + 512)
    csq, ssq, csk, ssk = rotary_tables(par)
    return {
        "x0w": x0w,
        "wqkv": _b16(np.asarray(inputs["L0_ffn_wqkv"], NF32)),
        "wo": _b16(np.asarray(inputs["L0_ffn_wo"], NF32)),
        "w1a": _b16(w1a),
        "w1g": _b16(w1g),
        "w2p": _b16(w2p),
        "wqT": _b16(np.asarray(inputs["L0_attn_Wq"], NF32).T[:, hsl]),
        "wkT": _b16(np.asarray(inputs["L0_attn_Wk"], NF32).T[:, hsl]),
        "wvT": _b16(np.asarray(inputs["L0_attn_Wv"], NF32).T[:, hsl]),
        "woT": _b16(np.asarray(inputs["L0_attn_Wo"], NF32).T[hsl, :]),
        "bv512": _b16(np.asarray(inputs["L0_attn_bv"], NF32)[hsl].reshape(1, 512)),
        "csq": csq, "ssq": ssq, "csk": csk, "ssk": ssk,
        "maskq": maskq_blob(par), "maskk": maskk_blob(par)[0],
        "maskv": maskv_blob(), "kaugc": a_kaug_rows(), "qaugc": a_qaug_rows(),
        "blkind": blkind_blob(), "cbf": const_bf(), "cf32": const_f32(),
        "vecs": vec_blob(inputs, par),
    }


class MK:
    def __init__(self):
        self.nc = bass.Bass("TRN2", target_bir_lowering=False)
        self.ctx = []
        self.sems = {}
        self.waited = {}
        self.namecount = {}

    def enter(self, cm):
        v = cm.__enter__()
        self.ctx.append(cm)
        return v

    ARENA = 212736

    def _uniq(self, name):
        n = self.namecount.get(name, 0)
        self.namecount[name] = n + 1
        return name if n == 0 else f"{name}_{n}"

    def init_arena(self):
        self.arena_h = self.enter(self.nc.sbuf_tensor("arena", [128, self.ARENA], mybir.dt.uint8))
        self.arena_base = self.nc.lookup_mloc(self.arena_h).addr
        self.aptr = 0
        self.apeak = 0

    def sbuf(self, name, shape, dt):
        per = 1
        for s in shape[1:]:
            per *= s
        per *= mybir.dt.size(dt)
        addr = (self.aptr + 31) & ~31
        assert addr + per <= self.ARENA, f"arena overflow at {name}: {addr+per} > {self.ARENA}"
        self.aptr = addr + per
        self.apeak = max(self.apeak, self.aptr)
        return self.nc.alloc_sbuf_tensor_at(name, shape, dt,
                                            offset=self.arena_base + addr)

    def psum(self, name, shape, dt=F32):
        return self.enter(self.nc.psum_tensor(self._uniq(name), shape, dt))

    def scope(self):
        return (len(self.ctx), self.aptr)

    def end_scope(self, mark):
        n, ap = mark
        while len(self.ctx) > n:
            self.ctx.pop().__exit__(None, None, None)
        self.aptr = ap

    def sem(self, name):
        if name not in self.sems:
            h = self.enter(self.nc.semaphore(name))
            self.sems[name] = [h, 0, name]
        return self.sems[name]

    def inc(self, instr, s, k=1):
        instr.then_inc(s[0], k)
        s[1] += k
        return (s, s[1])

    def wait(self, eng, engname, *tickets):
        for tk in tickets:
            if tk is None:
                continue
            s, v = tk
            key = (engname, s[2])
            if self.waited.get(key, -1) < v:
                eng.wait_ge(s[0], v)
                self.waited[key] = v

    def lk(self, e, en, instr):
        tk = self.inc(instr, self.sem(en))
        self.wait(e, en, tk)

    def barrier_from(self, eng, engname, *semnames):
        for sn in semnames:
            s = self.sem(sn)
            if s[1] > 0:
                self.wait(eng, engname, (s, s[1]))

    def _run(self, sec, fn, en):
        out = []
        sec(lambda e: out.append(fn(e, en)))
        return out[0]

    def T(self, fn):
        return self._run(self.blk.tensor, fn, "pe")

    def V(self, fn):
        return self._run(self.blk.vector, fn, "dve")

    def C(self, fn):
        return self._run(self.blk.scalar, fn, "act")

    def G(self, fn):
        return self._run(self.blk.gpsimd, fn, "pool")

    def Y(self, fn):
        return self._run(self.blk.sync, fn, "sp")

    def full_barrier(self):
        def mk(e, en):
            self.barrier_from(e, en, *list(self.sems.keys()))
        self.T(mk)
        self.V(mk)
        self.C(mk)
        self.G(mk)
        self.Y(mk)

    def build(self):
        nc = self.nc
        ap = lambda t: t.ap()
        self.d_x0w = ap(nc.dram_tensor("x0w", [D, PW], F32, kind="ExternalInput"))
        self.d_wqkv = ap(nc.dram_tensor("wqkv", [2, D, 1536], BF16, kind="ExternalInput"))
        self.d_wo = ap(nc.dram_tensor("wo", [2, 512, D], BF16, kind="ExternalInput"))
        self.d_w1a = ap(nc.dram_tensor("w1a", [2, D, HIDP], BF16, kind="ExternalInput"))
        self.d_w1g = ap(nc.dram_tensor("w1g", [2, D, HIDP], BF16, kind="ExternalInput"))
        self.d_w2p = ap(nc.dram_tensor("w2p", [2, HIDP, D], BF16, kind="ExternalInput"))
        self.d_wqT = ap(nc.dram_tensor("wqT", [D, 512], BF16, kind="ExternalInput"))
        self.d_wkT = ap(nc.dram_tensor("wkT", [D, 512], BF16, kind="ExternalInput"))
        self.d_wvT = ap(nc.dram_tensor("wvT", [D, 512], BF16, kind="ExternalInput"))
        self.d_woT = ap(nc.dram_tensor("woT", [512, D], BF16, kind="ExternalInput"))
        self.d_bv512 = ap(nc.dram_tensor("bv512", [1, 512], BF16, kind="ExternalInput"))
        for n in ("csq", "ssq", "csk", "ssk"):
            setattr(self, "d_" + n, ap(nc.dram_tensor(n, [128, W], BF16, kind="ExternalInput")))
        self.d_maskq = ap(nc.dram_tensor("maskq", [128, 6 * 640], BF16, kind="ExternalInput"))
        mkw = maskk_blob(0)[0].shape[1]
        self.d_maskk = ap(nc.dram_tensor("maskk", [128, mkw], BF16, kind="ExternalInput"))
        self.d_maskv = ap(nc.dram_tensor("maskv", [128, SEQ], BF16, kind="ExternalInput"))
        self.d_kaugc = ap(nc.dram_tensor("kaugc", [16, SEQ], BF16, kind="ExternalInput"))
        self.d_qaugc = ap(nc.dram_tensor("qaugc", [16, 512], BF16, kind="ExternalInput"))
        self.d_blkind = ap(nc.dram_tensor("blkind", [128, 32 * 128], BF16, kind="ExternalInput"))
        self.d_cbf = ap(nc.dram_tensor("cbf", [128, CBF_W], BF16, kind="ExternalInput"))
        self.d_cf32 = ap(nc.dram_tensor("cf32", [128, CF_W], F32, kind="ExternalInput"))
        self.d_vecs = ap(nc.dram_tensor("vecs", [128, VW], F32, kind="ExternalInput"))
        self.d_y = ap(nc.dram_tensor("y", [D, OWN], F32, kind="ExternalOutput"))
        if DUMP:
            self.d_dbg = [ap(nc.dram_tensor(f"dbg{f}", [128, PW], F32, kind="ExternalOutput"))
                          for f in range(NCH)]
        self.d_x1 = ap(nc.dram_tensor("x1d", [D, OWN], F32, kind="Internal"))
        self.d_x2 = ap(nc.dram_tensor("x2d", [D, OWN], F32, kind="Internal"))
        self.e1_in = ap(nc.dram_tensor("e1_in", [D, 514], F32, kind="Internal"))
        self.e1_out = ap(nc.dram_tensor("e1_out", [2 * D, 514], F32, kind="Internal"))
        self.e2a_in = ap(nc.dram_tensor("e2a_in", [D, OWN], BF16, kind="Internal"))
        self.e2a_out = ap(nc.dram_tensor("e2a_out", [2 * D, OWN], BF16, kind="Internal"))
        self.e2bi = [ap(nc.dram_tensor(f"e2bi{j}", [D, 1024], F32, kind="Internal"))
                     for j in range(4)]
        self.e2bo = [ap(nc.dram_tensor(f"e2bo{j}", [D, 1024], F32, kind="Internal"))
                     for j in range(4)]
        ik = "ExternalOutput" if DUMP else "Internal"
        self.vdram = ap(nc.dram_tensor("vdram", [NKC * 128, 512], BF16, kind=ik))
        self.cxdram = ap(nc.dram_tensor("cxdram", [512, W - 512], BF16, kind=ik))
        self.e3_in = ap(nc.dram_tensor("e3_in", [D, 514], F32, kind="Internal"))
        self.e3_out = ap(nc.dram_tensor("e3_out", [2 * D, 514], F32, kind="Internal"))
        self.rg = [[0, 1], [2, 3], [4, 5], [6, 7]]

        self.init_arena()
        self.cbf = self.sbuf("cbf_sb", [128, CBF_W], BF16)
        self.cf32 = self.sbuf("cf32_sb", [128, CF_W], F32)
        self.vecs = self.sbuf("vecs_sb", [128, VW], F32)
        self.qs8 = self.sbuf("qs8", [128, 2], F32)
        self.bq8 = self.sbuf("bq8", [128, 4], F32)
        self.hvi = self.sbuf("hvi", [128, 1], F32)
        self.mq = self.sbuf("mq_sb", [128, 6 * 640], BF16)
        self.mkb = self.sbuf("mk_sb", [128, mkw], BF16)
        for sn in ("pe", "dve", "act", "pool", "cc",
                   "cld", "xld", "wvld", "vst0", "vst1", "vhld", "wqld", "tbld",
                   "cxst0", "cxst1", "cxld0", "cxld1", "wold", "w1ld0", "w1ld1",
                   "w2ld0", "w2ld1", "exst", "exld",
                   "x1st", "e2ast", "ald", "awld", "akld", "aqld", "bkld", "qmst",
                   "e2bst0", "e2bst1", "x2ld", "x2st", "yst", "dbgst", "dbgst2"):
            self.sem(sn)

        with self.nc.Block() as blk:
            self.blk = blk
            self.ph_consts()
            mark_sa = self.scope()
            self.S = [self.sbuf(f"Sa{f}", [128, PW], F32) for f in range(NCH)]
            self.s_init(0)
            self.ph_y1()
            if STAGE >= 2:
                self.f_call(first=True)
            if STAGE >= 5:
                self.ph_x1y2()
            if DUMP and STAGE < 6:
                self.ph_dump()
            if STAGE >= 6:
                self.full_barrier()
                self.end_scope(mark_sa)
                self.ph_A()
                mark_sb = self.scope()
                self.S = [self.sbuf(f"Sb{f}", [128, PW], F32) for f in range(NCH)]
                self.s_init(641)
                self.ph_x2y3()
                if STAGE >= 7:
                    self.f_call(first=False)
                    self.ph_out()
                if DUMP:
                    self.ph_dump()
                self.full_barrier()
                self.end_scope(mark_sb)
            elif STAGE >= 7 or not DUMP:
                self.ph_out()
            self.ph_fin()
        while self.ctx:
            self.ctx.pop().__exit__(None, None, None)
        self.arena_peak = self.apeak
        return self.nc

    def ph_consts(self):
        sld, sdve = self.sem("cld"), self.sem("dve")

        def loads(y, en):
            tk = None
            for dst, src in ((self.cbf, self.d_cbf), (self.cf32, self.d_cf32),
                             (self.vecs, self.d_vecs), (self.mq, self.d_maskq),
                             (self.mkb, self.d_maskk)):
                tk = self.inc(y.dma_start(out=dst[:], in_=src), sld, 16)
            return tk
        tk = self.Y(loads)

        def prep(v, en):
            self.wait(v, en, tk)
            for l in range(2):
                v.tensor_scalar_mul(self.qs8[:, l:l + 1],
                                    self.vecs[:, VL * l + V_QS:VL * l + V_QS + 1], 8.0)
            v.tensor_scalar_mul(self.bq8[:], self.vecs[:, V_BQ:V_BQ + 4], 0.125)
            i = v.tensor_scalar(out=self.hvi[:], in0=self.vecs[:, V_HV:V_HV + 1],
                                scalar1=-1.0, scalar2=1.0, op0=ALU.mult, op1=ALU.add)
            return self.inc(i, sdve)
        self.V(prep)

    def s_init(self, upto):
        def z(v, en):
            i = None
            for f in range(NCH):
                i = v.memset(self.S[f][:, 0:max(1, upto)], 0.0)
            return self.inc(i, self.sem("dve"))
        self.V(z)

    def mk_ln_pools(self, tag):
        return {
            "sq": [self.sbuf(f"{tag}sq{f}", [128, 512], BF16) for f in range(NCH)],
            "mu": self.sbuf(f"{tag}mu", [1, 512], F32),
            "ex2": self.sbuf(f"{tag}e2", [1, 512], F32),
            "var": self.sbuf(f"{tag}va", [1, 512], F32),
            "sd": self.sbuf(f"{tag}sd", [1, 512], F32),
            "rstd": self.sbuf(f"{tag}rs", [1, 512], F32),
            "rbs": self.sbuf(f"{tag}rb", [128, 512], F32),
            "sx": self.psum(f"{tag}px", [1, 512]),
            "sq2": self.psum(f"{tag}pq", [1, 512]),
            "mub": self.psum(f"{tag}pm", [128, 512]),
            "rbp": self.psum(f"{tag}pr", [128, 512]),
            "war": {},
        }

    def ln_stats(self, xb_tiles, Tt, p, tk_in):
        spe, sdve, sact = self.sem("pe"), self.sem("dve"), self.sem("act")
        spool = self.sem("pool")

        def sqf(g, en):
            self.wait(g, en, tk_in, p["war"].get("sq"))
            i = None
            for f in range(NCH):
                i = g.tensor_mul(p["sq"][f][:, 0:Tt], xb_tiles[f][:, 0:Tt],
                                 xb_tiles[f][:, 0:Tt])
            return self.inc(i, spool)
        tk_sq = self.G(sqf)

        def mm(t, en):
            self.wait(t, en, tk_sq, p["war"].get("sx"))
            for f in range(NCH):
                t.matmul(p["sx"][0:1, 0:Tt], self.cbf[:, CBF_ONES1:CBF_ONES1 + 1],
                         xb_tiles[f][:, 0:Tt], start=(f == 0), stop=(f == NCH - 1))
            i = None
            for f in range(NCH):
                i = t.matmul(p["sq2"][0:1, 0:Tt], self.cbf[:, CBF_ONES1:CBF_ONES1 + 1],
                             p["sq"][f][:, 0:Tt], start=(f == 0), stop=(f == NCH - 1))
            return self.inc(i, spe)
        tk_mm = self.T(mm)
        p["war"]["sq"] = tk_mm

        def strow(v, en):
            self.wait(v, en, tk_mm)
            i = v.tensor_scalar_mul(p["mu"][:, 0:Tt], p["sx"][0:1, 0:Tt], 1.0 / D)
            self.lk(v, en, i)
            v.tensor_scalar_mul(p["ex2"][:, 0:Tt], p["sq2"][0:1, 0:Tt], 1.0 / D)
            i = v.tensor_mul(p["var"][:, 0:Tt], p["mu"][:, 0:Tt], p["mu"][:, 0:Tt])
            self.lk(v, en, i)
            i = v.tensor_sub(p["var"][:, 0:Tt], p["ex2"][:, 0:Tt], p["var"][:, 0:Tt])
            return self.inc(i, sdve)
        tk_st = self.V(strow)
        p["war"]["sx"] = tk_st

        def sdf(c, en):
            self.wait(c, en, tk_st)
            i = c.activation(out=p["sd"][:, 0:Tt], in_=p["var"][:, 0:Tt], func=AF.Ln,
                             bias=self.cf32[0:1, CF_EPS:CF_EPS + 1])
            self.lk(c, en, i)
            i = c.activation(out=p["rstd"][:, 0:Tt], in_=p["sd"][:, 0:Tt],
                             func=AF.Exp, scale=-0.5)
            return self.inc(i, sact)
        tk_rs = self.C(sdf)

        def bc(t, en):
            self.wait(t, en, tk_rs, p["war"].get("mub"), p["war"].get("rbp"))
            t.matmul(p["mub"][:, 0:Tt], self.cf32[0:1, CF_ONESROW:CF_ONESROW + 128],
                     p["mu"][:, 0:Tt], start=True, stop=True)
            i = t.matmul(p["rbp"][:, 0:Tt], self.cf32[0:1, CF_ONESROW:CF_ONESROW + 128],
                         p["rstd"][:, 0:Tt], start=True, stop=True)
            return self.inc(i, spe)
        tk_bc = self.T(bc)

        def rb(v, en):
            self.wait(v, en, tk_bc)
            i = v.tensor_copy(p["rbs"][:, 0:Tt], p["rbp"][:, 0:Tt])
            return self.inc(i, sdve)
        tk_rb = self.V(rb)
        p["war"]["rbp"] = tk_rb
        return tk_bc, tk_rb

    def ph_y1(self):
        mark = self.scope()
        sld, sdve = self.sem("xld"), self.sem("dve")
        xt = [self.sbuf(f"p1x{f}", [128, 512], F32) for f in range(NCH)]
        xb = [self.sbuf(f"p1b{f}", [128, 512], BF16) for f in range(NCH)]
        t1 = [self.sbuf(f"p1t{f}", [128, 512], F32) for f in range(NCH)]
        p = self.mk_ln_pools("p1")
        war_x = None
        for ti in range(NTI):
            a0 = 1 + 512 * ti
            Tt = min(512, PW - a0)

            def ld(y, en):
                self.wait(y, en, war_x)
                i = None
                for f in range(NCH):
                    i = self.inc(y.dma_start(out=xt[f][:, 0:Tt],
                                             in_=self.d_x0w[128 * f:128 * f + 128, a0:a0 + Tt]),
                                 sld, 16)
                return i
            tk_ld = self.Y(ld)

            def cast(v, en):
                self.wait(v, en, tk_ld)
                i = None
                for f in range(NCH):
                    i = v.tensor_copy(xb[f][:, 0:Tt], xt[f][:, 0:Tt])
                return self.inc(i, sdve)
            tk_cast = self.V(cast)
            tk_mub, tk_rb = self.ln_stats(xb, Tt, p, tk_cast)

            def app(v, en):
                self.wait(v, en, tk_rb)
                i = None
                for f in range(NCH):
                    i = v.tensor_sub(t1[f][:, 0:Tt], xt[f][:, 0:Tt], p["mub"][:, 0:Tt])
                self.lk(v, en, i)
                for f in range(NCH):
                    i = v.tensor_mul(self.S[f][:, a0:a0 + Tt], t1[f][:, 0:Tt], p["rbs"][:, 0:Tt])
                return self.inc(i, sdve)
            tk_ap = self.V(app)
            war_x = tk_ap
            p["war"]["mub"] = tk_ap
        self.full_barrier()
        self.end_scope(mark)

    def cbar(self):
        self.full_barrier()

    def f_call(self, first):
        maxil = 2
        if first and (STAGE == 2 or STOP):
            maxil = 1
        for l in range(maxil):
            self.f_il(l)
            if l == 0 and not STOP:
                self.exchange_halo(self.e1_in, self.e1_out)

    def exchange_halo(self, din, dout):
        mark = self.scope()
        sst, scc, sld, sdve = (self.sem(x) for x in ("exst", "cc", "exld", "dve"))
        hr = [self.sbuf(f"hr{f}", [128, 514], F32) for f in range(NCH)]

        def snd(y, en):
            self.barrier_from(y, en, "dve")
            i = None
            for f in range(NCH):
                i = self.inc(y.dma_start(out=din[128 * f:128 * f + 128, :],
                                         in_=self.S[f][:, 2175:2689]), sst, 16)
            return i
        tk = self.Y(snd)

        def coll(g, en):
            self.wait(g, en, tk)
            return self.inc(g.collective_compute(
                "AllGather", ALU.bypass, replica_groups=self.rg,
                ins=[din.opt()], outs=[dout.opt()]), scc, 1)
        tkc = self.G(coll)

        def rcv(y, en):
            self.wait(y, en, tkc)
            i = None
            for f in range(NCH):
                i = self.inc(y.dma_start(out=hr[f][:], in_=dout[128 * f:128 * f + 128, :]),
                             sld, 16)
            return i
        tkr = self.Y(rcv)

        def wr(v, en):
            self.wait(v, en, tkr)
            i = None
            for f in range(NCH):
                i = v.tensor_scalar_mul(self.S[f][:, 127:641], hr[f][:],
                                        self.vecs[:, V_HV:V_HV + 1])
            return self.inc(i, sdve)
        self.V(wr)
        self.full_barrier()
        self.end_scope(mark)

    def f_il(self, l):
        spe, sdve, sact, spool = (self.sem(x) for x in ("pe", "dve", "act", "pool"))
        mark_il = self.scope()
        qaug = [self.sbuf(f"qaug{h}", [65, W], BF16) for h in range(2)]
        kaug = [self.sbuf(f"kaug{h}", [65, W], BF16) for h in range(2)]

        mark_sb = self.scope()
        Sb = [self.sbuf(f"Sh{f}", [128, PW], BF16) for f in range(NCH)]

        def cast(v, en):
            self.barrier_from(v, en, *list(self.sems.keys()))
            for f in range(NCH):
                v.tensor_copy(Sb[f][:], self.S[f][:])
            v.memset(kaug[0][64:65, :], 1.0)
            v.memset(kaug[1][64:65, :], 1.0)
            v.memset(qaug[0][64:65, :], 0.0)
            i = v.memset(qaug[1][64:65, :], 0.0)
            return self.inc(i, sdve)
        tk_cast = self.V(cast)

        def xsh(f, a0, Tt):
            off = 1 if f < 4 else 0
            return Sb[f][:, off + a0:off + a0 + Tt]

        mark_v = self.scope()
        wv = [self.sbuf(f"wv{f}", [128, 512], BF16) for f in range(NCH)]

        def ldwv(g, en):
            i = None
            for f in range(NCH):
                i = self.inc(g.dma_start(out=wv[f][:],
                                         in_=self.d_wqkv[l, 128 * f:128 * f + 128, 1024:1536]),
                             self.sem("wvld"), 16)
            return i
        tk_wv = self.Y(ldwv)
        vstg = [self.sbuf(f"vst{i}", [128, 512], BF16) for i in range(2)]
        vps = [self.psum(f"vps{i}", [128, 512]) for i in range(2)]
        warv = [None, None]
        tk_vst = None
        for ts in range(NKC):
            a0, bi = 128 * ts, ts % 2

            def vmm(t, en, a0=a0, bi=bi):
                self.wait(t, en, tk_cast, tk_wv)
                if warv[bi] is not None:
                    self.wait(t, en, warv[bi])
                i = None
                for f in range(NCH):
                    i = t.matmul(vps[bi][:], xsh(f, a0, 128), wv[f][:],
                                 start=(f == 0), stop=(f == NCH - 1))
                return self.inc(i, spe)
            tkv = self.T(vmm)

            def vev(v, en, bi=bi):
                self.wait(v, en, tkv, warv[bi])
                return self.inc(v.tensor_copy(vstg[bi][:], vps[bi][:]), sdve)
            tke = self.V(vev)

            def vst(y, en, bi=bi, ts=ts):
                self.wait(y, en, tke)
                return self.inc(y.dma_start(out=self.vdram[128 * ts:128 * ts + 128, :],
                                            in_=vstg[bi][:]), self.sem(f"vst{bi}"), 16)
            warv[bi] = self.Y(vst)
        tk_vblob = (warv[0], warv[1])
        self.cbar()
        self.end_scope(mark_v)

        nhp = 1 if STOP == "i" else 4
        tcs = self.sbuf("tcs", [128, W], BF16)
        tss = self.sbuf("tss", [128, W], BF16)
        wqkq = self.sbuf("wqkq", [128, 1024], BF16)
        wqkk = self.sbuf("wqkk", [128, 1024], BF16)
        ilwar = {}
        for hp in range(nhp):
            mark_hp = self.scope()

            def ldw(g, en, hp=hp):
                self.wait(g, en, ilwar.get("wqk"))
                self.inc(g.dma_start(
                    out=wqkq[:].rearrange("p (c n) -> p c n", c=8),
                    in_=self.d_wqkv[l].rearrange("(c p) n -> p c n", p=128)[:, :, 128 * hp:128 * hp + 128]),
                    self.sem("wqld"), 16)
                return self.inc(g.dma_start(
                    out=wqkk[:].rearrange("p (c n) -> p c n", c=8),
                    in_=self.d_wqkv[l].rearrange("(c p) n -> p c n", p=128)[:, :, 512 + 128 * hp:512 + 128 * hp + 128]),
                    self.sem("wqld"), 16)
            tk_wqk = self.Y(ldw)

            mark_i = self.scope()
            pps = [self.psum(f"ipp{i}", [128, 512]) for i in range(2)]
            nps = [self.psum(f"inp{i}", [33, 512]) for i in range(2)]
            bps = [self.psum(f"ibp{i}", [128, 512]) for i in range(2)]
            rps = [self.psum(f"irp{i}", [128, 512]) for i in range(2)]
            sqs = [self.sbuf(f"isq{i}", [128, 512], BF16) for i in range(2)]
            nrm = [self.sbuf(f"inr{i}", [33, 512], F32) for i in range(2)]
            nrr = [self.sbuf(f"inrr{i}", [33, 512], BF16) for i in range(2)]
            bcb = [self.sbuf(f"ibc{i}", [128, 512], F32) for i in range(2)]
            qn = [self.sbuf(f"iqn{i}", [128, 512], BF16) for i in range(2)]
            rawb = [self.sbuf(f"irw{i}", [128, 512], F32) for i in range(2)]
            qcs = [self.sbuf(f"iqc{i}", [128, 512], BF16) for i in range(2)]
            qss = [self.sbuf(f"iqs{i}", [128, 512], BF16) for i in range(2)]
            war = {}
            self.cbar()
            tk10 = None
            for side in range(NSIDE):
                wsb = wqkq if side == 0 else wqkk
                dst = qaug if side == 0 else kaug

                def ldt(g, en, side=side):
                    self.wait(g, en, ilwar.get("tabs"))
                    srcs = (self.d_csq, self.d_ssq) if side == 0 else (self.d_csk, self.d_ssk)
                    self.inc(g.dma_start(out=tcs[:], in_=srcs[0]), self.sem("tbld"), 16)
                    return self.inc(g.dma_start(out=tss[:], in_=srcs[1]), self.sem("tbld"), 16)
                tk_tab = self.Y(ldt)

                for ti in range(NTILE):
                    a0 = 512 * ti
                    Tt = min(512, W - a0)
                    bi = ti % 2

                    def qmm(t, en, a0=a0, Tt=Tt, bi=bi, wsb=wsb):
                        self.wait(t, en, tk_cast, tk_wqk, war.get(("pp", bi)))
                        i = None
                        for f in range(NCH):
                            i = t.matmul(pps[bi][:, 0:Tt], wsb[:, 128 * f:128 * f + 128],
                                         xsh(f, a0, Tt), start=(f == 0),
                                         stop=(f == NCH - 1))
                        return self.inc(i, spe)
                    tk1 = self.T(qmm)

                    def rawev(v, en, Tt=Tt, bi=bi):
                        self.wait(v, en, tk1, war.get(("raw", bi)))
                        return self.inc(v.tensor_copy(rawb[bi][:, 0:Tt],
                                                      pps[bi][:, 0:Tt]), sdve)
                    tk1b = self.V(rawev)
                    war[("pp", bi)] = tk1b

                    def sqf(g_, en, Tt=Tt, bi=bi):
                        self.wait(g_, en, tk1b, war.get(("sqs", bi)))
                        return self.inc(g_.tensor_mul(sqs[bi][:, 0:Tt], rawb[bi][:, 0:Tt],
                                                      rawb[bi][:, 0:Tt]), spool)
                    tk2 = self.G(sqf)

                    def nmm(t, en, Tt=Tt, bi=bi):
                        self.wait(t, en, tk2, war.get(("np", bi)))
                        return self.inc(t.matmul(nps[bi][0:33, 0:Tt],
                                                 self.cbf[:, CBF_HS33:CBF_HS33 + 33],
                                                 sqs[bi][:, 0:Tt], start=True, stop=True), spe)
                    tk3 = self.T(nmm)
                    war[("sqs", bi)] = tk3

                    def sqrtf(c, en, Tt=Tt, bi=bi):
                        self.wait(c, en, tk3, war.get(("nrm", bi)))
                        i = c.activation(out=nrm[bi][:, 0:Tt], in_=nps[bi][0:33, 0:Tt],
                                         func=AF.Ln,
                                         bias=self.cf32[0:33, CF_EPS24:CF_EPS24 + 1])
                        self.lk(c, en, i)
                        return self.inc(c.activation(out=nrm[bi][:, 0:Tt], in_=nrm[bi][:, 0:Tt],
                                                     func=AF.Exp, scale=-0.5), sact)
                    tk4 = self.C(sqrtf)
                    war[("np", bi)] = tk4

                    def recf(g_, en, Tt=Tt, bi=bi):
                        self.wait(g_, en, tk4, war.get(("nrr", bi)))
                        i = g_.tensor_scalar_min(nrr[bi][:, 0:Tt], nrm[bi][:, 0:Tt], 1e12)
                        return self.inc(i, spool)
                    tk5 = self.G(recf)
                    war[("nrm", bi)] = tk5

                    def bcf(t, en, Tt=Tt, bi=bi):
                        self.wait(t, en, tk5, war.get(("bp", bi)))
                        return self.inc(t.matmul(bps[bi][:, 0:Tt],
                                                 self.cbf[0:33, CBF_SEL33:CBF_SEL33 + 128],
                                                 nrr[bi][:, 0:Tt], start=True, stop=True), spe)
                    tk6 = self.T(bcf)
                    war[("nrr", bi)] = tk6

                    def bcc(v, en, Tt=Tt, bi=bi):
                        self.wait(v, en, tk6, war.get(("bcb", bi)))
                        return self.inc(v.tensor_copy(bcb[bi][:, 0:Tt], bps[bi][:, 0:Tt]),
                                        sdve)
                    tk7 = self.V(bcc)
                    war[("bp", bi)] = tk7

                    def stf(v, en, Tt=Tt, bi=bi, side=side):
                        self.wait(v, en, tk7, war.get(("qn", bi)))
                        sc = self.qs8[:, l:l + 1] if side == 0 else \
                            self.vecs[:, VL * l + V_KS:VL * l + V_KS + 1]
                        i = v.scalar_tensor_tensor(out=qn[bi][:, 0:Tt], in0=rawb[bi][:, 0:Tt],
                                                   scalar=sc, in1=bcb[bi][:, 0:Tt],
                                                   op0=ALU.mult, op1=ALU.mult)
                        return self.inc(i, sdve)
                    tk8a = self.V(stf)
                    war[("raw", bi)] = tk8a
                    war[("bcb", bi)] = tk8a

                    def csf(g_, en, Tt=Tt, bi=bi, a0=a0):
                        self.wait(g_, en, tk8a, tk_tab, war.get(("qn", bi)))
                        g_.tensor_mul(qcs[bi][:, 0:Tt], qn[bi][:, 0:Tt], tcs[:, a0:a0 + Tt])
                        return self.inc(g_.tensor_mul(qss[bi][:, 0:Tt], qn[bi][:, 0:Tt],
                                                      tss[:, a0:a0 + Tt]), spool)
                    tk8 = self.G(csf)
                    ilwar["tabs"] = tk8

                    def rotf(t, en, Tt=Tt, bi=bi):
                        self.wait(t, en, tk8, war.get(("rp", bi)))
                        return self.inc(t.matmul(rps[bi][:, 0:Tt], self.cbf[:, CBF_R2:CBF_R2 + 128],
                                                 qss[bi][:, 0:Tt], start=True, stop=True), spe)
                    tk9 = self.T(rotf)

                    def evf(v, en, Tt=Tt, bi=bi, a0=a0, dst=dst):
                        self.wait(v, en, tk9)
                        self.lk(v, en, v.tensor_add(dst[0][0:64, a0:a0 + Tt],
                                                    rps[bi][0:64, 0:Tt],
                                                    qcs[bi][0:64, 0:Tt]))
                        return self.inc(v.tensor_add(dst[1][0:64, a0:a0 + Tt],
                                                     rps[bi][64:128, 0:Tt],
                                                     qcs[bi][64:128, 0:Tt]), sdve)
                    tk10 = self.V(evf)
                    war[("rp", bi)] = tk10
                    war[("qn", bi)] = tk10
                war[("aug", side)] = tk10
                ilwar["wqk"] = tk1
            self.cbar()
            self.end_scope(mark_i)
            if STOP != "i":
                self.f_attn_hp(l, hp, qaug, kaug,
                               war[("aug", 0)], war[("aug", 1)], tk_vblob)
            self.end_scope(mark_hp)
        self.cbar()
        if STOP == "i":
            self.full_barrier()

            def dmpa(g, en):
                i = None
                tensors = (qaug[0], qaug[1], kaug[0], kaug[1])[:2 * NSIDE]
                wd = 512 * NTILE
                for k, t in enumerate(tensors):
                    i = self.inc(g.dma_start(out=self.d_dbg[k][0:65, 0:wd], in_=t[0:65, 0:wd]),
                                 self.sem("dbgst2"), 16)
                return i
            self.G(dmpa)
            self.full_barrier()
            return
        self.end_scope(mark_sb)

        mark_wo = self.scope()
        wosb = [self.sbuf(f"wosb{c}", [128, 1024], BF16) for c in range(4)]

        def ldwo(g, en):
            i = None
            for c in range(4):
                i = self.inc(g.dma_start(out=wosb[c][:],
                                         in_=self.d_wo[l, 128 * c:128 * c + 128, :]),
                             self.sem("wold"), 16)
            return i
        tk_wo = self.Y(ldwo)
        wop = [self.psum(f"wop{i}", [128, 512]) for i in range(2)]
        cxt = [[self.sbuf(f"cxt{k}_{i}", [128, 512], BF16) for k in range(4)] for i in range(2)]
        warw = [None, None]
        warcx = [None, None]
        groups5 = [(512, 640)] + [(640 + 512 * g, 1152 + 512 * g) for g in range(4)]
        self.cbar()
        for gi, (glo, ghi) in enumerate(groups5):
            Tg = ghi - glo
            cbi = gi % 2

            def cxld(y, en, glo=glo, ghi=ghi, Tg=Tg, cbi=cbi):
                self.wait(y, en, warcx[cbi])
                self.barrier_from(y, en, "cxst0", "cxst1")
                i = None
                for kc in range(4):
                    i = self.inc(y.dma_start(out=cxt[cbi][kc][:, 0:Tg],
                                             in_=self.cxdram[128 * kc:128 * kc + 128,
                                                             glo - 512:ghi - 512]),
                                 self.sem(f"cxld{cbi}"), 16)
                return i
            tk_cx = self.Y(cxld)
            for f in range(NCH):
                bi = (gi * NCH + f) % 2

                def womm(t, en, f=f, bi=bi, cbi=cbi, Tg=Tg):
                    self.wait(t, en, tk_wo, tk_cx, warw[bi])
                    i = None
                    for kc in range(4):
                        i = t.matmul(wop[bi][:, 0:Tg], wosb[kc][:, 128 * f:128 * f + 128],
                                     cxt[cbi][kc][:, 0:Tg],
                                     start=(kc == 0), stop=(kc == 3))
                    return self.inc(i, spe)
                tkw = self.T(womm)
                if f == NCH - 1:
                    warcx[cbi] = tkw

                def woev(v, en, f=f, bi=bi, glo=glo, ghi=ghi, Tg=Tg):
                    self.wait(v, en, tkw)
                    i = v.scalar_tensor_tensor(out=self.S[f][:, glo + 1:ghi + 1],
                                               in0=wop[bi][:, 0:Tg], scalar=1.0,
                                               in1=self.S[f][:, glo + 1:ghi + 1],
                                               op0=ALU.mult, op1=ALU.add)
                    return self.inc(i, sdve)
                warw[bi] = self.V(woev)

        def hz(v, en):
            i = None
            for f in range(NCH):
                i = v.tensor_scalar_mul(self.S[f][:, 640:641], self.S[f][:, 640:641],
                                        self.vecs[:, V_HV:V_HV + 1])
            return self.inc(i, sdve)
        self.V(hz)
        self.cbar()
        self.end_scope(mark_wo)

        if STOP != "ii":
            self.f_mlp(l)
        self.end_scope(mark_il)

    def f_attn_hp(self, l, hp, qaug, kaug, tk_qaug, tk_kaug, tk_vblob):
        spe, sdve, sact, sld, spool = (self.sem(x) for x in
                                       ("pe", "dve", "act", "vhld", "pool"))
        sst = [self.sem("cxst0"), self.sem("cxst1")]
        groups5 = [(512, 640)] + [(640 + 512 * g, 1152 + 512 * g) for g in range(4)]
        for hl in range(2):
            hg = 2 * hp + hl
            mark_head = self.scope()
            vh = self.sbuf("vh", [128, NKC * 65], BF16)

            def vini(v, en):
                self.barrier_from(v, en, "pe")
                return self.inc(v.memset(vh[:], 1.0), sdve)
            tk_vh0 = self.V(vini)

            def vhl(y, en, hg=hg):
                self.wait(y, en, *tk_vblob)
                self.wait(y, en, tk_vh0)
                i = y.dma_start(
                    out=vh[:].rearrange("p (c e) -> p c e", c=NKC)[:, :, 0:64],
                    in_=self.vdram.rearrange("(c p) n -> p c n", p=128)[:, :, 64 * hg:64 * hg + 64])
                return self.inc(i, sld, 16)
            tk_vh = self.Y(vhl)

            mark_q = self.scope()
            pq = [self.psum(f"pq{i}", [128, 640]) for i in range(3)]
            tpp = self.psum("tpq", [NQT, 128])
            tpb = self.sbuf("tpb", [NQT, 128], F32)
            smq = [self.sbuf(f"smq{i}", [128, 640], F32) for i in range(3)]
            e8 = [self.sbuf(f"e8x{i}", [128, 640], BF16) for i in range(2)]
            s8a = self.sbuf("s8a", [128, NQT], F32)
            lcol = self.sbuf("lcq", [128, NQT], F32)
            war = {}
            self.cbar()
            tk3 = None
            msk_tk = {}
            for t in range(NQT):
                bi = t % 3
                mi = t if t < 5 else 5

                def qmm(tt, en, t=t, bi=bi):
                    self.wait(tt, en, tk_qaug, tk_kaug, war.get(("pq", bi)))
                    a = 128 * t
                    tt.matmul(pq[bi][:, 0:512], qaug[hl][0:64, 512 + a:640 + a],
                              kaug[hl][0:64, a:a + 512], start=True, stop=True)
                    i = tt.matmul(pq[bi][:, 512:640], qaug[hl][0:64, 512 + a:640 + a],
                                  kaug[hl][0:64, a + 512:a + 640], start=True, stop=True)
                    return self.inc(i, spe)
                tk1 = self.T(qmm)

                def mskf(v, en, bi=bi, mi=mi):
                    self.wait(v, en, tk1, war.get(("smq", bi)))
                    i = v.tensor_tensor(smq[bi][:], pq[bi][:],
                                        self.mq[:, 640 * mi:640 * mi + 640], ALU.add)
                    return self.inc(i, sdve)
                msk_tk[t] = self.V(mskf)
                war[("pq", bi)] = msk_tk[t]

                if t % 2 == 1 or t == NQT - 1:
                    ts = [t - 1, t] if t % 2 == 1 else [t]

                    def expf(c, en, ts=tuple(ts)):
                        self.wait(c, en, war.get("e8"))
                        for tj in ts:
                            self.wait(c, en, msk_tk[tj], war.get(("smq", tj % 3)))
                        i = None
                        for tj in ts:
                            i = c.activation(out=e8[tj % 2][:], in_=smq[tj % 3][:],
                                             func=AF.Exp, scale=0.125,
                                             accum_out=s8a[:, tj:tj + 1])
                        return self.inc(i, sact)
                    tk3 = self.C(expf)
                    war["e8"] = tk3
                    for tj in ts:
                        war[("smq", tj % 3)] = tk3

            def maxf(v, en):
                self.wait(v, en, tk3)
                i = v.tensor_scalar_max(s8a[:], s8a[:], 1e-35)
                return self.inc(i, sdve)
            tk4 = self.V(maxf)

            def lnf(c, en):
                self.wait(c, en, tk4)
                i = c.activation(out=lcol[:], in_=s8a[:], func=AF.Ln)
                return self.inc(i, sact)
            tk5 = self.C(lnf)

            def trf(tt, en):
                self.wait(tt, en, tk5)
                i = tt.transpose(tpp[:], lcol[:], self.cf32[:, CF_ID:CF_ID + 128])
                return self.inc(i, spe)
            tk6 = self.T(trf)

            def tcp(v, en):
                self.wait(v, en, tk6)
                return self.inc(v.tensor_scalar_mul(tpb[:], tpp[:], -8.0), sdve)
            tk6b = self.V(tcp)

            def mcp(g, en, hl=hl):
                self.wait(g, en, tk6b)
                i = g.dma_start(out=qaug[hl][64:65, 512:512 + 128 * NQT],
                                in_=tpb[0:NQT, 0:128])
                return self.inc(i, self.sem("qmst"), 16)
            tk_mrow = self.G(mcp)
            self.cbar()
            self.end_scope(mark_q)

            mark_k = self.scope()
            sk = [self.psum(f"sk{i}", [128, 640]) for i in range(3)]
            gctx = [self.psum(f"gx{i}", [65, 512]) for i in range(2)]
            zbp = gctx[1]
            smk = [self.sbuf(f"smk{i}", [128, 640], F32) for i in range(3)]
            probs = [self.sbuf(f"pb{i}", [128, 640], BF16) for i in range(2)]
            cstg = self.sbuf("cstg", [65, 2176], F32)
            ctev = self.sbuf("ctevh", [64, 2176], BF16)
            zrh = cstg[64:65, :]
            stg_tk = {}
            kmsk_tk = {}
            exp_tk = {}
            war = {}
            self.cbar()
            gfirst = {gi: True for gi in range(5)}
            ghw = {gi: 0 for gi in range(5)}
            for c in range(NKC):
                qlo, qhi = kspan(c)
                spanw = qhi - qlo
                bi = c % 3
                pi = c % 2
                mo, mw = maskk_slice(c)
                assert mw == spanw

                def kmm(tt, en, c=c, bi=bi, qlo=qlo, qhi=qhi):
                    self.wait(tt, en, tk_mrow, tk_kaug, war.get(("sk", bi)))
                    i = None
                    off = 0
                    while off < qhi - qlo:
                        wpc = min(512, qhi - qlo - off)
                        i = tt.matmul(sk[bi][:, off:off + wpc],
                                      kaug[hl][0:65, 128 * c:128 * c + 128],
                                      qaug[hl][0:65, qlo + off:qlo + off + wpc],
                                      start=True, stop=True)
                        off += wpc
                    return self.inc(i, spe)
                tk1 = self.T(kmm)

                def mskf(v, en, bi=bi, mo=mo, mw=mw):
                    self.wait(v, en, tk1, war.get(("smk", bi)))
                    i = v.tensor_tensor(smk[bi][:, 0:mw], sk[bi][:, 0:mw],
                                        self.mkb[:, mo:mo + mw], ALU.add)
                    return self.inc(i, sdve)
                kmsk_tk[c] = self.V(mskf)
                war[("sk", bi)] = kmsk_tk[c]

                if c % 2 == 1 or c == NKC - 1:
                    cs_ = [c - 1, c] if c % 2 == 1 else [c]

                    def expf(cc, en, cs_=tuple(cs_)):
                        for cj in cs_:
                            self.wait(cc, en, kmsk_tk[cj], war.get(("pb", cj % 2)))
                        i = None
                        for cj in cs_:
                            mwj = kspan(cj)[1] - kspan(cj)[0]
                            i = cc.activation(out=probs[cj % 2][:, 0:mwj],
                                              in_=smk[cj % 3][:, 0:mwj], func=AF.Exp)
                        return self.inc(i, sact)
                    tk3 = self.C(expf)
                    for cj in cs_:
                        war[("smk", cj % 3)] = tk3
                        exp_tk[cj] = tk3

                    for cj in cs_:
                        qloj, qhij = kspan(cj)
                        pij = cj % 2
                        for gi, (glo, ghi) in enumerate(groups5):
                            a, bnd = max(qloj, glo), min(qhij, ghi)
                            if a >= bnd:
                                continue
                            gslot = gi % 2
                            clast = (ghi - 1) // 128
                            lo_g, hi_g = a - glo, bnd - glo
                            vhw = ghw[gi]
                            parts = []
                            if lo_g < vhw:
                                parts.append((lo_g, min(hi_g, vhw)))
                            if vhw < hi_g:
                                parts.append((max(lo_g, vhw), hi_g))
                            ghw[gi] = max(vhw, hi_g)

                            def cmm(tt, en, c=cj, gi=gi, gslot=gslot, glo=glo,
                                    qlo=qloj, pi=pij, clast=clast,
                                    parts=tuple(parts), first=gfirst[gi]):
                                self.wait(tt, en, exp_tk[c], tk_vh)
                                if first:
                                    self.wait(tt, en, war.get(("gx", gslot)))
                                i = None
                                for n, (plo, phi) in enumerate(parts):
                                    i = tt.matmul(gctx[gslot][0:65, plo:phi],
                                                  vh[:, 65 * c:65 * c + 65],
                                                  probs[pi][:, plo + glo - qlo:phi + glo - qlo],
                                                  start=(first and n == 0),
                                                  stop=(c == clast and n == len(parts) - 1),
                                                  skip_group_check=True)
                                gfirst[gi] = False
                                return self.inc(i, spe)
                            tk_ctx = self.T(cmm)
                            war[("pb", pij)] = tk_ctx
                            if cj == clast:
                                Tg = ghi - glo

                                def stg(v, en, gslot=gslot, glo=glo, Tg=Tg, tk_ctx=tk_ctx):
                                    self.wait(v, en, tk_ctx)
                                    i = v.tensor_copy(cstg[0:65, glo - 512:glo - 512 + Tg],
                                                      gctx[gslot][0:65, 0:Tg])
                                    return self.inc(i, sdve)
                                stg_tk[gi] = self.V(stg)
                                war[("gx", gslot)] = stg_tk[gi]

            def zmax(v, en):
                self.wait(v, en, *stg_tk.values())
                self.lk(v, en, v.tensor_scalar_max(zrh, zrh, 1e-35))
                return self.inc(v.reciprocal(zrh, zrh), sdve)
            tkz = self.V(zmax)
            tke = None
            for gi, (glo, ghi) in enumerate(groups5):
                Tg = ghi - glo

                def zb(tt, en, glo=glo, Tg=Tg):
                    self.wait(tt, en, tkz, war.get("zbp"))
                    i = tt.matmul(zbp[0:64, 0:Tg],
                                  self.cf32[64:65, CF_ONESROW:CF_ONESROW + 64],
                                  cstg[64:65, glo - 512:glo - 512 + Tg],
                                  start=True, stop=True)
                    return self.inc(i, spe)
                tkb = self.T(zb)

                def ctv(v, en, glo=glo, Tg=Tg):
                    self.wait(v, en, tkb)
                    i = v.tensor_tensor(ctev[0:64, glo - 512:glo - 512 + Tg],
                                        cstg[0:64, glo - 512:glo - 512 + Tg],
                                        zbp[0:64, 0:Tg], ALU.mult)
                    return self.inc(i, sdve)
                tke = self.V(ctv)
                war["zbp"] = tke

            def cst(y, en, hg=hg):
                self.wait(y, en, tke)
                return self.inc(y.dma_start(
                    out=self.cxdram[64 * hg:64 * hg + 64, :],
                    in_=ctev[0:64, :]), sst[0], 16)
            self.Y(cst)
            self.cbar()
            self.end_scope(mark_k)
            self.end_scope(mark_head)

    def f_mlp(self, l):
        spe, sdve, sact, spool = (self.sem(x) for x in ("pe", "dve", "act", "pool"))
        mark_il3 = self.scope()
        xh2 = [self.sbuf(f"xh2_{f}", [128, OWN], BF16) for f in range(NCH)]
        mark_a = self.scope()
        p = self.mk_ln_pools("m")
        xmb = [self.sbuf(f"mx{f}", [128, 512], BF16) for f in range(NCH)]
        t1 = [self.sbuf(f"mt{f}", [128, 512], F32) for f in range(NCH)]
        self.cbar()
        war_xh = None
        for g in range(4):
            scol = 641 + 512 * g

            def cast(v, en, scol=scol):
                self.wait(v, en, war_xh)
                i = None
                for f in range(NCH):
                    src = self.S[f][:, scol:scol + 512] if f < 4 else \
                        self.S[f][:, scol - 1:scol - 1 + 512]
                    i = v.tensor_copy(xmb[f][:], src)
                return self.inc(i, sdve)
            tk_cast = self.V(cast)
            tk_mub, tk_rb = self.ln_stats(xmb, 512, p, tk_cast)

            def app(v, en, scol=scol):
                self.wait(v, en, tk_rb)
                i = None
                for f in range(NCH):
                    srcv = self.S[f][:, scol:scol + 512] if f < 4 else \
                        self.S[f][:, scol - 1:scol - 1 + 512]
                    i = v.tensor_sub(t1[f][:], srcv, p["mub"][:, 0:512])
                self.lk(v, en, i)
                for f in range(NCH):
                    i = v.scalar_tensor_tensor(
                        out=t1[f][:], in0=t1[f][:],
                        scalar=self.vecs[:, VL * l + V_LG + f:VL * l + V_LG + f + 1],
                        in1=p["rbs"][:, 0:512], op0=ALU.mult, op1=ALU.mult)
                return self.inc(i, sdve)
            tk_ap = self.V(app)
            p["war"]["mub"] = tk_ap

            def xw(v, en, g=g):
                self.wait(v, en, tk_ap)
                i = None
                for f in range(NCH):
                    i = v.tensor_scalar_add(
                        xh2[f][:, 512 * g:512 * g + 512], t1[f][:],
                        self.vecs[:, VL * l + V_LB + f:VL * l + V_LB + f + 1])
                return self.inc(i, sdve)
            war_xh = self.V(xw)
        tk_xh = war_xh
        self.cbar()
        self.end_scope(mark_a)

        mark_b = self.scope()
        ublob = self.sbuf("ublob", [128, NHC * 512], BF16)
        w1s = [[self.sbuf(f"w1s{a}{i}", [128, 1024], BF16) for i in range(2)] for a in range(2)]
        w2s = [self.sbuf(f"w2s{i}", [128, NHC * 128], BF16) for i in range(2)]
        aps = [self.psum(f"aps{i}", [128, 512]) for i in range(2)]
        gps = [self.psum(f"gps{i}", [128, 512]) for i in range(2)]
        w2p = [self.psum(f"w2ps{i}", [128, 512]) for i in range(2)]
        gl = [self.sbuf(f"gl{i}", [128, 512], BF16) for i in range(2)]
        war = {}
        self.cbar()
        for g in range(4):
            scol = 641 + 512 * g
            for hc in range(NHC):
                bi = hc % 2

                def ldw(gg, en, hc=hc, bi=bi):
                    self.wait(gg, en, war.get(("w1", bi)))
                    swt_ = self.sem(f"w1ld{bi}")
                    self.inc(gg.dma_start(
                        out=w1s[0][bi][:].rearrange("p (c n) -> p c n", c=8),
                        in_=self.d_w1a[l].rearrange("(c p) n -> p c n", p=128)[:, :, 128 * hc:128 * hc + 128]),
                        swt_, 16)
                    return self.inc(gg.dma_start(
                        out=w1s[1][bi][:].rearrange("p (c n) -> p c n", c=8),
                        in_=self.d_w1g[l].rearrange("(c p) n -> p c n", p=128)[:, :, 128 * hc:128 * hc + 128]),
                        swt_, 16)
                tk_w = self.Y(ldw)

                def amm(tt, en, g=g, bi=bi):
                    self.wait(tt, en, tk_w, tk_xh, war.get(("aps", bi)), war.get(("gps", bi)))
                    for f in range(NCH):
                        tt.matmul(aps[bi][:], w1s[0][bi][:, 128 * f:128 * f + 128],
                                  xh2[f][:, 512 * g:512 * g + 512],
                                  start=(f == 0), stop=(f == NCH - 1))
                    i = None
                    for f in range(NCH):
                        i = tt.matmul(gps[bi][:], w1s[1][bi][:, 128 * f:128 * f + 128],
                                      xh2[f][:, 512 * g:512 * g + 512],
                                      start=(f == 0), stop=(f == NCH - 1))
                    return self.inc(i, spe)
                tk_mm = self.T(amm)
                war[("w1", bi)] = tk_mm

                def gel(cc, en, hc=hc, bi=bi):
                    self.wait(cc, en, tk_mm, war.get(("gl", bi)))
                    i = cc.activation(out=gl[bi][:], in_=gps[bi][:], func=AF.Gelu,
                                      bias=self.vecs[:, VL * l + V_B1G + hc:VL * l + V_B1G + hc + 1])
                    return self.inc(i, sact)
                tk_g = self.C(gel)
                war[("gps", bi)] = tk_g

                def uf(v, en, hc=hc, bi=bi):
                    self.wait(v, en, tk_g, war.get(("ub", hc)))
                    i = v.scalar_tensor_tensor(
                        out=ublob[:, 512 * hc:512 * hc + 512], in0=aps[bi][:],
                        scalar=self.vecs[:, VL * l + V_B1A + hc:VL * l + V_B1A + hc + 1],
                        in1=gl[bi][:], op0=ALU.add, op1=ALU.mult)
                    return self.inc(i, sdve)
                tk_u = self.V(uf)
                war[("aps", bi)] = tk_u
                war[("gl", bi)] = tk_u
                war[("u", hc)] = tk_u
            tk_ulast = tk_u
            for f in range(NCH):
                bi = f % 2

                def ldw2(gg, en, f=f, bi=bi):
                    self.wait(gg, en, war.get(("w2", bi)))
                    swt_ = self.sem(f"w2ld{bi}")
                    return self.inc(gg.dma_start(
                        out=w2s[bi][:].rearrange("p (c n) -> p c n", c=NHC),
                        in_=self.d_w2p[l].rearrange("(c p) n -> p c n", p=128)[:, :, 128 * f:128 * f + 128]),
                        swt_, 16)
                tk_w2 = self.Y(ldw2)

                def wmm(tt, en, f=f, bi=bi):
                    self.wait(tt, en, tk_w2, tk_ulast, war.get(("w2p", bi)))
                    i = None
                    for hc in range(NHC):
                        i = tt.matmul(w2p[bi][:], w2s[bi][:, 128 * hc:128 * hc + 128],
                                      ublob[:, 512 * hc:512 * hc + 512],
                                      start=(hc == 0), stop=(hc == NHC - 1))
                    return self.inc(i, spe)
                tk_m = self.T(wmm)
                war[("w2", bi)] = tk_m

                def ev(v, en, f=f, bi=bi, scol=scol):
                    self.wait(v, en, tk_m)
                    i = v.scalar_tensor_tensor(
                        out=self.S[f][:, scol:scol + 512], in0=w2p[bi][:],
                        scalar=self.vecs[:, VL * l + V_B2 + f:VL * l + V_B2 + f + 1],
                        in1=self.S[f][:, scol:scol + 512], op0=ALU.add, op1=ALU.add)
                    return self.inc(i, sdve)
                war[("w2p", bi)] = self.V(ev)
            for hc in range(NHC):
                war[("ub", hc)] = war[("w2", (NCH - 1) % 2)]
        self.cbar()
        self.end_scope(mark_b)
        self.end_scope(mark_il3)

    def ph_x1y2(self):
        mark = self.scope()
        sld, sdve, sact = (self.sem(x) for x in ("xld", "dve", "act"))
        xt = [self.sbuf(f"xx{f}", [128, 512], F32) for f in range(NCH)]
        x1t = [self.sbuf(f"x1t{f}", [128, 512], F32) for f in range(NCH)]
        xb = [self.sbuf(f"xxb{f}", [128, 512], BF16) for f in range(NCH)]
        y2b = [self.sbuf(f"y2b{f}", [128, 512], BF16) for f in range(NCH)]
        t1 = [self.sbuf(f"xt1{f}", [128, 512], F32) for f in range(NCH)]
        p = self.mk_ln_pools("x1")
        self.full_barrier()
        war = {}
        for g in range(4):
            scol = 641 + 512 * g

            def ld(y, en, g=g, scol=scol):
                self.wait(y, en, war.get("xt"))
                i = None
                for f in range(NCH):
                    i = self.inc(y.dma_start(out=xt[f][:],
                                             in_=self.d_x0w[128 * f:128 * f + 128, scol:scol + 512]),
                                 sld, 16)
                return i
            tk_ld = self.Y(ld)

            def x1f(v, en, scol=scol):
                self.wait(v, en, tk_ld, war.get("x1t"), war.get("x1st"))
                i = None
                for f in range(NCH):
                    i = v.scalar_tensor_tensor(out=x1t[f][:], in0=self.S[f][:, scol:scol + 512],
                                               scalar=0.5, in1=xt[f][:],
                                               op0=ALU.mult, op1=ALU.add)
                self.lk(v, en, i)
                for f in range(NCH):
                    i = v.tensor_copy(xb[f][:], x1t[f][:])
                return self.inc(i, sdve)
            tk_x1 = self.V(x1f)
            war["xt"] = tk_x1

            def st1(y, en, g=g):
                self.wait(y, en, tk_x1)
                i = None
                for f in range(NCH):
                    i = self.inc(y.dma_start(out=self.d_x1[128 * f:128 * f + 128,
                                                          512 * g:512 * g + 512],
                                             in_=x1t[f][:]), self.sem("x1st"), 16)
                return i
            war["x1st"] = self.Y(st1)
            tk_mub, tk_rb = self.ln_stats(xb, 512, p, tk_x1)

            def app(v, en):
                self.wait(v, en, tk_rb, war.get("y2b"))
                i = None
                for f in range(NCH):
                    i = v.tensor_sub(t1[f][:], x1t[f][:], p["mub"][:, 0:512])
                self.lk(v, en, i)
                for f in range(NCH):
                    i = v.tensor_mul(y2b[f][:], t1[f][:], p["rbs"][:, 0:512])
                return self.inc(i, sdve)
            tk_y2 = self.V(app)
            war["x1t"] = tk_y2
            p["war"]["mub"] = tk_y2

            def st2(y, en, g=g):
                self.wait(y, en, tk_y2)
                i = None
                for f in range(NCH):
                    i = self.inc(y.dma_start(out=self.e2a_in[128 * f:128 * f + 128,
                                                             512 * g:512 * g + 512],
                                             in_=y2b[f][:]), self.sem("e2ast"), 16)
                return i
            war["y2b"] = self.Y(st2)

        def coll(gg, en):
            self.barrier_from(gg, en, "e2ast")
            return self.inc(gg.collective_compute(
                "AllGather", ALU.bypass, replica_groups=self.rg,
                ins=[self.e2a_in.opt()], outs=[self.e2a_out.opt()]), self.sem("cc"), 1)
        self.tk_e2a = self.G(coll)
        self.full_barrier()
        self.end_scope(mark)

    def ph_A(self):
        spe, sdve, sact, swt, sld, spool = (self.sem(x) for x in
                                            ("pe", "dve", "act", "awld", "ald", "pool"))
        sst = [self.sem("e2bst0"), self.sem("e2bst1")]
        mark = self.scope()
        y2sb = [self.sbuf(f"ya{f}", [128, SEQ], BF16) for f in range(NCH)]
        mvb = self.sbuf("mv_sb", [128, SEQ], BF16)
        bvr = self.sbuf("bvr", [1, 512], BF16)
        vblA = self.sbuf("vblA", [128, 32 * 520], BF16)
        bkV = self.sbuf("bkV", [128, 8 * 65], BF16)
        ctxA = [self.sbuf(f"cA{c}", [128, SEQ], BF16) for c in range(4)]

        def ldy(y, en):
            self.wait(y, en, self.tk_e2a)
            i = None
            for f in range(NCH):
                i = self.inc(y.dma_start(out=y2sb[f][:, 0:OWN],
                                         in_=self.e2a_out[128 * f:128 * f + 128, :]), sld, 16)
                i = self.inc(y.dma_start(out=y2sb[f][:, OWN:SEQ],
                                         in_=self.e2a_out[D + 128 * f:D + 128 * f + 128, :]),
                             sld, 16)
            i = self.inc(y.dma_start(out=mvb[:], in_=self.d_maskv), sld, 16)
            return i
        tk_y2 = self.Y(ldy)

        def ldbv(g, en):
            return self.inc(g.dma_start(out=bvr[:], in_=self.d_bv512), self.sem("awld"), 16)
        tk_bv = self.Y(ldbv)

        mark_v = self.scope()
        bki = self.sbuf("bki", [128, 32 * 128], BF16)
        wvsb = [self.sbuf(f"wva{f}", [128, 512], BF16) for f in range(NCH)]

        def ldbki(y, en):
            return self.inc(y.dma_start(out=bki[:], in_=self.d_blkind), self.sem("bkld"), 16)
        tk_bki = self.Y(ldbki)

        def ldwv(g, en):
            i = None
            for f in range(NCH):
                i = self.inc(g.dma_start(out=wvsb[f][:],
                                         in_=self.d_wvT[128 * f:128 * f + 128, :]), swt, 16)
            return i
        tk_wv = self.Y(ldwv)

        def vinit(v, en):
            return self.inc(v.memset(vblA[:], 1.0), sdve)
        tk_vi = self.V(vinit)
        vps = [self.psum(f"avp{i}", [128, 512]) for i in range(2)]
        warv = [None, None]
        for tc in range(32):
            bi = tc % 2

            def vmm(t, en, tc=tc, bi=bi):
                self.wait(t, en, tk_y2, tk_wv, tk_bv, warv[bi])
                for f in range(NCH):
                    t.matmul(vps[bi][:], y2sb[f][:, 128 * tc:128 * tc + 128], wvsb[f][:],
                             start=(f == 0), stop=False)
                i = t.matmul(vps[bi][:], self.cbf[0:1, CBF_ONESROW:CBF_ONESROW + 128],
                             bvr[:], start=False, stop=True)
                return self.inc(i, spe)
            tkv = self.T(vmm)

            def vev(v, en, tc=tc, bi=bi):
                self.wait(v, en, tkv, tk_vi)
                dst = vblA.rearrange("p (c h e) -> p c h e", c=32, h=FH)[:, tc, :, 0:64]
                src = vps[bi][:].rearrange("p (h e) -> p h e", h=FH)
                return self.inc(v.tensor_copy(dst, src), sdve)
            warv[bi] = self.V(vev)
        tk_vbl = warv[31 % 2]
        bkp = [self.psum(f"bkp{i}", [128, 65]) for i in range(2)]
        warb = [None, None]
        for ha in range(FH):
            bi = ha % 2

            def bmm(t, en, ha=ha, bi=bi):
                self.wait(t, en, tk_vbl, tk_bki, warb[bi])
                i = None
                for tc in range(32):
                    i = t.matmul(bkp[bi][:], bki[:, 128 * tc:128 * tc + 128],
                                 vblA[:, 520 * tc + 65 * ha:520 * tc + 65 * ha + 65],
                                 start=(tc == 0), stop=(tc == 31))
                return self.inc(i, spe)
            tkb = self.T(bmm)

            def bev(v, en, ha=ha, bi=bi):
                self.wait(v, en, tkb)
                return self.inc(v.tensor_copy(bkV[:, 65 * ha:65 * ha + 65],
                                              bkp[bi][:]), sdve)
            warb[bi] = self.V(bev)
        self.cbar()
        self.end_scope(mark_v)

        awar = {}
        for hp in range(4):
            mark_hp = self.scope()
            kaugA = [self.sbuf(f"kA{i}", [81, SEQ], BF16) for i in range(2)]
            kmx2 = self.sbuf("kmx2", [33, 1], F32)
            kred = self.sbuf("kred", [33, 1], F32)
            kmax = self.sbuf("kmax", [33, 1], F32)
            wksb = self.sbuf("wksb", [128, 1024], BF16)
            wqsb = self.sbuf("wqsb", [128, 1024], BF16)
            qaugA = [self.sbuf(f"qA{i}", [81, 512], BF16) for i in range(2)]
            qnorm = self.sbuf("qnorm", [33, 512], F32)
            ksq = [self.sbuf(f"ksq{i}", [128, 512], BF16) for i in range(2)]
            smv = self.sbuf("smv", [128, 512], F32)
            vpb = self.sbuf("vpb", [128, 512], BF16)
            probsA = [self.sbuf(f"pA{i}", [128, 512], BF16) for i in range(3)]
            z2 = self.sbuf("az2", [1, 512], F32)
            zr = self.sbuf("azr", [1, 512], F32)
            zbs = self.sbuf("azbs", [64, 512], F32)

            def ldk(g, en, hp=hp):
                self.wait(g, en, awar.get("hp"))
                i = self.inc(g.dma_start(
                    out=wksb[:].rearrange("p (c n) -> p c n", c=8),
                    in_=self.d_wkT.rearrange("(c p) n -> p c n", p=128)[:, :, 128 * hp:128 * hp + 128]),
                    swt, 16)
                i = self.inc(g.dma_start(
                    out=wqsb[:].rearrange("p (c n) -> p c n", c=8),
                    in_=self.d_wqT.rearrange("(c p) n -> p c n", p=128)[:, :, 128 * hp:128 * hp + 128]),
                    swt, 16)
                return i
            tk_wk = self.Y(ldk)

            def kinit(y, en):
                self.wait(y, en, awar.get("hp"))
                i = None
                for hl in range(2):
                    i = self.inc(y.dma_start(out=kaugA[hl][65:81, :], in_=self.d_kaugc),
                                 self.sem("akld"), 16)
                return i
            tk_kc = self.Y(kinit)

            def kini2(v, en):
                i = None
                for hl in range(2):
                    i = v.memset(kaugA[hl][64:65, :], 1.0)
                return self.inc(i, sdve)
            tk_k1 = self.V(kini2)

            def qinit(y, en):
                self.wait(y, en, awar.get("hp"))
                i = None
                for hl in range(2):
                    i = self.inc(y.dma_start(out=qaugA[hl][65:81, :], in_=self.d_qaugc),
                                 self.sem("aqld"), 16)
                return i
            tk_qc = self.Y(qinit)

            mark_kb = self.scope()
            kps = [self.psum(f"akp{i}", [128, 512]) for i in range(2)]
            nkp = [self.psum(f"ank{i}", [33, 512]) for i in range(2)]
            self.cbar()
            war = {}
            for g8 in range(8):
                bi = g8 % 2

                def kmm(t, en, g8=g8, bi=bi):
                    self.wait(t, en, tk_y2, tk_wk, war.get(("kps", bi)))
                    i = None
                    for f in range(NCH):
                        i = t.matmul(kps[bi][:], wksb[:, 128 * f:128 * f + 128],
                                     y2sb[f][:, 512 * g8:512 * g8 + 512],
                                     start=(f == 0), stop=(f == NCH - 1))
                    return self.inc(i, spe)
                tk1 = self.T(kmm)

                def kev(v, en, g8=g8, bi=bi, hp=hp):
                    self.wait(v, en, tk1, tk_k1, war.get(("ksq", bi)))
                    i = None
                    for hl in range(2):
                        i = v.tensor_scalar_add(
                            kaugA[hl][0:64, 512 * g8:512 * g8 + 512],
                            kps[bi][64 * hl:64 * hl + 64, :],
                            self.vecs[64 * hl:64 * hl + 64, V_BK + hp:V_BK + hp + 1])
                    return self.inc(i, sdve)
                tk2a = self.V(kev)

                def ksqf(g_, en, g8=g8, bi=bi):
                    self.wait(g_, en, tk2a, war.get(("ksq", bi)))
                    i = None
                    for hl in range(2):
                        i = g_.tensor_mul(ksq[bi][64 * hl:64 * hl + 64, :],
                                          kaugA[hl][0:64, 512 * g8:512 * g8 + 512],
                                          kaugA[hl][0:64, 512 * g8:512 * g8 + 512])
                    return self.inc(i, spool)
                tk2 = self.G(ksqf)

                def nmm(t, en, bi=bi):
                    self.wait(t, en, tk2, war.get(("nkp", bi)))
                    return self.inc(t.matmul(nkp[bi][0:33, :],
                                             self.cbf[:, CBF_HS33:CBF_HS33 + 33],
                                             ksq[bi][:], start=True, stop=True), spe)
                tk3 = self.T(nmm)
                war[("kps", bi)] = tk2
                war[("ksq", bi)] = tk3

                def nred(v, en, g8=g8, bi=bi):
                    self.wait(v, en, tk3, war.get("kred"))
                    self.lk(v, en, v.reduce_max(kred[:], nkp[bi][0:33, :],
                                                axis=mybir.AxisListType.X))
                    if g8 == 0:
                        i = v.tensor_copy(kmx2[:], kred[:])
                    else:
                        i = v.tensor_max(kmx2[:], kmx2[:], kred[:])
                    return self.inc(i, sdve)
                war[("nkp", bi)] = self.V(nred)
                war["kred"] = war[("nkp", bi)]

            def kmx(c, en):
                self.wait(c, en, war["kred"])
                return self.inc(c.activation(out=kmax[:], in_=kmx2[:],
                                             func=AF.Sqrt, scale=1.0), sact)
            tk_sq1 = self.C(kmx)

            def kneg(v, en):
                self.wait(v, en, tk_sq1)
                return self.inc(v.tensor_scalar_mul(kmax[:], kmax[:], -1.0), sdve)
            tk_kmx = self.V(kneg)
            self.cbar()
            self.end_scope(mark_kb)

            mark_qa = self.scope()
            qps = [self.psum(f"aqp{i}", [128, 512]) for i in range(2)]
            nzp = self.psum("anzp", [128, 512])
            nqp = nzp
            skp = [self.psum(f"ask{i}", [128, 512]) for i in range(3)]
            gcx = [self.psum(f"agx{i}", [65, 512]) for i in range(2)]
            zbp = nzp
            qsq = [self.sbuf(f"qsq{i}", [128, 512], BF16) for i in range(2)]
            self.cbar()
            war = {}
            for g in range(8):
                bi = g % 2

                def qmm(t, en, g=g, bi=bi):
                    self.wait(t, en, tk_y2, tk_wk, war.get(("qps", bi)))
                    i = None
                    for f in range(NCH):
                        i = t.matmul(qps[bi][:], wqsb[:, 128 * f:128 * f + 128],
                                     y2sb[f][:, 512 * g:512 * g + 512],
                                     start=(f == 0), stop=(f == NCH - 1))
                    return self.inc(i, spe)
                tk1 = self.T(qmm)

                def qev(v, en, g=g, bi=bi, hp=hp):
                    self.wait(v, en, tk1, tk_qc, war.get("qaug"), war.get("qsqf"))
                    i = None
                    for hl in range(2):
                        i = v.tensor_scalar(
                            out=qaugA[hl][0:64, :],
                            in0=qps[bi][64 * hl:64 * hl + 64, :],
                            scalar1=0.125, op0=ALU.mult,
                            scalar2=self.bq8[64 * hl:64 * hl + 64, hp:hp + 1],
                            op1=ALU.add)
                    return self.inc(i, sdve)
                tk2b = self.V(qev)

                def qsqf(g_, en, bi=bi):
                    self.wait(g_, en, tk2b, war.get(("qsq", bi)))
                    i = None
                    for hl in range(2):
                        i = g_.tensor_mul(qsq[bi][64 * hl:64 * hl + 64, :],
                                          qaugA[hl][0:64, :], qaugA[hl][0:64, :])
                    return self.inc(i, spool)
                tk2 = self.G(qsqf)
                war["qsqf"] = tk2

                def nmm(t, en, bi=bi):
                    self.wait(t, en, tk2, war.get("nqp"))
                    return self.inc(t.matmul(nqp[0:33, :], self.cbf[:, CBF_HS33:CBF_HS33 + 33],
                                             qsq[bi][:], start=True, stop=True), spe)
                tk3 = self.T(nmm)
                war[("qps", bi)] = tk2
                war[("qsq", bi)] = tk3

                def sqf(c, en):
                    self.wait(c, en, tk3)
                    return self.inc(c.activation(out=qnorm[:], in_=nqp[0:33, :],
                                                 func=AF.Sqrt), sact)
                tk4 = self.C(sqf)
                war["nqp"] = tk4

                def mrow(v, en, bi=bi):
                    self.wait(v, en, tk4, tk_kmx)
                    i = None
                    for hl in range(2):
                        i = v.tensor_scalar_mul(qaugA[hl][64:65, :],
                                                qnorm[32 * hl:32 * hl + 1, :],
                                                kmax[32 * hl:32 * hl + 1, 0:1])
                    return self.inc(i, sdve)
                tk_mr = self.V(mrow)

                last_aug_reader = []
                for hl in range(2):
                    ha = 2 * hp + hl
                    gfirst = True
                    tk_ctx = None
                    smm_tk = {}
                    exp_tkA = {}
                    for ck in range(4 * g + 4):
                        sbi = ck % 3
                        diag = (ck >= 4 * g)
                        rows = 81 if diag else 65

                        def smm(t, en, ck=ck, sbi=sbi, hl=hl, rows=rows):
                            self.wait(t, en, tk_mr, tk_k1, war.get(("skp", sbi)))
                            return self.inc(t.matmul(
                                skp[sbi][:], kaugA[hl][0:rows, 128 * ck:128 * ck + 128],
                                qaugA[hl][0:rows, :], start=True, stop=True), spe)
                        smm_tk[ck] = self.T(smm)

                        if ck % 2 == 0:
                            continue
                        cks = (ck - 1, ck)

                        def pexp(c, en, cks=cks):
                            for cj in cks:
                                self.wait(c, en, smm_tk[cj], war.get(("pA", cj % 3)))
                            i = None
                            for cj in cks:
                                i = c.activation(out=probsA[cj % 3][:],
                                                 in_=skp[cj % 3][:], func=AF.Exp)
                            return self.inc(i, sact)
                        tk_p = self.C(pexp)
                        for cj in cks:
                            war[("skp", cj % 3)] = tk_p
                            exp_tkA[cj] = tk_p

                        for cj in cks:
                            def cmm(t, en, ck=cj, pi=cj % 3, ha=ha,
                                    gfirst=gfirst, hl=hl):
                                self.wait(t, en, exp_tkA[ck], tk_vbl)
                                if gfirst:
                                    self.wait(t, en, war.get(("gcx", hl)))
                                return self.inc(t.matmul(
                                    gcx[hl][0:65, :],
                                    vblA[:, 520 * ck + 65 * ha:520 * ck + 65 * ha + 65],
                                    probsA[pi][:], start=gfirst, stop=False,
                                    skip_group_check=True), spe)
                            tk_ctx = self.T(cmm)
                            war[("pA", cj % 3)] = tk_ctx
                            gfirst = False
                    vsl = (4 * g + 4) % 3

                    def vmm(t, en, g=g, hl=hl, vsl=vsl):
                        self.wait(t, en, tk_mr, war.get(("skp", vsl)))
                        return self.inc(t.matmul(
                            skp[vsl][:], self.cbf[64:65, CBF_ONES64:CBF_ONES64 + 128],
                            qaugA[hl][64:65, :], start=True, stop=True), spe)
                    tk_v1 = self.T(vmm)

                    def vmsk(v, en, g=g, vsl=vsl):
                        self.wait(v, en, tk_v1, war.get("smv"))
                        return self.inc(v.tensor_tensor(
                            smv[:], skp[vsl][:],
                            mvb[:, 512 * g:512 * g + 512], ALU.add), sdve)
                    tk_v2 = self.V(vmsk)
                    war[("skp", vsl)] = tk_v2

                    def vexp(c, en):
                        self.wait(c, en, tk_v2, war.get("vpb"))
                        return self.inc(c.activation(out=vpb[:], in_=smv[:], func=AF.Exp), sact)
                    tk_v3 = self.C(vexp)
                    war["smv"] = tk_v3

                    def vcm(t, en, ha=ha, hl=hl):
                        self.wait(t, en, tk_v3)
                        return self.inc(t.matmul(
                            gcx[hl][0:65, :], bkV[:, 65 * ha:65 * ha + 65], vpb[:],
                            start=False, stop=True, skip_group_check=True), spe)
                    tk_v4 = self.T(vcm)
                    war["vpb"] = tk_v4

                    def zf(v, en, hl=hl):
                        self.wait(v, en, tk_v4, war.get("zch"))
                        self.lk(v, en, v.tensor_scalar_max(z2[:], gcx[hl][64:65, :], 1e-35))
                        return self.inc(v.reciprocal(zr[:], z2[:]), sdve)
                    tkz = self.V(zf)

                    def zb(t, en):
                        self.wait(t, en, tkz, war.get("zbp"))
                        return self.inc(t.matmul(zbp[0:64, :],
                                                 self.cf32[0:1, CF_ONESROW:CF_ONESROW + 64],
                                                 zr[0:1, :], start=True, stop=True), spe)
                    tkb2 = self.T(zb)
                    war["zch"] = tkb2

                    def zc(v, en):
                        self.wait(v, en, tkb2, war.get("zbs"))
                        return self.inc(v.tensor_copy(zbs[:], zbp[0:64, :]), sdve)
                    tkc2 = self.V(zc)
                    war["zbp"] = tkc2
                    war["nqp"] = tkc2

                    def cev(v, en, ha=ha, g=g, hl=hl):
                        self.wait(v, en, tkc2)
                        i = v.tensor_tensor(
                            ctxA[ha // 2][(ha % 2) * 64:(ha % 2) * 64 + 64,
                                          512 * g:512 * g + 512],
                            gcx[hl][0:64, :], zbs[:], ALU.mult)
                        return self.inc(i, sdve)
                    tke = self.V(cev)
                    war[("gcx", hl)] = tke
                    war["zbs"] = tke
                    last_aug_reader.append(tk_v4)
                war["qaug"] = last_aug_reader[-1]
            awar["hp"] = war["qaug"]
            self.cbar()
            self.end_scope(mark_qa)
            self.end_scope(mark_hp)

        mark_wo = self.scope()
        wotsb = [self.sbuf(f"wot{c}", [128, 1024], BF16) for c in range(4)]
        pot = [self.sbuf(f"pot{i}", [128, 512], F32) for i in range(2)]
        wop = [self.psum(f"awo{i}", [128, 512]) for i in range(2)]

        def ldwo(g, en):
            i = None
            for c in range(4):
                i = self.inc(g.dma_start(out=wotsb[c][:],
                                         in_=self.d_woT[128 * c:128 * c + 128, :]), swt, 16)
            return i
        tk_wo = self.Y(ldwo)
        self.cbar()
        war = {}
        self.tk_e2b = [None] * 4
        for gn, g in enumerate([0, 4, 1, 5, 2, 6, 3, 7]):
            for f in range(NCH):
                bi = (gn * NCH + f) % 2

                def womm(t, en, g=g, f=f, bi=bi):
                    self.wait(t, en, tk_wo, war.get(("wop", bi)))
                    self.barrier_from(t, en, "dve")
                    i = None
                    for kc in range(4):
                        i = t.matmul(wop[bi][:], wotsb[kc][:, 128 * f:128 * f + 128],
                                     ctxA[kc][:, 512 * g:512 * g + 512],
                                     start=(kc == 0), stop=(kc == 3))
                    return self.inc(i, spe)
                tkw = self.T(womm)

                def wev(v, en, bi=bi):
                    self.wait(v, en, tkw, war.get(("pot", bi)))
                    return self.inc(v.tensor_copy(pot[bi][:], wop[bi][:]), sdve)
                tkv = self.V(wev)
                war[("wop", bi)] = tkv

                def wst(y, en, g=g, f=f, bi=bi):
                    self.wait(y, en, tkv)
                    return self.inc(y.dma_start(
                        out=self.e2bi[g % 4][128 * f:128 * f + 128,
                                            (g // 4) * 512:(g // 4) * 512 + 512],
                        in_=pot[bi][:]), sst[bi], 16)
                war[("pot", bi)] = self.Y(wst)
            if g >= 4:
                def coll(gg, en, j=g % 4):
                    self.barrier_from(gg, en, "e2bst0", "e2bst1")
                    return self.inc(gg.collective_compute(
                        "AllReduce", ALU.add, replica_groups=self.rg,
                        ins=[self.e2bi[j].opt()], outs=[self.e2bo[j].opt()]),
                        self.sem("cc"), 1)
                self.tk_e2b[g % 4] = self.G(coll)
        self.full_barrier()
        self.end_scope(mark)

    def ph_x2y3(self):
        mark = self.scope()
        sld, sdve, sst = (self.sem(x) for x in ("x2ld", "dve", "x2st"))
        x1t = [self.sbuf(f"zx{f}", [128, 512], F32) for f in range(NCH)]
        sa = [self.sbuf(f"za{f}", [128, 512], F32) for f in range(NCH)]
        sb = [self.sbuf(f"zb{f}", [128, 512], F32) for f in range(NCH)]
        x2t = [self.sbuf(f"zc{f}", [128, 512], F32) for f in range(NCH)]
        xb = [self.sbuf(f"zd{f}", [128, 512], BF16) for f in range(NCH)]
        t1 = x1t
        p = self.mk_ln_pools("x2")
        war = {}
        for g in range(4):
            scol = 641 + 512 * g

            def ld(y, en, g=g):
                self.wait(y, en, self.tk_e2b[g], war.get("in"), war.get("x2t"))
                i = None
                for f in range(NCH):
                    r = slice(128 * f, 128 * f + 128)
                    i = self.inc(y.dma_start(out=x1t[f][:],
                                             in_=self.d_x1[r, 512 * g:512 * g + 512]), sld, 16)
                    i = self.inc(y.dma_start(out=sa[f][:],
                                             in_=self.e2bo[g][r, 0:512]), sld, 16)
                    i = self.inc(y.dma_start(out=sb[f][:],
                                             in_=self.e2bo[g][r, 512:1024]), sld, 16)
                return i
            tk_ld = self.Y(ld)

            def x2f(v, en, g=g):
                self.wait(v, en, tk_ld, war.get("x2t"), war.get("x2st"))
                i = None
                for f in range(NCH):
                    i = v.scalar_tensor_tensor(out=x2t[f][:], in0=sb[f][:],
                                               scalar=self.vecs[:, V_HV:V_HV + 1], in1=x1t[f][:],
                                               op0=ALU.mult, op1=ALU.add)
                self.lk(v, en, i)
                for f in range(NCH):
                    i = v.scalar_tensor_tensor(out=x2t[f][:], in0=sa[f][:],
                                               scalar=self.hvi[:, 0:1], in1=x2t[f][:],
                                               op0=ALU.mult, op1=ALU.add)
                self.lk(v, en, i)
                for f in range(NCH):
                    i = v.tensor_scalar_add(x2t[f][:], x2t[f][:],
                                            self.vecs[:, V_BO + f:V_BO + f + 1])
                self.lk(v, en, i)
                for f in range(NCH):
                    i = v.tensor_copy(xb[f][:], x2t[f][:])
                return self.inc(i, sdve)
            tk_x2 = self.V(x2f)
            war["in"] = tk_x2

            def st1(y, en, g=g):
                self.wait(y, en, tk_x2)
                i = None
                for f in range(NCH):
                    i = self.inc(y.dma_start(out=self.d_x2[128 * f:128 * f + 128,
                                                           512 * g:512 * g + 512],
                                             in_=x2t[f][:]), sst, 16)
                return i
            war["x2st"] = self.Y(st1)
            tk_mub, tk_rb = self.ln_stats(xb, 512, p, tk_x2)

            def app(v, en, scol=scol):
                self.wait(v, en, tk_rb)
                i = None
                for f in range(NCH):
                    i = v.tensor_sub(t1[f][:], x2t[f][:], p["mub"][:, 0:512])
                self.lk(v, en, i)
                for f in range(NCH):
                    i = v.tensor_mul(self.S[f][:, scol:scol + 512], t1[f][:], p["rbs"][:, 0:512])
                return self.inc(i, sdve)
            tk_y3 = self.V(app)
            war["x2t"] = tk_y3
            p["war"]["mub"] = tk_y3
        self.full_barrier()
        self.end_scope(mark)
        self.exchange_halo(self.e3_in, self.e3_out)

    def ph_out(self):
        mark = self.scope()
        sld, sdve, sst = (self.sem(x) for x in ("x2ld", "dve", "yst"))
        x2t = [self.sbuf(f"ox{f}", [128, 512], F32) for f in range(NCH)]
        ot = [self.sbuf(f"oo{f}", [128, 512], F32) for f in range(NCH)]
        self.full_barrier()
        war = {}
        for g in range(4):
            scol = 641 + 512 * g

            def ld(y, en, g=g):
                self.wait(y, en, war.get("x2t"))
                i = None
                for f in range(NCH):
                    i = self.inc(y.dma_start(out=x2t[f][:],
                                             in_=self.d_x2[128 * f:128 * f + 128,
                                                           512 * g:512 * g + 512]), sld, 16)
                return i
            tk_ld = self.Y(ld)

            def of(v, en, scol=scol):
                self.wait(v, en, tk_ld, war.get("ot"))
                i = None
                for f in range(NCH):
                    i = v.scalar_tensor_tensor(out=ot[f][:], in0=self.S[f][:, scol:scol + 512],
                                               scalar=0.5, in1=x2t[f][:],
                                               op0=ALU.mult, op1=ALU.add)
                return self.inc(i, sdve)
            tk_o = self.V(of)
            war["x2t"] = tk_o

            def st(y, en, g=g):
                self.wait(y, en, tk_o)
                i = None
                for f in range(NCH):
                    i = self.inc(y.dma_start(out=self.d_y[128 * f:128 * f + 128,
                                                          512 * g:512 * g + 512],
                                             in_=ot[f][:]), sst, 16)
                return i
            war["ot"] = self.Y(st)
        self.end_scope(mark)

    def ph_dump(self):
        sst = self.sem("dbgst")
        self.full_barrier()

        def dp(y, en):
            i = None
            for f in range(NCH):
                i = self.inc(y.dma_start(out=self.d_dbg[f], in_=self.S[f][:]), sst, 16)
            return i
        self.Y(dp)

    def ph_fin(self):
        self.full_barrier()



def build_program():
    return MK().build()


def kernel(**inputs):
    nc = build_program()
    in_maps = [core_inputs(inputs, c) for c in range(8)]
    res = run_bass_kernel_spmd(nc, in_maps, core_ids=list(range(8)))
    out = np.zeros((B, SEQ, D), NF32)
    for c in range(8):
        b, par = c // 2, c % 2
        y = np.asarray(res.results[c]["y"], NF32)
        out[b, par * OWN:par * OWN + OWN, :] = y.T
    if DUMP:
        kernel.dumps = [{f: np.asarray(res.results[c][f"dbg{f}"], NF32) for f in range(NCH)}
                        for c in range(8)]
        kernel.vdram = [np.asarray(res.results[c]["vdram"]) for c in range(8)]
        kernel.cxdram = [np.asarray(res.results[c]["cxdram"]) for c in range(8)]
        kernel.raw = res
    return out


if __name__ == "__main__":
    mkk = MK()
    nc = mkk.build()
    n = sum(len(b.instructions) for fn in nc.m.functions for b in fn.blocks)
    print(f"built ok: stage={STAGE} instructions={n}")


# version 11
# speedup vs baseline: 1.0402x; 1.0402x over previous
import os
import numpy as np
import ml_dtypes
import concourse.bass as bass
import concourse.mybir as mybir
from concourse.bass_utils import run_bass_kernel_spmd

F32 = mybir.dt.float32
BF16 = mybir.dt.bfloat16
AF = mybir.ActivationFunctionType
ALU = mybir.AluOpType

NF32 = np.float32
NBF = ml_dtypes.bfloat16

B, SEQ, D = 4, 4096, 1024
HALO, OWN = 640, 2048
W = HALO + OWN
PW = 1 + W
NCH = 8
FH, DH, WIN = 8, 64, 512
HID, HIDP = 2730, 2816
NHC = HIDP // 128
BLK = 32
NEG = np.float32(-1e30)
QT0 = 512
NQT = 17
NKC = 21
NTI = 6
STAGE = int(os.environ.get("MK_STAGE", "99"))
DUMP = os.environ.get("MK_DUMP", "0") == "1"
STOP = os.environ.get("MK_STOP", "")
NSIDE = int(os.environ.get("MK_NSIDE", "2"))
NTILE = int(os.environ.get("MK_NTILE", str(NTI)))


def _bfarr(x):
    return np.ascontiguousarray(np.asarray(x, NF32).astype(NBF))


def rotary_tables(par):
    t = (np.arange(W, dtype=np.float64) + (par * OWN - HALO))
    half = np.arange(0, DH, 2, dtype=np.float64)
    inv = 1.0 / 10000.0 ** (half / DH)
    fr = np.concatenate([t[None, :] * inv[:, None]] * 2, 0)
    base = (half + 0.4 * DH) / (1.4 * DH)
    power = (t - SEQ // 2) / float(WIN // 2)
    sc = np.concatenate([base[:, None] ** power[None, :]] * 2, 0)
    cos, sin = np.cos(fr), np.sin(fr)
    csq, ssq = np.asarray(cos * sc, NF32), np.asarray(sin * sc, NF32)
    csk, ssk = np.asarray(cos / sc, NF32), np.asarray(sin / sc, NF32)
    pk = lambda a: np.concatenate([a, a], 0)
    return tuple(_bfarr(pk(a)) for a in (csq, ssq, csk, ssk))


def _allowed(par, qa, ka):
    qg = par * OWN - HALO + qa
    kg = par * OWN - HALO + ka
    return (qg - kg >= 0) & (qg - kg <= WIN) & (kg >= 0)


def maskq_blob(par):
    out = np.zeros((128, 6 * 640), NF32)
    for s in range(6):
        t = s if s < 5 else 8
        q0 = QT0 + 128 * t
        qa = q0 + np.arange(128)[:, None]
        ka = (q0 - 512) + np.arange(640)[None, :]
        out[:, 640 * s:640 * (s + 1)] = np.where(_allowed(par, qa, ka), 0.0, NEG)
    return _bfarr(out)


def kspan(c):
    return max(512, 128 * c), min(W, 128 * c + 640)


def maskk_blob(par):
    pieces, offs, pos = [], {}, 0
    for c in range(5):
        lo, hi = kspan(c)
        ka = 128 * c + np.arange(128)[:, None]
        qa = np.arange(lo, hi)[None, :]
        pieces.append(np.where(_allowed(par, qa, ka), 0.0, NEG))
        offs[c] = (pos, hi - lo)
        pos += hi - lo
    ka = np.arange(128)[:, None]
    qa = np.arange(640)[None, :]
    pieces.append(np.where((qa - ka >= 0) & (qa - ka <= WIN), 0.0, NEG))
    offs["gen"] = (pos, 640)
    pos += 640
    return _bfarr(np.concatenate(pieces, 1)), offs


MK_OFFS = maskk_blob(0)[1]


def maskk_slice(c):
    if c < 5:
        return MK_OFFS[c]
    lo, hi = kspan(c)
    g0, _ = MK_OFFS["gen"]
    return g0 + (lo - 128 * c), hi - lo


def maskv_blob():
    gv = np.arange(128)[:, None]
    i = np.arange(SEQ)[None, :]
    return _bfarr(np.where(gv > (i // BLK), 0.0, NEG))


def a_kaug_rows():
    j = np.arange(SEQ)[None, :]
    g = np.arange(16)[:, None]
    return _bfarr(np.where(((j % 512) // BLK) == g, NEG, 0.0))


def a_qaug_rows():
    i = np.arange(512)[None, :]
    g = np.arange(16)[:, None]
    return _bfarr((g > (i // BLK)).astype(NF32))


def blkind_blob():
    out = np.zeros((128, 32 * 128), NF32)
    t = np.arange(128)[:, None]
    g = np.arange(128)[None, :]
    for tc in range(32):
        out[:, 128 * tc:128 * (tc + 1)] = (g == (4 * tc + t // 32)).astype(NF32)
    return _bfarr(out)


CBF_R2 = 0
CBF_ID = 128
CBF_HS33 = 256
CBF_SEL33 = 289
CBF_ONES1 = 417
CBF_ONESROW = 418
CBF_ONES64 = 546
CBF_W = 674


def const_bf():
    R2 = np.zeros((128, 128), NF32)
    for hh in range(2):
        for d in range(32):
            R2[64 * hh + d, 64 * hh + d + 32] = -1.0
            R2[64 * hh + d + 32, 64 * hh + d] = 1.0
    R2T = R2.T.copy()
    ident = np.eye(128, dtype=NF32)
    hs33 = np.zeros((128, 33), NF32)
    hs33[0:64, 0] = 1.0
    hs33[64:128, 32] = 1.0
    sel33 = np.zeros((128, 128), NF32)
    sel33[0, 0:64] = 1.0
    sel33[32, 64:128] = 1.0
    ones1 = np.ones((128, 1), NF32)
    onesrow = np.zeros((128, 128), NF32)
    onesrow[0, :] = 1.0
    ones64 = np.zeros((128, 128), NF32)
    ones64[64, :] = 1.0
    return _bfarr(np.concatenate([R2T, ident, hs33, sel33, ones1, onesrow, ones64], 1))


CF_ID = 0
CF_NEG8 = 128
CF_ONESROW = 256
CF_EPS = 384
CF_W = 385


def const_f32():
    ident = np.eye(128, dtype=NF32)
    neg8 = (-8.0 * np.eye(128)).astype(NF32)
    onesrow = np.zeros((128, 128), NF32)
    onesrow[0, :] = 1.0
    onesrow[64, :] = 1.0
    eps = np.full((128, 1), 1e-5, NF32)
    return np.ascontiguousarray(np.concatenate([ident, neg8, onesrow, eps], 1))


VL = 70
V_QS, V_KS, V_LG, V_LB, V_B2, V_B1A, V_B1G = 0, 1, 2, 10, 18, 26, 48
V_BO, V_BQ, V_BK, V_BV, V_HV = 140, 148, 152, 156, 160
VW = 161


def vec_blob(inputs, par):
    cols = []

    def pp(v, width):
        return np.asarray(v, NF32).reshape(width, 128).T

    qs = np.asarray(inputs["L0_ffn_q_scale"], NF32)
    ks = np.asarray(inputs["L0_ffn_k_scale"], NF32)
    lg = np.asarray(inputs["L0_ffn_ln_g"], NF32)
    lb = np.asarray(inputs["L0_ffn_ln_b"], NF32)
    b2 = np.asarray(inputs["L0_ffn_b2"], NF32)
    b1 = np.asarray(inputs["L0_ffn_b1"], NF32)
    for l in range(2):
        cols.append(np.concatenate([qs[l], qs[l]]).reshape(128, 1))
        cols.append(np.concatenate([ks[l], ks[l]]).reshape(128, 1))
        cols.append(pp(lg[l], 8))
        cols.append(pp(lb[l], 8))
        cols.append(pp(b2[l], 8))
        b1a = np.zeros(HIDP, NF32)
        b1g = np.zeros(HIDP, NF32)
        b1a[:HID] = b1[l, :HID]
        b1g[:HID] = b1[l, HID:]
        cols.append(pp(b1a, NHC))
        cols.append(pp(b1g, NHC))
    hsl = slice(par * 512, par * 512 + 512)
    cols.append(pp(np.asarray(inputs["L0_attn_bo"], NF32), 8))
    cols.append(pp(np.asarray(inputs["L0_attn_bq"], NF32)[hsl], 4))
    cols.append(pp(np.asarray(inputs["L0_attn_bk"], NF32)[hsl], 4))
    cols.append(pp(np.asarray(inputs["L0_attn_bv"], NF32)[hsl], 4))
    cols.append(np.full((128, 1), 1.0 if par == 1 else 0.0, NF32))
    return np.ascontiguousarray(np.concatenate(cols, 1))


def _b16(a):
    return np.ascontiguousarray(np.asarray(a, NF32).astype(ml_dtypes.bfloat16))


def core_inputs(inputs, core):
    b, par = core // 2, core % 2
    x0 = np.asarray(inputs["L0_attn_hidden_states"], NF32)
    xb = np.ascontiguousarray(x0[b].T)
    x0w = np.zeros((D, PW), NF32)
    lo = par * OWN - HALO
    s0 = max(0, lo)
    x0w[:, 1 + (s0 - lo):] = xb[:, s0:par * OWN + OWN]
    w1 = np.asarray(inputs["L0_ffn_w1"], NF32)
    w1a = np.zeros((2, D, HIDP), NF32)
    w1g = np.zeros((2, D, HIDP), NF32)
    w1a[:, :, :HID] = w1[:, :, :HID]
    w1g[:, :, :HID] = w1[:, :, HID:]
    w2 = np.asarray(inputs["L0_ffn_w2"], NF32)
    w2p = np.zeros((2, HIDP, D), NF32)
    w2p[:, :HID, :] = w2
    hsl = slice(par * 512, par * 512 + 512)
    csq, ssq, csk, ssk = rotary_tables(par)
    return {
        "x0w": x0w,
        "wqkv": _b16(np.asarray(inputs["L0_ffn_wqkv"], NF32)),
        "wo": _b16(np.asarray(inputs["L0_ffn_wo"], NF32)),
        "w1a": _b16(w1a),
        "w1g": _b16(w1g),
        "w2p": _b16(w2p),
        "wqT": _b16(np.asarray(inputs["L0_attn_Wq"], NF32).T[:, hsl]),
        "wkT": _b16(np.asarray(inputs["L0_attn_Wk"], NF32).T[:, hsl]),
        "wvT": _b16(np.asarray(inputs["L0_attn_Wv"], NF32).T[:, hsl]),
        "woT": _b16(np.asarray(inputs["L0_attn_Wo"], NF32).T[hsl, :]),
        "bv512": _b16(np.asarray(inputs["L0_attn_bv"], NF32)[hsl].reshape(1, 512)),
        "csq": csq, "ssq": ssq, "csk": csk, "ssk": ssk,
        "maskq": maskq_blob(par), "maskk": maskk_blob(par)[0],
        "maskv": maskv_blob(), "kaugc": a_kaug_rows(), "qaugc": a_qaug_rows(),
        "blkind": blkind_blob(), "cbf": const_bf(), "cf32": const_f32(),
        "vecs": vec_blob(inputs, par),
    }


class MK:
    def __init__(self):
        self.nc = bass.Bass("TRN2", target_bir_lowering=False)
        self.ctx = []
        self.sems = {}
        self.waited = {}
        self.namecount = {}

    def enter(self, cm):
        v = cm.__enter__()
        self.ctx.append(cm)
        return v

    ARENA = 212736

    def _uniq(self, name):
        n = self.namecount.get(name, 0)
        self.namecount[name] = n + 1
        return name if n == 0 else f"{name}_{n}"

    def init_arena(self):
        self.arena_h = self.enter(self.nc.sbuf_tensor("arena", [128, self.ARENA], mybir.dt.uint8))
        self.arena_base = self.nc.lookup_mloc(self.arena_h).addr
        self.aptr = 0
        self.apeak = 0

    def sbuf(self, name, shape, dt):
        per = 1
        for s in shape[1:]:
            per *= s
        per *= mybir.dt.size(dt)
        addr = (self.aptr + 31) & ~31
        assert addr + per <= self.ARENA, f"arena overflow at {name}: {addr+per} > {self.ARENA}"
        self.aptr = addr + per
        self.apeak = max(self.apeak, self.aptr)
        return self.nc.alloc_sbuf_tensor_at(name, shape, dt,
                                            offset=self.arena_base + addr)

    def psum(self, name, shape, dt=F32):
        return self.enter(self.nc.psum_tensor(self._uniq(name), shape, dt))

    def scope(self):
        return (len(self.ctx), self.aptr)

    def end_scope(self, mark):
        n, ap = mark
        while len(self.ctx) > n:
            self.ctx.pop().__exit__(None, None, None)
        self.aptr = ap

    def sem(self, name):
        if name not in self.sems:
            h = self.enter(self.nc.semaphore(name))
            self.sems[name] = [h, 0, name]
        return self.sems[name]

    def inc(self, instr, s, k=1):
        instr.then_inc(s[0], k)
        s[1] += k
        return (s, s[1])

    def wait(self, eng, engname, *tickets):
        for tk in tickets:
            if tk is None:
                continue
            s, v = tk
            key = (engname, s[2])
            if self.waited.get(key, -1) < v:
                eng.wait_ge(s[0], v)
                self.waited[key] = v

    def lk(self, e, en, instr):
        tk = self.inc(instr, self.sem(en))
        self.wait(e, en, tk)

    def barrier_from(self, eng, engname, *semnames):
        for sn in semnames:
            s = self.sem(sn)
            if s[1] > 0:
                self.wait(eng, engname, (s, s[1]))

    def _run(self, sec, fn, en):
        out = []
        sec(lambda e: out.append(fn(e, en)))
        return out[0]

    def T(self, fn):
        return self._run(self.blk.tensor, fn, "pe")

    def V(self, fn):
        return self._run(self.blk.vector, fn, "dve")

    def C(self, fn):
        return self._run(self.blk.scalar, fn, "act")

    def G(self, fn):
        return self._run(self.blk.gpsimd, fn, "pool")

    def Y(self, fn):
        return self._run(self.blk.sync, fn, "sp")

    def full_barrier(self):
        def mk(e, en):
            self.barrier_from(e, en, *list(self.sems.keys()))
        self.T(mk)
        self.V(mk)
        self.C(mk)
        self.G(mk)
        self.Y(mk)

    def build(self):
        nc = self.nc
        ap = lambda t: t.ap()
        self.d_x0w = ap(nc.dram_tensor("x0w", [D, PW], F32, kind="ExternalInput"))
        self.d_wqkv = ap(nc.dram_tensor("wqkv", [2, D, 1536], BF16, kind="ExternalInput"))
        self.d_wo = ap(nc.dram_tensor("wo", [2, 512, D], BF16, kind="ExternalInput"))
        self.d_w1a = ap(nc.dram_tensor("w1a", [2, D, HIDP], BF16, kind="ExternalInput"))
        self.d_w1g = ap(nc.dram_tensor("w1g", [2, D, HIDP], BF16, kind="ExternalInput"))
        self.d_w2p = ap(nc.dram_tensor("w2p", [2, HIDP, D], BF16, kind="ExternalInput"))
        self.d_wqT = ap(nc.dram_tensor("wqT", [D, 512], BF16, kind="ExternalInput"))
        self.d_wkT = ap(nc.dram_tensor("wkT", [D, 512], BF16, kind="ExternalInput"))
        self.d_wvT = ap(nc.dram_tensor("wvT", [D, 512], BF16, kind="ExternalInput"))
        self.d_woT = ap(nc.dram_tensor("woT", [512, D], BF16, kind="ExternalInput"))
        self.d_bv512 = ap(nc.dram_tensor("bv512", [1, 512], BF16, kind="ExternalInput"))
        for n in ("csq", "ssq", "csk", "ssk"):
            setattr(self, "d_" + n, ap(nc.dram_tensor(n, [128, W], BF16, kind="ExternalInput")))
        self.d_maskq = ap(nc.dram_tensor("maskq", [128, 6 * 640], BF16, kind="ExternalInput"))
        mkw = maskk_blob(0)[0].shape[1]
        self.d_maskk = ap(nc.dram_tensor("maskk", [128, mkw], BF16, kind="ExternalInput"))
        self.d_maskv = ap(nc.dram_tensor("maskv", [128, SEQ], BF16, kind="ExternalInput"))
        self.d_kaugc = ap(nc.dram_tensor("kaugc", [16, SEQ], BF16, kind="ExternalInput"))
        self.d_qaugc = ap(nc.dram_tensor("qaugc", [16, 512], BF16, kind="ExternalInput"))
        self.d_blkind = ap(nc.dram_tensor("blkind", [128, 32 * 128], BF16, kind="ExternalInput"))
        self.d_cbf = ap(nc.dram_tensor("cbf", [128, CBF_W], BF16, kind="ExternalInput"))
        self.d_cf32 = ap(nc.dram_tensor("cf32", [128, CF_W], F32, kind="ExternalInput"))
        self.d_vecs = ap(nc.dram_tensor("vecs", [128, VW], F32, kind="ExternalInput"))
        self.d_y = ap(nc.dram_tensor("y", [D, OWN], F32, kind="ExternalOutput"))
        if DUMP:
            self.d_dbg = [ap(nc.dram_tensor(f"dbg{f}", [128, PW], F32, kind="ExternalOutput"))
                          for f in range(NCH)]
        self.d_x1 = ap(nc.dram_tensor("x1d", [D, OWN], F32, kind="Internal"))
        self.d_x2 = ap(nc.dram_tensor("x2d", [D, OWN], F32, kind="Internal"))
        self.e1_in = ap(nc.dram_tensor("e1_in", [D, 514], F32, kind="Internal"))
        self.e1_out = ap(nc.dram_tensor("e1_out", [2 * D, 514], F32, kind="Internal"))
        self.e2a_in = ap(nc.dram_tensor("e2a_in", [D, OWN], BF16, kind="Internal"))
        self.e2a_out = ap(nc.dram_tensor("e2a_out", [2 * D, OWN], BF16, kind="Internal"))
        self.e2bi = [ap(nc.dram_tensor(f"e2bi{j}", [D, 1024], F32, kind="Internal"))
                     for j in range(4)]
        self.e2bo = [ap(nc.dram_tensor(f"e2bo{j}", [D, 1024], F32, kind="Internal"))
                     for j in range(4)]
        ik = "ExternalOutput" if DUMP else "Internal"
        self.vdram = ap(nc.dram_tensor("vdram", [NKC * 128, 512], BF16, kind=ik))
        self.cxdram = ap(nc.dram_tensor("cxdram", [512, W - 512], BF16, kind=ik))
        self.e3_in = ap(nc.dram_tensor("e3_in", [D, 514], F32, kind="Internal"))
        self.e3_out = ap(nc.dram_tensor("e3_out", [2 * D, 514], F32, kind="Internal"))
        self.rg = [[0, 1], [2, 3], [4, 5], [6, 7]]

        self.init_arena()
        self.cbf = self.sbuf("cbf_sb", [128, CBF_W], BF16)
        self.cf32 = self.sbuf("cf32_sb", [128, CF_W], F32)
        self.vecs = self.sbuf("vecs_sb", [128, VW], F32)
        self.qs8 = self.sbuf("qs8", [128, 2], F32)
        self.bq8 = self.sbuf("bq8", [128, 4], F32)
        self.hvi = self.sbuf("hvi", [128, 1], F32)
        self.mq = self.sbuf("mq_sb", [128, 6 * 640], BF16)
        self.mkb = self.sbuf("mk_sb", [128, mkw], BF16)
        for sn in ("pe", "dve", "act", "pool", "cc",
                   "cld", "xld", "wvld", "vst0", "vst1", "vhld", "wqld", "tbld",
                   "cxst0", "cxst1", "cxld0", "cxld1", "wold", "w1ld0", "w1ld1",
                   "w2ld0", "w2ld1", "exst", "exld",
                   "x1st", "e2ast", "ald", "awld", "akld", "aqld", "bkld", "qmst",
                   "e2bst0", "e2bst1", "x2ld", "x2st", "yst", "dbgst", "dbgst2"):
            self.sem(sn)

        with self.nc.Block() as blk:
            self.blk = blk
            self.ph_consts()
            mark_sa = self.scope()
            self.S = [self.sbuf(f"Sa{f}", [128, PW], F32) for f in range(NCH)]
            self.s_init(0)
            self.ph_y1()
            if STAGE >= 2:
                self.f_call(first=True)
            if STAGE >= 5:
                self.ph_x1y2()
            if DUMP and STAGE < 6:
                self.ph_dump()
            if STAGE >= 6:
                self.full_barrier()
                self.end_scope(mark_sa)
                self.ph_A()
                mark_sb = self.scope()
                self.S = [self.sbuf(f"Sb{f}", [128, PW], F32) for f in range(NCH)]
                self.s_init(641)
                self.ph_x2y3()
                if STAGE >= 7:
                    self.f_call(first=False)
                    self.ph_out()
                if DUMP:
                    self.ph_dump()
                self.full_barrier()
                self.end_scope(mark_sb)
            elif STAGE >= 7 or not DUMP:
                self.ph_out()
            self.ph_fin()
        while self.ctx:
            self.ctx.pop().__exit__(None, None, None)
        self.arena_peak = self.apeak
        return self.nc

    def ph_consts(self):
        sld, sdve = self.sem("cld"), self.sem("dve")

        def loads(y, en):
            tk = None
            for dst, src in ((self.cbf, self.d_cbf), (self.cf32, self.d_cf32),
                             (self.vecs, self.d_vecs), (self.mq, self.d_maskq),
                             (self.mkb, self.d_maskk)):
                tk = self.inc(y.dma_start(out=dst[:], in_=src), sld, 16)
            return tk
        tk = self.Y(loads)

        def prep(v, en):
            self.wait(v, en, tk)
            for l in range(2):
                v.tensor_scalar_mul(self.qs8[:, l:l + 1],
                                    self.vecs[:, VL * l + V_QS:VL * l + V_QS + 1], 8.0)
            v.tensor_scalar_mul(self.bq8[:], self.vecs[:, V_BQ:V_BQ + 4], 0.125)
            i = v.tensor_scalar(out=self.hvi[:], in0=self.vecs[:, V_HV:V_HV + 1],
                                scalar1=-1.0, scalar2=1.0, op0=ALU.mult, op1=ALU.add)
            return self.inc(i, sdve)
        self.V(prep)

    def s_init(self, upto):
        def z(v, en):
            i = None
            for f in range(NCH):
                i = v.memset(self.S[f][:, 0:max(1, upto)], 0.0)
            return self.inc(i, self.sem("dve"))
        self.V(z)

    def mk_ln_pools(self, tag):
        return {
            "sq": [self.sbuf(f"{tag}sq{f}", [128, 512], BF16) for f in range(NCH)],
            "mu": self.sbuf(f"{tag}mu", [1, 512], F32),
            "ex2": self.sbuf(f"{tag}e2", [1, 512], F32),
            "var": self.sbuf(f"{tag}va", [1, 512], F32),
            "sd": self.sbuf(f"{tag}sd", [1, 512], F32),
            "rstd": self.sbuf(f"{tag}rs", [1, 512], F32),
            "rbs": self.sbuf(f"{tag}rb", [128, 512], F32),
            "sx": self.psum(f"{tag}px", [1, 512]),
            "sq2": self.psum(f"{tag}pq", [1, 512]),
            "mub": self.psum(f"{tag}pm", [128, 512]),
            "rbp": self.psum(f"{tag}pr", [128, 512]),
            "war": {},
        }

    def ln_stats(self, xb_tiles, Tt, p, tk_in):
        spe, sdve, sact = self.sem("pe"), self.sem("dve"), self.sem("act")
        spool = self.sem("pool")

        def sqf(g, en):
            self.wait(g, en, tk_in, p["war"].get("sq"))
            i = None
            for f in range(NCH):
                i = g.tensor_mul(p["sq"][f][:, 0:Tt], xb_tiles[f][:, 0:Tt],
                                 xb_tiles[f][:, 0:Tt])
            return self.inc(i, spool)
        tk_sq = self.G(sqf)

        def mm(t, en):
            self.wait(t, en, tk_sq, p["war"].get("sx"))
            for f in range(NCH):
                t.matmul(p["sx"][0:1, 0:Tt], self.cbf[:, CBF_ONES1:CBF_ONES1 + 1],
                         xb_tiles[f][:, 0:Tt], start=(f == 0), stop=(f == NCH - 1))
            i = None
            for f in range(NCH):
                i = t.matmul(p["sq2"][0:1, 0:Tt], self.cbf[:, CBF_ONES1:CBF_ONES1 + 1],
                             p["sq"][f][:, 0:Tt], start=(f == 0), stop=(f == NCH - 1))
            return self.inc(i, spe)
        tk_mm = self.T(mm)
        p["war"]["sq"] = tk_mm

        def strow(v, en):
            self.wait(v, en, tk_mm)
            i = v.tensor_scalar_mul(p["mu"][:, 0:Tt], p["sx"][0:1, 0:Tt], 1.0 / D)
            self.lk(v, en, i)
            v.tensor_scalar_mul(p["ex2"][:, 0:Tt], p["sq2"][0:1, 0:Tt], 1.0 / D)
            i = v.tensor_mul(p["var"][:, 0:Tt], p["mu"][:, 0:Tt], p["mu"][:, 0:Tt])
            self.lk(v, en, i)
            i = v.tensor_sub(p["var"][:, 0:Tt], p["ex2"][:, 0:Tt], p["var"][:, 0:Tt])
            return self.inc(i, sdve)
        tk_st = self.V(strow)
        p["war"]["sx"] = tk_st

        def sdf(c, en):
            self.wait(c, en, tk_st)
            i = c.activation(out=p["sd"][:, 0:Tt], in_=p["var"][:, 0:Tt], func=AF.Sqrt,
                             bias=self.cf32[0:1, CF_EPS:CF_EPS + 1])
            return self.inc(i, sact)
        tk_sd = self.C(sdf)

        def rs(v, en):
            self.wait(v, en, tk_sd)
            i = v.reciprocal(p["rstd"][:, 0:Tt], p["sd"][:, 0:Tt])
            return self.inc(i, sdve)
        tk_rs = self.V(rs)

        def bc(t, en):
            self.wait(t, en, tk_rs, p["war"].get("mub"), p["war"].get("rbp"))
            t.matmul(p["mub"][:, 0:Tt], self.cf32[0:1, CF_ONESROW:CF_ONESROW + 128],
                     p["mu"][:, 0:Tt], start=True, stop=True)
            i = t.matmul(p["rbp"][:, 0:Tt], self.cf32[0:1, CF_ONESROW:CF_ONESROW + 128],
                         p["rstd"][:, 0:Tt], start=True, stop=True)
            return self.inc(i, spe)
        tk_bc = self.T(bc)

        def rb(v, en):
            self.wait(v, en, tk_bc)
            i = v.tensor_copy(p["rbs"][:, 0:Tt], p["rbp"][:, 0:Tt])
            return self.inc(i, sdve)
        tk_rb = self.V(rb)
        p["war"]["rbp"] = tk_rb
        return tk_bc, tk_rb

    def ph_y1(self):
        mark = self.scope()
        sld, sdve = self.sem("xld"), self.sem("dve")
        xt = [self.sbuf(f"p1x{f}", [128, 512], F32) for f in range(NCH)]
        xb = [self.sbuf(f"p1b{f}", [128, 512], BF16) for f in range(NCH)]
        t1 = [self.sbuf(f"p1t{f}", [128, 512], F32) for f in range(NCH)]
        p = self.mk_ln_pools("p1")
        war_x = None
        for ti in range(NTI):
            a0 = 1 + 512 * ti
            Tt = min(512, PW - a0)

            def ld(y, en):
                self.wait(y, en, war_x)
                i = None
                for f in range(NCH):
                    i = self.inc(y.dma_start(out=xt[f][:, 0:Tt],
                                             in_=self.d_x0w[128 * f:128 * f + 128, a0:a0 + Tt]),
                                 sld, 16)
                return i
            tk_ld = self.Y(ld)

            def cast(v, en):
                self.wait(v, en, tk_ld)
                i = None
                for f in range(NCH):
                    i = v.tensor_copy(xb[f][:, 0:Tt], xt[f][:, 0:Tt])
                return self.inc(i, sdve)
            tk_cast = self.V(cast)
            tk_mub, tk_rb = self.ln_stats(xb, Tt, p, tk_cast)

            def app(v, en):
                self.wait(v, en, tk_rb)
                i = None
                for f in range(NCH):
                    i = v.tensor_sub(t1[f][:, 0:Tt], xt[f][:, 0:Tt], p["mub"][:, 0:Tt])
                self.lk(v, en, i)
                for f in range(NCH):
                    i = v.tensor_mul(self.S[f][:, a0:a0 + Tt], t1[f][:, 0:Tt], p["rbs"][:, 0:Tt])
                return self.inc(i, sdve)
            tk_ap = self.V(app)
            war_x = tk_ap
            p["war"]["mub"] = tk_ap
        self.full_barrier()
        self.end_scope(mark)

    def cbar(self):
        self.full_barrier()

    def f_call(self, first):
        maxil = 2
        if first and (STAGE == 2 or STOP):
            maxil = 1
        for l in range(maxil):
            self.f_il(l)
            if l == 0 and not STOP:
                self.exchange_halo(self.e1_in, self.e1_out)

    def exchange_halo(self, din, dout):
        mark = self.scope()
        sst, scc, sld, sdve = (self.sem(x) for x in ("exst", "cc", "exld", "dve"))
        hr = [self.sbuf(f"hr{f}", [128, 514], F32) for f in range(NCH)]

        def snd(y, en):
            self.barrier_from(y, en, "dve")
            i = None
            for f in range(NCH):
                i = self.inc(y.dma_start(out=din[128 * f:128 * f + 128, :],
                                         in_=self.S[f][:, 2175:2689]), sst, 16)
            return i
        tk = self.Y(snd)

        def coll(g, en):
            self.wait(g, en, tk)
            return self.inc(g.collective_compute(
                "AllGather", ALU.bypass, replica_groups=self.rg,
                ins=[din.opt()], outs=[dout.opt()]), scc, 1)
        tkc = self.G(coll)

        def rcv(y, en):
            self.wait(y, en, tkc)
            i = None
            for f in range(NCH):
                i = self.inc(y.dma_start(out=hr[f][:], in_=dout[128 * f:128 * f + 128, :]),
                             sld, 16)
            return i
        tkr = self.Y(rcv)

        def wr(v, en):
            self.wait(v, en, tkr)
            i = None
            for f in range(NCH):
                i = v.tensor_scalar_mul(self.S[f][:, 127:641], hr[f][:],
                                        self.vecs[:, V_HV:V_HV + 1])
            return self.inc(i, sdve)
        self.V(wr)
        self.full_barrier()
        self.end_scope(mark)

    def f_il(self, l):
        spe, sdve, sact, spool = (self.sem(x) for x in ("pe", "dve", "act", "pool"))
        mark_il = self.scope()
        qaug = [self.sbuf(f"qaug{h}", [65, W], BF16) for h in range(2)]
        kaug = [self.sbuf(f"kaug{h}", [65, W], BF16) for h in range(2)]

        mark_sb = self.scope()
        Sb = [self.sbuf(f"Sh{f}", [128, PW], BF16) for f in range(NCH)]

        def cast(v, en):
            self.barrier_from(v, en, *list(self.sems.keys()))
            for f in range(NCH):
                v.tensor_copy(Sb[f][:], self.S[f][:])
            v.memset(kaug[0][64:65, :], 1.0)
            v.memset(kaug[1][64:65, :], 1.0)
            v.memset(qaug[0][64:65, :], 0.0)
            i = v.memset(qaug[1][64:65, :], 0.0)
            return self.inc(i, sdve)
        tk_cast = self.V(cast)

        def xsh(f, a0, Tt):
            off = 1 if f < 4 else 0
            return Sb[f][:, off + a0:off + a0 + Tt]

        mark_v = self.scope()
        wv = [self.sbuf(f"wv{f}", [128, 512], BF16) for f in range(NCH)]

        def ldwv(g, en):
            i = None
            for f in range(NCH):
                i = self.inc(g.dma_start(out=wv[f][:],
                                         in_=self.d_wqkv[l, 128 * f:128 * f + 128, 1024:1536]),
                             self.sem("wvld"), 16)
            return i
        tk_wv = self.Y(ldwv)
        vstg = [self.sbuf(f"vst{i}", [128, 512], BF16) for i in range(2)]
        vps = [self.psum(f"vps{i}", [128, 512]) for i in range(2)]
        warv = [None, None]
        tk_vst = None
        for ts in range(NKC):
            a0, bi = 128 * ts, ts % 2

            def vmm(t, en, a0=a0, bi=bi):
                self.wait(t, en, tk_cast, tk_wv)
                if warv[bi] is not None:
                    self.wait(t, en, warv[bi])
                i = None
                for f in range(NCH):
                    i = t.matmul(vps[bi][:], xsh(f, a0, 128), wv[f][:],
                                 start=(f == 0), stop=(f == NCH - 1))
                return self.inc(i, spe)
            tkv = self.T(vmm)

            def vev(v, en, bi=bi):
                self.wait(v, en, tkv, warv[bi])
                return self.inc(v.tensor_copy(vstg[bi][:], vps[bi][:]), sdve)
            tke = self.V(vev)

            def vst(y, en, bi=bi, ts=ts):
                self.wait(y, en, tke)
                return self.inc(y.dma_start(out=self.vdram[128 * ts:128 * ts + 128, :],
                                            in_=vstg[bi][:]), self.sem(f"vst{bi}"), 16)
            warv[bi] = self.Y(vst)
        tk_vblob = (warv[0], warv[1])
        self.cbar()
        self.end_scope(mark_v)

        nhp = 1 if STOP == "i" else 4
        tcs = self.sbuf("tcs", [128, W], BF16)
        tss = self.sbuf("tss", [128, W], BF16)
        wqkq = [self.sbuf(f"wqkq{s}", [128, 1024], BF16) for s in range(2)]
        wqkk = [self.sbuf(f"wqkk{s}", [128, 1024], BF16) for s in range(2)]
        ilwar = {}
        for hp in range(nhp):
            mark_hp = self.scope()

            def ldw(g, en, hp=hp):
                self.wait(g, en, ilwar.get(("wqk", hp % 2)))
                self.inc(g.dma_start(
                    out=wqkq[hp % 2][:].rearrange("p (c n) -> p c n", c=8),
                    in_=self.d_wqkv[l].rearrange("(c p) n -> p c n", p=128)[:, :, 128 * hp:128 * hp + 128]),
                    self.sem("wqld"), 16)
                return self.inc(g.dma_start(
                    out=wqkk[hp % 2][:].rearrange("p (c n) -> p c n", c=8),
                    in_=self.d_wqkv[l].rearrange("(c p) n -> p c n", p=128)[:, :, 512 + 128 * hp:512 + 128 * hp + 128]),
                    self.sem("wqld"), 16)
            tk_wqk = self.Y(ldw)

            mark_i = self.scope()
            pps = [self.psum(f"ipp{i}", [128, 512]) for i in range(2)]
            nps = [self.psum(f"inp{i}", [33, 512]) for i in range(2)]
            bps = [self.psum(f"ibp{i}", [128, 512]) for i in range(2)]
            rps = [self.psum(f"irp{i}", [128, 512]) for i in range(2)]
            sqs = [self.sbuf(f"isq{i}", [128, 512], BF16) for i in range(2)]
            nrm = [self.sbuf(f"inr{i}", [33, 512], BF16) for i in range(2)]
            nrr = [self.sbuf(f"inrr{i}", [33, 512], BF16) for i in range(2)]
            bcb = [self.sbuf(f"ibc{i}", [128, 512], BF16) for i in range(2)]
            qn = [self.sbuf(f"iqn{i}", [128, 512], BF16) for i in range(2)]
            rawb = [self.sbuf(f"irw{i}", [128, 512], F32) for i in range(2)]
            qcs = [self.sbuf(f"iqc{i}", [128, 512], BF16) for i in range(2)]
            qss = [self.sbuf(f"iqs{i}", [128, 512], BF16) for i in range(2)]
            war = {}
            self.cbar()
            tk10 = None
            for side in range(NSIDE):
                wsb = wqkq[hp % 2] if side == 0 else wqkk[hp % 2]
                dst = qaug if side == 0 else kaug

                def ldt(g, en, side=side):
                    self.wait(g, en, ilwar.get("tabs"))
                    srcs = (self.d_csq, self.d_ssq) if side == 0 else (self.d_csk, self.d_ssk)
                    self.inc(g.dma_start(out=tcs[:], in_=srcs[0]), self.sem("tbld"), 16)
                    return self.inc(g.dma_start(out=tss[:], in_=srcs[1]), self.sem("tbld"), 16)
                tk_tab = self.Y(ldt)

                for ti in range(NTILE):
                    a0 = 512 * ti
                    Tt = min(512, W - a0)
                    bi = ti % 2

                    def qmm(t, en, a0=a0, Tt=Tt, bi=bi, wsb=wsb):
                        self.wait(t, en, tk_cast, tk_wqk, war.get(("pp", bi)))
                        i = None
                        for f in range(NCH):
                            i = t.matmul(pps[bi][:, 0:Tt], wsb[:, 128 * f:128 * f + 128],
                                         xsh(f, a0, Tt), start=(f == 0),
                                         stop=(f == NCH - 1))
                        return self.inc(i, spe)
                    tk1 = self.T(qmm)

                    def rawev(v, en, Tt=Tt, bi=bi):
                        self.wait(v, en, tk1, war.get(("raw", bi)))
                        return self.inc(v.tensor_copy(rawb[bi][:, 0:Tt],
                                                      pps[bi][:, 0:Tt]), sdve)
                    tk1b = self.V(rawev)
                    war[("pp", bi)] = tk1b

                    def sqf(g_, en, Tt=Tt, bi=bi):
                        self.wait(g_, en, tk1b, war.get(("sqs", bi)))
                        return self.inc(g_.tensor_mul(sqs[bi][:, 0:Tt], rawb[bi][:, 0:Tt],
                                                      rawb[bi][:, 0:Tt]), spool)
                    tk2 = self.G(sqf)

                    def nmm(t, en, Tt=Tt, bi=bi):
                        self.wait(t, en, tk2, war.get(("np", bi)))
                        return self.inc(t.matmul(nps[bi][0:33, 0:Tt],
                                                 self.cbf[:, CBF_HS33:CBF_HS33 + 33],
                                                 sqs[bi][:, 0:Tt], start=True, stop=True), spe)
                    tk3 = self.T(nmm)
                    war[("sqs", bi)] = tk3

                    def sqrtf(c, en, Tt=Tt, bi=bi):
                        self.wait(c, en, tk3, war.get(("nrm", bi)))
                        return self.inc(c.activation(out=nrm[bi][:, 0:Tt], in_=nps[bi][0:33, 0:Tt],
                                                     func=AF.Sqrt), sact)
                    tk4 = self.C(sqrtf)
                    war[("np", bi)] = tk4

                    def recf(v, en, Tt=Tt, bi=bi):
                        self.wait(v, en, tk4, war.get(("nrr", bi)))
                        self.lk(v, en, v.tensor_scalar_max(nrm[bi][:, 0:Tt], nrm[bi][:, 0:Tt], 1e-12))
                        with self.nc.allow_low_precision(reason="bf16 recip-norm feeds bf16 matmul broadcast; modeled in mk_math"):
                            i = v.reciprocal(nrr[bi][:, 0:Tt], nrm[bi][:, 0:Tt])
                        return self.inc(i, sdve)
                    tk5 = self.V(recf)
                    war[("nrm", bi)] = tk5

                    def bcf(t, en, Tt=Tt, bi=bi):
                        self.wait(t, en, tk5, war.get(("bp", bi)))
                        return self.inc(t.matmul(bps[bi][:, 0:Tt],
                                                 self.cbf[0:33, CBF_SEL33:CBF_SEL33 + 128],
                                                 nrr[bi][:, 0:Tt], start=True, stop=True), spe)
                    tk6 = self.T(bcf)
                    war[("nrr", bi)] = tk6

                    def bcc(v, en, Tt=Tt, bi=bi):
                        self.wait(v, en, tk6, war.get(("bcb", bi)))
                        return self.inc(v.tensor_copy(bcb[bi][:, 0:Tt], bps[bi][:, 0:Tt]),
                                        sdve)
                    tk7 = self.V(bcc)
                    war[("bp", bi)] = tk7

                    def stf(v, en, Tt=Tt, bi=bi, side=side):
                        self.wait(v, en, tk7, war.get(("qn", bi)))
                        sc = self.qs8[:, l:l + 1] if side == 0 else \
                            self.vecs[:, VL * l + V_KS:VL * l + V_KS + 1]
                        i = v.scalar_tensor_tensor(out=qn[bi][:, 0:Tt], in0=rawb[bi][:, 0:Tt],
                                                   scalar=sc, in1=bcb[bi][:, 0:Tt],
                                                   op0=ALU.mult, op1=ALU.mult)
                        return self.inc(i, sdve)
                    tk8a = self.V(stf)
                    war[("raw", bi)] = tk8a
                    war[("bcb", bi)] = tk8a

                    def csf(g_, en, Tt=Tt, bi=bi, a0=a0):
                        self.wait(g_, en, tk8a, tk_tab, war.get(("qn", bi)))
                        g_.tensor_mul(qcs[bi][:, 0:Tt], qn[bi][:, 0:Tt], tcs[:, a0:a0 + Tt])
                        return self.inc(g_.tensor_mul(qss[bi][:, 0:Tt], qn[bi][:, 0:Tt],
                                                      tss[:, a0:a0 + Tt]), spool)
                    tk8 = self.G(csf)
                    ilwar["tabs"] = tk8

                    def rotf(t, en, Tt=Tt, bi=bi):
                        self.wait(t, en, tk8, war.get(("rp", bi)))
                        return self.inc(t.matmul(rps[bi][:, 0:Tt], self.cbf[:, CBF_R2:CBF_R2 + 128],
                                                 qss[bi][:, 0:Tt], start=True, stop=True), spe)
                    tk9 = self.T(rotf)

                    def evf(v, en, Tt=Tt, bi=bi, a0=a0, dst=dst):
                        self.wait(v, en, tk9)
                        self.lk(v, en, v.tensor_add(dst[0][0:64, a0:a0 + Tt],
                                                    rps[bi][0:64, 0:Tt],
                                                    qcs[bi][0:64, 0:Tt]))
                        return self.inc(v.tensor_add(dst[1][0:64, a0:a0 + Tt],
                                                     rps[bi][64:128, 0:Tt],
                                                     qcs[bi][64:128, 0:Tt]), sdve)
                    tk10 = self.V(evf)
                    war[("rp", bi)] = tk10
                    war[("qn", bi)] = tk10
                war[("aug", side)] = tk10
                ilwar[("wqk", hp % 2)] = tk1
            self.cbar()
            self.end_scope(mark_i)
            if STOP != "i":
                self.f_attn_hp(l, hp, qaug, kaug,
                               war[("aug", 0)], war[("aug", 1)], tk_vblob)
            self.end_scope(mark_hp)
        self.cbar()
        if STOP == "i":
            self.full_barrier()

            def dmpa(g, en):
                i = None
                tensors = (qaug[0], qaug[1], kaug[0], kaug[1])[:2 * NSIDE]
                wd = 512 * NTILE
                for k, t in enumerate(tensors):
                    i = self.inc(g.dma_start(out=self.d_dbg[k][0:65, 0:wd], in_=t[0:65, 0:wd]),
                                 self.sem("dbgst2"), 16)
                return i
            self.G(dmpa)
            self.full_barrier()
            return
        self.end_scope(mark_sb)

        mark_wo = self.scope()
        wosb = [self.sbuf(f"wosb{c}", [128, 1024], BF16) for c in range(4)]

        def ldwo(g, en):
            i = None
            for c in range(4):
                i = self.inc(g.dma_start(out=wosb[c][:],
                                         in_=self.d_wo[l, 128 * c:128 * c + 128, :]),
                             self.sem("wold"), 16)
            return i
        tk_wo = self.Y(ldwo)
        wop = [self.psum(f"wop{i}", [128, 512]) for i in range(2)]
        cxt = [[self.sbuf(f"cxt{k}_{i}", [128, 512], BF16) for k in range(4)] for i in range(2)]
        warw = [None, None]
        warcx = [None, None]
        groups5 = [(512, 640)] + [(640 + 512 * g, 1152 + 512 * g) for g in range(4)]
        self.cbar()
        for gi, (glo, ghi) in enumerate(groups5):
            Tg = ghi - glo
            cbi = gi % 2

            def cxld(y, en, glo=glo, ghi=ghi, Tg=Tg, cbi=cbi):
                self.wait(y, en, warcx[cbi])
                self.barrier_from(y, en, "cxst0", "cxst1")
                i = None
                for kc in range(4):
                    i = self.inc(y.dma_start(out=cxt[cbi][kc][:, 0:Tg],
                                             in_=self.cxdram[128 * kc:128 * kc + 128,
                                                             glo - 512:ghi - 512]),
                                 self.sem(f"cxld{cbi}"), 16)
                return i
            tk_cx = self.Y(cxld)
            for f in range(NCH):
                bi = (gi * NCH + f) % 2

                def womm(t, en, f=f, bi=bi, cbi=cbi, Tg=Tg):
                    self.wait(t, en, tk_wo, tk_cx, warw[bi])
                    i = None
                    for kc in range(4):
                        i = t.matmul(wop[bi][:, 0:Tg], wosb[kc][:, 128 * f:128 * f + 128],
                                     cxt[cbi][kc][:, 0:Tg],
                                     start=(kc == 0), stop=(kc == 3))
                    return self.inc(i, spe)
                tkw = self.T(womm)
                if f == NCH - 1:
                    warcx[cbi] = tkw

                def woev(v, en, f=f, bi=bi, glo=glo, ghi=ghi, Tg=Tg):
                    self.wait(v, en, tkw)
                    i = v.scalar_tensor_tensor(out=self.S[f][:, glo + 1:ghi + 1],
                                               in0=wop[bi][:, 0:Tg], scalar=1.0,
                                               in1=self.S[f][:, glo + 1:ghi + 1],
                                               op0=ALU.mult, op1=ALU.add)
                    return self.inc(i, sdve)
                warw[bi] = self.V(woev)

        def hz(v, en):
            i = None
            for f in range(NCH):
                i = v.tensor_scalar_mul(self.S[f][:, 640:641], self.S[f][:, 640:641],
                                        self.vecs[:, V_HV:V_HV + 1])
            return self.inc(i, sdve)
        self.V(hz)
        self.cbar()
        self.end_scope(mark_wo)

        if STOP != "ii":
            self.f_mlp(l)
        self.end_scope(mark_il)

    def f_attn_hp(self, l, hp, qaug, kaug, tk_qaug, tk_kaug, tk_vblob):
        spe, sdve, sact, sld, spool = (self.sem(x) for x in
                                       ("pe", "dve", "act", "vhld", "pool"))
        sst = [self.sem("cxst0"), self.sem("cxst1")]
        groups5 = [(512, 640)] + [(640 + 512 * g, 1152 + 512 * g) for g in range(4)]
        for hl in range(2):
            hg = 2 * hp + hl
            mark_head = self.scope()
            vh = self.sbuf("vh", [128, NKC * 65], BF16)

            def vini(v, en):
                self.barrier_from(v, en, "pe")
                return self.inc(v.memset(vh[:], 1.0), sdve)
            tk_vh0 = self.V(vini)

            def vhl(y, en, hg=hg):
                self.wait(y, en, *tk_vblob)
                self.wait(y, en, tk_vh0)
                i = y.dma_start(
                    out=vh[:].rearrange("p (c e) -> p c e", c=NKC)[:, :, 0:64],
                    in_=self.vdram.rearrange("(c p) n -> p c n", p=128)[:, :, 64 * hg:64 * hg + 64])
                return self.inc(i, sld, 16)
            tk_vh = self.Y(vhl)

            mark_q = self.scope()
            pq = [self.psum(f"pq{i}", [128, 640]) for i in range(3)]
            tpp = self.psum("tpq", [NQT, 128])
            tpb = self.sbuf("tpb", [NQT, 128], F32)
            smq = [self.sbuf(f"smq{i}", [128, 640], F32) for i in range(2)]
            e8 = [self.sbuf(f"e8x{i}", [128, 640], BF16) for i in range(2)]
            s8a = self.sbuf("s8a", [128, NQT], F32)
            lcol = self.sbuf("lcq", [128, NQT], F32)
            war = {}
            self.cbar()
            tk3 = None
            msk_tk = {}
            for t in range(NQT):
                bi = t % 3
                mi = t if t < 5 else 5

                def qmm(tt, en, t=t, bi=bi):
                    self.wait(tt, en, tk_qaug, tk_kaug, war.get(("pq", bi)))
                    a = 128 * t
                    tt.matmul(pq[bi][:, 0:512], qaug[hl][0:64, 512 + a:640 + a],
                              kaug[hl][0:64, a:a + 512], start=True, stop=True)
                    i = tt.matmul(pq[bi][:, 512:640], qaug[hl][0:64, 512 + a:640 + a],
                                  kaug[hl][0:64, a + 512:a + 640], start=True, stop=True)
                    return self.inc(i, spe)
                tk1 = self.T(qmm)

                def mskf(v, en, t=t, bi=bi, mi=mi):
                    self.wait(v, en, tk1, war.get(("smq", t % 2)))
                    i = v.tensor_tensor(smq[t % 2][:], pq[bi][:],
                                        self.mq[:, 640 * mi:640 * mi + 640], ALU.add)
                    return self.inc(i, sdve)
                msk_tk[t] = self.V(mskf)
                war[("pq", bi)] = msk_tk[t]

                if t % 2 == 1 or t == NQT - 1:
                    ts = [t - 1, t] if t % 2 == 1 else [t]

                    def expf(c, en, ts=tuple(ts)):
                        self.wait(c, en, war.get("e8"))
                        for tj in ts:
                            self.wait(c, en, msk_tk[tj], war.get(("smq", tj % 2)))
                        i = None
                        for tj in ts:
                            i = c.activation(out=e8[tj % 2][:], in_=smq[tj % 2][:],
                                             func=AF.Exp, scale=0.125,
                                             accum_out=s8a[:, tj:tj + 1])
                        return self.inc(i, sact)
                    tk3 = self.C(expf)
                    war["e8"] = tk3
                    for tj in ts:
                        war[("smq", tj % 2)] = tk3

            def maxf(v, en):
                self.wait(v, en, tk3)
                i = v.tensor_scalar_max(s8a[:], s8a[:], 1e-35)
                return self.inc(i, sdve)
            tk4 = self.V(maxf)

            def lnf(c, en):
                self.wait(c, en, tk4)
                i = c.activation(out=lcol[:], in_=s8a[:], func=AF.Ln)
                return self.inc(i, sact)
            tk5 = self.C(lnf)

            def trf(tt, en):
                self.wait(tt, en, tk5)
                i = tt.transpose(tpp[:], lcol[:], self.cf32[:, CF_ID:CF_ID + 128])
                return self.inc(i, spe)
            tk6 = self.T(trf)

            def tcp(v, en):
                self.wait(v, en, tk6)
                return self.inc(v.tensor_scalar_mul(tpb[:], tpp[:], -8.0), sdve)
            tk6b = self.V(tcp)

            def mcp(g, en, hl=hl):
                self.wait(g, en, tk6b)
                i = g.dma_start(out=qaug[hl][64:65, 512:512 + 128 * NQT],
                                in_=tpb[0:NQT, 0:128])
                return self.inc(i, self.sem("qmst"), 16)
            tk_mrow = self.G(mcp)
            self.cbar()
            self.end_scope(mark_q)

            mark_k = self.scope()
            sk = [self.psum(f"sk{i}", [128, 640]) for i in range(3)]
            gctx = [self.psum(f"gx{i}", [65, 512]) for i in range(2)]
            zbp = gctx[1]
            smk = [self.sbuf(f"smk{i}", [128, 640], F32) for i in range(3)]
            probs = [self.sbuf(f"pb{i}", [128, 640], BF16) for i in range(2)]
            cstg = self.sbuf("cstg", [65, 2176], F32)
            ctev = self.sbuf("ctevh", [64, 2176], BF16)
            zrh = cstg[64:65, :]
            stg_tk = {}
            kmsk_tk = {}
            exp_tk = {}
            war = {}
            self.cbar()
            gfirst = {gi: True for gi in range(5)}
            ghw = {gi: 0 for gi in range(5)}
            for c in range(NKC):
                qlo, qhi = kspan(c)
                spanw = qhi - qlo
                bi = c % 3
                pi = c % 2
                mo, mw = maskk_slice(c)
                assert mw == spanw

                def kmm(tt, en, c=c, bi=bi, qlo=qlo, qhi=qhi):
                    self.wait(tt, en, tk_mrow, tk_kaug, war.get(("sk", bi)))
                    i = None
                    off = 0
                    while off < qhi - qlo:
                        wpc = min(512, qhi - qlo - off)
                        i = tt.matmul(sk[bi][:, off:off + wpc],
                                      kaug[hl][0:65, 128 * c:128 * c + 128],
                                      qaug[hl][0:65, qlo + off:qlo + off + wpc],
                                      start=True, stop=True)
                        off += wpc
                    return self.inc(i, spe)
                tk1 = self.T(kmm)

                def mskf(v, en, bi=bi, mo=mo, mw=mw):
                    self.wait(v, en, tk1, war.get(("smk", bi)))
                    i = v.tensor_tensor(smk[bi][:, 0:mw], sk[bi][:, 0:mw],
                                        self.mkb[:, mo:mo + mw], ALU.add)
                    return self.inc(i, sdve)
                kmsk_tk[c] = self.V(mskf)
                war[("sk", bi)] = kmsk_tk[c]

                if c % 2 == 1 or c == NKC - 1:
                    cs_ = [c - 1, c] if c % 2 == 1 else [c]

                    def expf(cc, en, cs_=tuple(cs_)):
                        for cj in cs_:
                            self.wait(cc, en, kmsk_tk[cj], war.get(("pb", cj % 2)))
                        i = None
                        for cj in cs_:
                            mwj = kspan(cj)[1] - kspan(cj)[0]
                            i = cc.activation(out=probs[cj % 2][:, 0:mwj],
                                              in_=smk[cj % 3][:, 0:mwj], func=AF.Exp)
                        return self.inc(i, sact)
                    tk3 = self.C(expf)
                    for cj in cs_:
                        war[("smk", cj % 3)] = tk3
                        exp_tk[cj] = tk3

                    for cj in cs_:
                        qloj, qhij = kspan(cj)
                        pij = cj % 2
                        for gi, (glo, ghi) in enumerate(groups5):
                            a, bnd = max(qloj, glo), min(qhij, ghi)
                            if a >= bnd:
                                continue
                            gslot = gi % 2
                            clast = (ghi - 1) // 128
                            lo_g, hi_g = a - glo, bnd - glo
                            vhw = ghw[gi]
                            parts = []
                            if lo_g < vhw:
                                parts.append((lo_g, min(hi_g, vhw)))
                            if vhw < hi_g:
                                parts.append((max(lo_g, vhw), hi_g))
                            ghw[gi] = max(vhw, hi_g)

                            def cmm(tt, en, c=cj, gi=gi, gslot=gslot, glo=glo,
                                    qlo=qloj, pi=pij, clast=clast,
                                    parts=tuple(parts), first=gfirst[gi]):
                                self.wait(tt, en, exp_tk[c], tk_vh)
                                if first:
                                    self.wait(tt, en, war.get(("gx", gslot)))
                                i = None
                                for n, (plo, phi) in enumerate(parts):
                                    i = tt.matmul(gctx[gslot][0:65, plo:phi],
                                                  vh[:, 65 * c:65 * c + 65],
                                                  probs[pi][:, plo + glo - qlo:phi + glo - qlo],
                                                  start=(first and n == 0),
                                                  stop=(c == clast and n == len(parts) - 1),
                                                  skip_group_check=True)
                                gfirst[gi] = False
                                return self.inc(i, spe)
                            tk_ctx = self.T(cmm)
                            war[("pb", pij)] = tk_ctx
                            if cj == clast:
                                Tg = ghi - glo

                                def stg(v, en, gslot=gslot, glo=glo, Tg=Tg, tk_ctx=tk_ctx):
                                    self.wait(v, en, tk_ctx)
                                    i = v.tensor_copy(cstg[0:65, glo - 512:glo - 512 + Tg],
                                                      gctx[gslot][0:65, 0:Tg])
                                    return self.inc(i, sdve)
                                stg_tk[gi] = self.V(stg)
                                war[("gx", gslot)] = stg_tk[gi]

            def zmax(v, en):
                self.wait(v, en, *stg_tk.values())
                self.lk(v, en, v.tensor_scalar_max(zrh, zrh, 1e-35))
                return self.inc(v.reciprocal(zrh, zrh), sdve)
            tkz = self.V(zmax)
            tke = None
            for gi, (glo, ghi) in enumerate(groups5):
                Tg = ghi - glo

                def zb(tt, en, glo=glo, Tg=Tg):
                    self.wait(tt, en, tkz, war.get("zbp"))
                    i = tt.matmul(zbp[0:64, 0:Tg],
                                  self.cf32[64:65, CF_ONESROW:CF_ONESROW + 64],
                                  cstg[64:65, glo - 512:glo - 512 + Tg],
                                  start=True, stop=True)
                    return self.inc(i, spe)
                tkb = self.T(zb)

                def ctv(v, en, glo=glo, Tg=Tg):
                    self.wait(v, en, tkb)
                    i = v.tensor_tensor(ctev[0:64, glo - 512:glo - 512 + Tg],
                                        cstg[0:64, glo - 512:glo - 512 + Tg],
                                        zbp[0:64, 0:Tg], ALU.mult)
                    return self.inc(i, sdve)
                tke = self.V(ctv)
                war["zbp"] = tke

            def cst(y, en, hg=hg):
                self.wait(y, en, tke)
                return self.inc(y.dma_start(
                    out=self.cxdram[64 * hg:64 * hg + 64, :],
                    in_=ctev[0:64, :]), sst[0], 16)
            self.Y(cst)
            self.cbar()
            self.end_scope(mark_k)
            self.end_scope(mark_head)

    def f_mlp(self, l):
        spe, sdve, sact, spool = (self.sem(x) for x in ("pe", "dve", "act", "pool"))
        mark_il3 = self.scope()
        xh2 = [self.sbuf(f"xh2_{f}", [128, OWN], BF16) for f in range(NCH)]
        mark_a = self.scope()
        p = self.mk_ln_pools("m")
        xmb = [self.sbuf(f"mx{f}", [128, 512], BF16) for f in range(NCH)]
        t1 = [self.sbuf(f"mt{f}", [128, 512], F32) for f in range(NCH)]
        self.cbar()
        war_xh = None
        for g in range(4):
            scol = 641 + 512 * g

            def cast(v, en, scol=scol):
                self.wait(v, en, war_xh)
                i = None
                for f in range(NCH):
                    src = self.S[f][:, scol:scol + 512] if f < 4 else \
                        self.S[f][:, scol - 1:scol - 1 + 512]
                    i = v.tensor_copy(xmb[f][:], src)
                return self.inc(i, sdve)
            tk_cast = self.V(cast)
            tk_mub, tk_rb = self.ln_stats(xmb, 512, p, tk_cast)

            def app(v, en, scol=scol):
                self.wait(v, en, tk_rb)
                i = None
                for f in range(NCH):
                    srcv = self.S[f][:, scol:scol + 512] if f < 4 else \
                        self.S[f][:, scol - 1:scol - 1 + 512]
                    i = v.tensor_sub(t1[f][:], srcv, p["mub"][:, 0:512])
                self.lk(v, en, i)
                for f in range(NCH):
                    i = v.scalar_tensor_tensor(
                        out=t1[f][:], in0=t1[f][:],
                        scalar=self.vecs[:, VL * l + V_LG + f:VL * l + V_LG + f + 1],
                        in1=p["rbs"][:, 0:512], op0=ALU.mult, op1=ALU.mult)
                return self.inc(i, sdve)
            tk_ap = self.V(app)
            p["war"]["mub"] = tk_ap

            def xw(v, en, g=g):
                self.wait(v, en, tk_ap)
                i = None
                for f in range(NCH):
                    i = v.tensor_scalar_add(
                        xh2[f][:, 512 * g:512 * g + 512], t1[f][:],
                        self.vecs[:, VL * l + V_LB + f:VL * l + V_LB + f + 1])
                return self.inc(i, sdve)
            war_xh = self.V(xw)
        tk_xh = war_xh
        self.cbar()
        self.end_scope(mark_a)

        mark_b = self.scope()
        ublob = self.sbuf("ublob", [128, NHC * 512], BF16)
        w1s = [[self.sbuf(f"w1s{a}{i}", [128, 1024], BF16) for i in range(2)] for a in range(2)]
        w2s = [self.sbuf(f"w2s{i}", [128, NHC * 128], BF16) for i in range(2)]
        aps = [self.psum(f"aps{i}", [128, 512]) for i in range(2)]
        gps = [self.psum(f"gps{i}", [128, 512]) for i in range(2)]
        w2p = [self.psum(f"w2ps{i}", [128, 512]) for i in range(2)]
        gl = [self.sbuf(f"gl{i}", [128, 512], BF16) for i in range(2)]
        war = {}
        self.cbar()
        for g in range(4):
            scol = 641 + 512 * g
            for hc in range(NHC):
                bi = hc % 2

                def ldw(gg, en, hc=hc, bi=bi):
                    self.wait(gg, en, war.get(("w1", bi)))
                    swt_ = self.sem(f"w1ld{bi}")
                    self.inc(gg.dma_start(
                        out=w1s[0][bi][:].rearrange("p (c n) -> p c n", c=8),
                        in_=self.d_w1a[l].rearrange("(c p) n -> p c n", p=128)[:, :, 128 * hc:128 * hc + 128]),
                        swt_, 16)
                    return self.inc(gg.dma_start(
                        out=w1s[1][bi][:].rearrange("p (c n) -> p c n", c=8),
                        in_=self.d_w1g[l].rearrange("(c p) n -> p c n", p=128)[:, :, 128 * hc:128 * hc + 128]),
                        swt_, 16)
                tk_w = self.Y(ldw)

                def amm(tt, en, g=g, bi=bi):
                    self.wait(tt, en, tk_w, tk_xh, war.get(("aps", bi)), war.get(("gps", bi)))
                    for f in range(NCH):
                        tt.matmul(aps[bi][:], w1s[0][bi][:, 128 * f:128 * f + 128],
                                  xh2[f][:, 512 * g:512 * g + 512],
                                  start=(f == 0), stop=(f == NCH - 1))
                    i = None
                    for f in range(NCH):
                        i = tt.matmul(gps[bi][:], w1s[1][bi][:, 128 * f:128 * f + 128],
                                      xh2[f][:, 512 * g:512 * g + 512],
                                      start=(f == 0), stop=(f == NCH - 1))
                    return self.inc(i, spe)
                tk_mm = self.T(amm)
                war[("w1", bi)] = tk_mm

                def gel(cc, en, hc=hc, bi=bi):
                    self.wait(cc, en, tk_mm, war.get(("gl", bi)))
                    i = cc.activation(out=gl[bi][:], in_=gps[bi][:], func=AF.Gelu,
                                      bias=self.vecs[:, VL * l + V_B1G + hc:VL * l + V_B1G + hc + 1])
                    return self.inc(i, sact)
                tk_g = self.C(gel)
                war[("gps", bi)] = tk_g

                def uf(v, en, hc=hc, bi=bi):
                    self.wait(v, en, tk_g, war.get(("ub", hc)))
                    i = v.scalar_tensor_tensor(
                        out=ublob[:, 512 * hc:512 * hc + 512], in0=aps[bi][:],
                        scalar=self.vecs[:, VL * l + V_B1A + hc:VL * l + V_B1A + hc + 1],
                        in1=gl[bi][:], op0=ALU.add, op1=ALU.mult)
                    return self.inc(i, sdve)
                tk_u = self.V(uf)
                war[("aps", bi)] = tk_u
                war[("gl", bi)] = tk_u
                war[("u", hc)] = tk_u
            tk_ulast = tk_u
            for f in range(NCH):
                bi = f % 2

                def ldw2(gg, en, f=f, bi=bi):
                    self.wait(gg, en, war.get(("w2", bi)))
                    swt_ = self.sem(f"w2ld{bi}")
                    return self.inc(gg.dma_start(
                        out=w2s[bi][:].rearrange("p (c n) -> p c n", c=NHC),
                        in_=self.d_w2p[l].rearrange("(c p) n -> p c n", p=128)[:, :, 128 * f:128 * f + 128]),
                        swt_, 16)
                tk_w2 = self.Y(ldw2)

                def wmm(tt, en, f=f, bi=bi):
                    self.wait(tt, en, tk_w2, tk_ulast, war.get(("w2p", bi)))
                    i = None
                    for hc in range(NHC):
                        i = tt.matmul(w2p[bi][:], w2s[bi][:, 128 * hc:128 * hc + 128],
                                      ublob[:, 512 * hc:512 * hc + 512],
                                      start=(hc == 0), stop=(hc == NHC - 1))
                    return self.inc(i, spe)
                tk_m = self.T(wmm)
                war[("w2", bi)] = tk_m

                def ev(v, en, f=f, bi=bi, scol=scol):
                    self.wait(v, en, tk_m)
                    i = v.scalar_tensor_tensor(
                        out=self.S[f][:, scol:scol + 512], in0=w2p[bi][:],
                        scalar=self.vecs[:, VL * l + V_B2 + f:VL * l + V_B2 + f + 1],
                        in1=self.S[f][:, scol:scol + 512], op0=ALU.add, op1=ALU.add)
                    return self.inc(i, sdve)
                war[("w2p", bi)] = self.V(ev)
            for hc in range(NHC):
                war[("ub", hc)] = war[("w2", (NCH - 1) % 2)]
        self.cbar()
        self.end_scope(mark_b)
        self.end_scope(mark_il3)

    def ph_x1y2(self):
        mark = self.scope()
        sld, sdve, sact = (self.sem(x) for x in ("xld", "dve", "act"))
        xt = [self.sbuf(f"xx{f}", [128, 512], F32) for f in range(NCH)]
        x1t = [self.sbuf(f"x1t{f}", [128, 512], F32) for f in range(NCH)]
        xb = [self.sbuf(f"xxb{f}", [128, 512], BF16) for f in range(NCH)]
        y2b = [self.sbuf(f"y2b{f}", [128, 512], BF16) for f in range(NCH)]
        t1 = [self.sbuf(f"xt1{f}", [128, 512], F32) for f in range(NCH)]
        p = self.mk_ln_pools("x1")
        self.full_barrier()
        war = {}
        for g in range(4):
            scol = 641 + 512 * g

            def ld(y, en, g=g, scol=scol):
                self.wait(y, en, war.get("xt"))
                i = None
                for f in range(NCH):
                    i = self.inc(y.dma_start(out=xt[f][:],
                                             in_=self.d_x0w[128 * f:128 * f + 128, scol:scol + 512]),
                                 sld, 16)
                return i
            tk_ld = self.Y(ld)

            def x1f(v, en, scol=scol):
                self.wait(v, en, tk_ld, war.get("x1t"), war.get("x1st"))
                i = None
                for f in range(NCH):
                    i = v.scalar_tensor_tensor(out=x1t[f][:], in0=self.S[f][:, scol:scol + 512],
                                               scalar=0.5, in1=xt[f][:],
                                               op0=ALU.mult, op1=ALU.add)
                self.lk(v, en, i)
                for f in range(NCH):
                    i = v.tensor_copy(xb[f][:], x1t[f][:])
                return self.inc(i, sdve)
            tk_x1 = self.V(x1f)
            war["xt"] = tk_x1

            def st1(y, en, g=g):
                self.wait(y, en, tk_x1)
                i = None
                for f in range(NCH):
                    i = self.inc(y.dma_start(out=self.d_x1[128 * f:128 * f + 128,
                                                          512 * g:512 * g + 512],
                                             in_=x1t[f][:]), self.sem("x1st"), 16)
                return i
            war["x1st"] = self.Y(st1)
            tk_mub, tk_rb = self.ln_stats(xb, 512, p, tk_x1)

            def app(v, en):
                self.wait(v, en, tk_rb, war.get("y2b"))
                i = None
                for f in range(NCH):
                    i = v.tensor_sub(t1[f][:], x1t[f][:], p["mub"][:, 0:512])
                self.lk(v, en, i)
                for f in range(NCH):
                    i = v.tensor_mul(y2b[f][:], t1[f][:], p["rbs"][:, 0:512])
                return self.inc(i, sdve)
            tk_y2 = self.V(app)
            war["x1t"] = tk_y2
            p["war"]["mub"] = tk_y2

            def st2(y, en, g=g):
                self.wait(y, en, tk_y2)
                i = None
                for f in range(NCH):
                    i = self.inc(y.dma_start(out=self.e2a_in[128 * f:128 * f + 128,
                                                             512 * g:512 * g + 512],
                                             in_=y2b[f][:]), self.sem("e2ast"), 16)
                return i
            war["y2b"] = self.Y(st2)

        def coll(gg, en):
            self.barrier_from(gg, en, "e2ast")
            return self.inc(gg.collective_compute(
                "AllGather", ALU.bypass, replica_groups=self.rg,
                ins=[self.e2a_in.opt()], outs=[self.e2a_out.opt()]), self.sem("cc"), 1)
        self.tk_e2a = self.G(coll)
        self.full_barrier()
        self.end_scope(mark)

    def ph_A(self):
        spe, sdve, sact, swt, sld, spool = (self.sem(x) for x in
                                            ("pe", "dve", "act", "awld", "ald", "pool"))
        sst = [self.sem("e2bst0"), self.sem("e2bst1")]
        mark = self.scope()
        y2sb = [self.sbuf(f"ya{f}", [128, SEQ], BF16) for f in range(NCH)]
        mvb = self.sbuf("mv_sb", [128, SEQ], BF16)
        bvr = self.sbuf("bvr", [1, 512], BF16)
        vblA = self.sbuf("vblA", [128, 32 * 520], BF16)
        bkV = self.sbuf("bkV", [128, 8 * 65], BF16)
        ctxA = [self.sbuf(f"cA{c}", [128, SEQ], BF16) for c in range(4)]

        def ldy(y, en):
            self.wait(y, en, self.tk_e2a)
            i = None
            for f in range(NCH):
                i = self.inc(y.dma_start(out=y2sb[f][:, 0:OWN],
                                         in_=self.e2a_out[128 * f:128 * f + 128, :]), sld, 16)
                i = self.inc(y.dma_start(out=y2sb[f][:, OWN:SEQ],
                                         in_=self.e2a_out[D + 128 * f:D + 128 * f + 128, :]),
                             sld, 16)
            i = self.inc(y.dma_start(out=mvb[:], in_=self.d_maskv), sld, 16)
            return i
        tk_y2 = self.Y(ldy)

        def ldbv(g, en):
            return self.inc(g.dma_start(out=bvr[:], in_=self.d_bv512), self.sem("awld"), 16)
        tk_bv = self.Y(ldbv)

        mark_v = self.scope()
        bki = self.sbuf("bki", [128, 32 * 128], BF16)
        wvsb = [self.sbuf(f"wva{f}", [128, 512], BF16) for f in range(NCH)]

        def ldbki(y, en):
            return self.inc(y.dma_start(out=bki[:], in_=self.d_blkind), self.sem("bkld"), 16)
        tk_bki = self.Y(ldbki)

        def ldwv(g, en):
            i = None
            for f in range(NCH):
                i = self.inc(g.dma_start(out=wvsb[f][:],
                                         in_=self.d_wvT[128 * f:128 * f + 128, :]), swt, 16)
            return i
        tk_wv = self.Y(ldwv)

        def vinit(v, en):
            return self.inc(v.memset(vblA[:], 1.0), sdve)
        tk_vi = self.V(vinit)
        vps = [self.psum(f"avp{i}", [128, 512]) for i in range(2)]
        warv = [None, None]
        for tc in range(32):
            bi = tc % 2

            def vmm(t, en, tc=tc, bi=bi):
                self.wait(t, en, tk_y2, tk_wv, tk_bv, warv[bi])
                for f in range(NCH):
                    t.matmul(vps[bi][:], y2sb[f][:, 128 * tc:128 * tc + 128], wvsb[f][:],
                             start=(f == 0), stop=False)
                i = t.matmul(vps[bi][:], self.cbf[0:1, CBF_ONESROW:CBF_ONESROW + 128],
                             bvr[:], start=False, stop=True)
                return self.inc(i, spe)
            tkv = self.T(vmm)

            def vev(v, en, tc=tc, bi=bi):
                self.wait(v, en, tkv, tk_vi)
                dst = vblA.rearrange("p (c h e) -> p c h e", c=32, h=FH)[:, tc, :, 0:64]
                src = vps[bi][:].rearrange("p (h e) -> p h e", h=FH)
                return self.inc(v.tensor_copy(dst, src), sdve)
            warv[bi] = self.V(vev)
        tk_vbl = warv[31 % 2]
        bkp = [self.psum(f"bkp{i}", [128, 65]) for i in range(2)]
        warb = [None, None]
        for ha in range(FH):
            bi = ha % 2

            def bmm(t, en, ha=ha, bi=bi):
                self.wait(t, en, tk_vbl, tk_bki, warb[bi])
                i = None
                for tc in range(32):
                    i = t.matmul(bkp[bi][:], bki[:, 128 * tc:128 * tc + 128],
                                 vblA[:, 520 * tc + 65 * ha:520 * tc + 65 * ha + 65],
                                 start=(tc == 0), stop=(tc == 31))
                return self.inc(i, spe)
            tkb = self.T(bmm)

            def bev(v, en, ha=ha, bi=bi):
                self.wait(v, en, tkb)
                return self.inc(v.tensor_copy(bkV[:, 65 * ha:65 * ha + 65],
                                              bkp[bi][:]), sdve)
            warb[bi] = self.V(bev)
        self.cbar()
        self.end_scope(mark_v)

        awar = {}
        for hp in range(4):
            mark_hp = self.scope()
            kaugA = [self.sbuf(f"kA{i}", [81, SEQ], BF16) for i in range(2)]
            kmx2 = self.sbuf("kmx2", [33, 1], F32)
            kred = self.sbuf("kred", [33, 1], F32)
            kmax = self.sbuf("kmax", [33, 1], F32)
            wksb = self.sbuf("wksb", [128, 1024], BF16)
            wqsb = self.sbuf("wqsb", [128, 1024], BF16)
            qaugA = [self.sbuf(f"qA{i}", [81, 512], BF16) for i in range(2)]
            qnorm = self.sbuf("qnorm", [33, 512], F32)
            ksq = [self.sbuf(f"ksq{i}", [128, 512], BF16) for i in range(2)]
            smv = self.sbuf("smv", [128, 512], F32)
            vpb = self.sbuf("vpb", [128, 512], BF16)
            probsA = [self.sbuf(f"pA{i}", [128, 512], BF16) for i in range(3)]
            z2 = self.sbuf("az2", [1, 512], F32)
            zr = self.sbuf("azr", [1, 512], F32)
            zbs = self.sbuf("azbs", [64, 512], F32)

            def ldk(g, en, hp=hp):
                self.wait(g, en, awar.get("hp"))
                i = self.inc(g.dma_start(
                    out=wksb[:].rearrange("p (c n) -> p c n", c=8),
                    in_=self.d_wkT.rearrange("(c p) n -> p c n", p=128)[:, :, 128 * hp:128 * hp + 128]),
                    swt, 16)
                i = self.inc(g.dma_start(
                    out=wqsb[:].rearrange("p (c n) -> p c n", c=8),
                    in_=self.d_wqT.rearrange("(c p) n -> p c n", p=128)[:, :, 128 * hp:128 * hp + 128]),
                    swt, 16)
                return i
            tk_wk = self.Y(ldk)

            def kinit(y, en):
                self.wait(y, en, awar.get("hp"))
                i = None
                for hl in range(2):
                    i = self.inc(y.dma_start(out=kaugA[hl][65:81, :], in_=self.d_kaugc),
                                 self.sem("akld"), 16)
                return i
            tk_kc = self.Y(kinit)

            def kini2(v, en):
                i = None
                for hl in range(2):
                    i = v.memset(kaugA[hl][64:65, :], 1.0)
                return self.inc(i, sdve)
            tk_k1 = self.V(kini2)

            def qinit(y, en):
                self.wait(y, en, awar.get("hp"))
                i = None
                for hl in range(2):
                    i = self.inc(y.dma_start(out=qaugA[hl][65:81, :], in_=self.d_qaugc),
                                 self.sem("aqld"), 16)
                return i
            tk_qc = self.Y(qinit)

            mark_kb = self.scope()
            kps = [self.psum(f"akp{i}", [128, 512]) for i in range(2)]
            nkp = [self.psum(f"ank{i}", [33, 512]) for i in range(2)]
            self.cbar()
            war = {}
            for g8 in range(8):
                bi = g8 % 2

                def kmm(t, en, g8=g8, bi=bi):
                    self.wait(t, en, tk_y2, tk_wk, war.get(("kps", bi)))
                    i = None
                    for f in range(NCH):
                        i = t.matmul(kps[bi][:], wksb[:, 128 * f:128 * f + 128],
                                     y2sb[f][:, 512 * g8:512 * g8 + 512],
                                     start=(f == 0), stop=(f == NCH - 1))
                    return self.inc(i, spe)
                tk1 = self.T(kmm)

                def kev(v, en, g8=g8, bi=bi, hp=hp):
                    self.wait(v, en, tk1, tk_k1, war.get(("ksq", bi)))
                    i = None
                    for hl in range(2):
                        i = v.tensor_scalar_add(
                            kaugA[hl][0:64, 512 * g8:512 * g8 + 512],
                            kps[bi][64 * hl:64 * hl + 64, :],
                            self.vecs[64 * hl:64 * hl + 64, V_BK + hp:V_BK + hp + 1])
                    return self.inc(i, sdve)
                tk2a = self.V(kev)

                def ksqf(g_, en, g8=g8, bi=bi):
                    self.wait(g_, en, tk2a, war.get(("ksq", bi)))
                    i = None
                    for hl in range(2):
                        i = g_.tensor_mul(ksq[bi][64 * hl:64 * hl + 64, :],
                                          kaugA[hl][0:64, 512 * g8:512 * g8 + 512],
                                          kaugA[hl][0:64, 512 * g8:512 * g8 + 512])
                    return self.inc(i, spool)
                tk2 = self.G(ksqf)

                def nmm(t, en, bi=bi):
                    self.wait(t, en, tk2, war.get(("nkp", bi)))
                    return self.inc(t.matmul(nkp[bi][0:33, :],
                                             self.cbf[:, CBF_HS33:CBF_HS33 + 33],
                                             ksq[bi][:], start=True, stop=True), spe)
                tk3 = self.T(nmm)
                war[("kps", bi)] = tk2
                war[("ksq", bi)] = tk3

                def nred(v, en, g8=g8, bi=bi):
                    self.wait(v, en, tk3, war.get("kred"))
                    self.lk(v, en, v.reduce_max(kred[:], nkp[bi][0:33, :],
                                                axis=mybir.AxisListType.X))
                    if g8 == 0:
                        i = v.tensor_copy(kmx2[:], kred[:])
                    else:
                        i = v.tensor_max(kmx2[:], kmx2[:], kred[:])
                    return self.inc(i, sdve)
                war[("nkp", bi)] = self.V(nred)
                war["kred"] = war[("nkp", bi)]

            def kmx(c, en):
                self.wait(c, en, war["kred"])
                return self.inc(c.activation(out=kmax[:], in_=kmx2[:],
                                             func=AF.Sqrt, scale=1.0), sact)
            tk_sq1 = self.C(kmx)

            def kneg(v, en):
                self.wait(v, en, tk_sq1)
                return self.inc(v.tensor_scalar_mul(kmax[:], kmax[:], -1.0), sdve)
            tk_kmx = self.V(kneg)
            self.cbar()
            self.end_scope(mark_kb)

            mark_qa = self.scope()
            qps = [self.psum(f"aqp{i}", [128, 512]) for i in range(2)]
            nzp = self.psum("anzp", [128, 512])
            nqp = nzp
            skp = [self.psum(f"ask{i}", [128, 512]) for i in range(3)]
            gcx = [self.psum(f"agx{i}", [65, 512]) for i in range(2)]
            zbp = nzp
            qsq = [self.sbuf(f"qsq{i}", [128, 512], BF16) for i in range(2)]
            self.cbar()
            war = {}
            for g in range(8):
                bi = g % 2

                def qmm(t, en, g=g, bi=bi):
                    self.wait(t, en, tk_y2, tk_wk, war.get(("qps", bi)))
                    i = None
                    for f in range(NCH):
                        i = t.matmul(qps[bi][:], wqsb[:, 128 * f:128 * f + 128],
                                     y2sb[f][:, 512 * g:512 * g + 512],
                                     start=(f == 0), stop=(f == NCH - 1))
                    return self.inc(i, spe)
                tk1 = self.T(qmm)

                def qev(v, en, g=g, bi=bi, hp=hp):
                    self.wait(v, en, tk1, tk_qc, war.get("qaug"), war.get("qsqf"))
                    i = None
                    for hl in range(2):
                        i = v.tensor_scalar(
                            out=qaugA[hl][0:64, :],
                            in0=qps[bi][64 * hl:64 * hl + 64, :],
                            scalar1=0.125, op0=ALU.mult,
                            scalar2=self.bq8[64 * hl:64 * hl + 64, hp:hp + 1],
                            op1=ALU.add)
                    return self.inc(i, sdve)
                tk2b = self.V(qev)

                def qsqf(g_, en, bi=bi):
                    self.wait(g_, en, tk2b, war.get(("qsq", bi)))
                    i = None
                    for hl in range(2):
                        i = g_.tensor_mul(qsq[bi][64 * hl:64 * hl + 64, :],
                                          qaugA[hl][0:64, :], qaugA[hl][0:64, :])
                    return self.inc(i, spool)
                tk2 = self.G(qsqf)
                war["qsqf"] = tk2

                def nmm(t, en, bi=bi):
                    self.wait(t, en, tk2, war.get("nqp"))
                    return self.inc(t.matmul(nqp[0:33, :], self.cbf[:, CBF_HS33:CBF_HS33 + 33],
                                             qsq[bi][:], start=True, stop=True), spe)
                tk3 = self.T(nmm)
                war[("qps", bi)] = tk2
                war[("qsq", bi)] = tk3

                def sqf(c, en):
                    self.wait(c, en, tk3)
                    return self.inc(c.activation(out=qnorm[:], in_=nqp[0:33, :],
                                                 func=AF.Sqrt), sact)
                tk4 = self.C(sqf)
                war["nqp"] = tk4

                def mrow(v, en, bi=bi):
                    self.wait(v, en, tk4, tk_kmx)
                    i = None
                    for hl in range(2):
                        i = v.tensor_scalar_mul(qaugA[hl][64:65, :],
                                                qnorm[32 * hl:32 * hl + 1, :],
                                                kmax[32 * hl:32 * hl + 1, 0:1])
                    return self.inc(i, sdve)
                tk_mr = self.V(mrow)

                last_aug_reader = []
                for hl in range(2):
                    ha = 2 * hp + hl
                    gfirst = True
                    tk_ctx = None
                    smm_tk = {}
                    exp_tkA = {}
                    for ck in range(4 * g + 4):
                        sbi = ck % 3
                        diag = (ck >= 4 * g)
                        rows = 81 if diag else 65

                        def smm(t, en, ck=ck, sbi=sbi, hl=hl, rows=rows):
                            self.wait(t, en, tk_mr, tk_k1, war.get(("skp", sbi)))
                            return self.inc(t.matmul(
                                skp[sbi][:], kaugA[hl][0:rows, 128 * ck:128 * ck + 128],
                                qaugA[hl][0:rows, :], start=True, stop=True), spe)
                        smm_tk[ck] = self.T(smm)

                        if ck % 2 == 0:
                            continue
                        cks = (ck - 1, ck)

                        def pexp(c, en, cks=cks):
                            for cj in cks:
                                self.wait(c, en, smm_tk[cj], war.get(("pA", cj % 3)))
                            i = None
                            for cj in cks:
                                i = c.activation(out=probsA[cj % 3][:],
                                                 in_=skp[cj % 3][:], func=AF.Exp)
                            return self.inc(i, sact)
                        tk_p = self.C(pexp)
                        for cj in cks:
                            war[("skp", cj % 3)] = tk_p
                            exp_tkA[cj] = tk_p

                        for cj in cks:
                            def cmm(t, en, ck=cj, pi=cj % 3, ha=ha,
                                    gfirst=gfirst, hl=hl):
                                self.wait(t, en, exp_tkA[ck], tk_vbl)
                                if gfirst:
                                    self.wait(t, en, war.get(("gcx", hl)))
                                return self.inc(t.matmul(
                                    gcx[hl][0:65, :],
                                    vblA[:, 520 * ck + 65 * ha:520 * ck + 65 * ha + 65],
                                    probsA[pi][:], start=gfirst, stop=False,
                                    skip_group_check=True), spe)
                            tk_ctx = self.T(cmm)
                            war[("pA", cj % 3)] = tk_ctx
                            gfirst = False
                    vsl = (4 * g + 4) % 3

                    def vmm(t, en, g=g, hl=hl, vsl=vsl):
                        self.wait(t, en, tk_mr, war.get(("skp", vsl)))
                        return self.inc(t.matmul(
                            skp[vsl][:], self.cbf[64:65, CBF_ONES64:CBF_ONES64 + 128],
                            qaugA[hl][64:65, :], start=True, stop=True), spe)
                    tk_v1 = self.T(vmm)

                    def vmsk(v, en, g=g, vsl=vsl):
                        self.wait(v, en, tk_v1, war.get("smv"))
                        return self.inc(v.tensor_tensor(
                            smv[:], skp[vsl][:],
                            mvb[:, 512 * g:512 * g + 512], ALU.add), sdve)
                    tk_v2 = self.V(vmsk)
                    war[("skp", vsl)] = tk_v2

                    def vexp(c, en):
                        self.wait(c, en, tk_v2, war.get("vpb"))
                        return self.inc(c.activation(out=vpb[:], in_=smv[:], func=AF.Exp), sact)
                    tk_v3 = self.C(vexp)
                    war["smv"] = tk_v3

                    def vcm(t, en, ha=ha, hl=hl):
                        self.wait(t, en, tk_v3)
                        return self.inc(t.matmul(
                            gcx[hl][0:65, :], bkV[:, 65 * ha:65 * ha + 65], vpb[:],
                            start=False, stop=True, skip_group_check=True), spe)
                    tk_v4 = self.T(vcm)
                    war["vpb"] = tk_v4

                    def zf(v, en, hl=hl):
                        self.wait(v, en, tk_v4, war.get("zch"))
                        self.lk(v, en, v.tensor_scalar_max(z2[:], gcx[hl][64:65, :], 1e-35))
                        return self.inc(v.reciprocal(zr[:], z2[:]), sdve)
                    tkz = self.V(zf)

                    def zb(t, en):
                        self.wait(t, en, tkz, war.get("zbp"))
                        return self.inc(t.matmul(zbp[0:64, :],
                                                 self.cf32[0:1, CF_ONESROW:CF_ONESROW + 64],
                                                 zr[0:1, :], start=True, stop=True), spe)
                    tkb2 = self.T(zb)
                    war["zch"] = tkb2

                    def zc(v, en):
                        self.wait(v, en, tkb2, war.get("zbs"))
                        return self.inc(v.tensor_copy(zbs[:], zbp[0:64, :]), sdve)
                    tkc2 = self.V(zc)
                    war["zbp"] = tkc2
                    war["nqp"] = tkc2

                    def cev(v, en, ha=ha, g=g, hl=hl):
                        self.wait(v, en, tkc2)
                        i = v.tensor_tensor(
                            ctxA[ha // 2][(ha % 2) * 64:(ha % 2) * 64 + 64,
                                          512 * g:512 * g + 512],
                            gcx[hl][0:64, :], zbs[:], ALU.mult)
                        return self.inc(i, sdve)
                    tke = self.V(cev)
                    war[("gcx", hl)] = tke
                    war["zbs"] = tke
                    last_aug_reader.append(tk_v4)
                war["qaug"] = last_aug_reader[-1]
            awar["hp"] = war["qaug"]
            self.cbar()
            self.end_scope(mark_qa)
            self.end_scope(mark_hp)

        mark_wo = self.scope()
        wotsb = [self.sbuf(f"wot{c}", [128, 1024], BF16) for c in range(4)]
        pot = [self.sbuf(f"pot{i}", [128, 512], F32) for i in range(2)]
        wop = [self.psum(f"awo{i}", [128, 512]) for i in range(2)]

        def ldwo(g, en):
            i = None
            for c in range(4):
                i = self.inc(g.dma_start(out=wotsb[c][:],
                                         in_=self.d_woT[128 * c:128 * c + 128, :]), swt, 16)
            return i
        tk_wo = self.Y(ldwo)
        self.cbar()
        war = {}
        self.tk_e2b = [None] * 4
        for gn, g in enumerate([0, 4, 1, 5, 2, 6, 3, 7]):
            for f in range(NCH):
                bi = (gn * NCH + f) % 2

                def womm(t, en, g=g, f=f, bi=bi):
                    self.wait(t, en, tk_wo, war.get(("wop", bi)))
                    self.barrier_from(t, en, "dve")
                    i = None
                    for kc in range(4):
                        i = t.matmul(wop[bi][:], wotsb[kc][:, 128 * f:128 * f + 128],
                                     ctxA[kc][:, 512 * g:512 * g + 512],
                                     start=(kc == 0), stop=(kc == 3))
                    return self.inc(i, spe)
                tkw = self.T(womm)

                def wev(v, en, bi=bi):
                    self.wait(v, en, tkw, war.get(("pot", bi)))
                    return self.inc(v.tensor_copy(pot[bi][:], wop[bi][:]), sdve)
                tkv = self.V(wev)
                war[("wop", bi)] = tkv

                def wst(y, en, g=g, f=f, bi=bi):
                    self.wait(y, en, tkv)
                    return self.inc(y.dma_start(
                        out=self.e2bi[g % 4][128 * f:128 * f + 128,
                                            (g // 4) * 512:(g // 4) * 512 + 512],
                        in_=pot[bi][:]), sst[bi], 16)
                war[("pot", bi)] = self.Y(wst)
            if g >= 4:
                def coll(gg, en, j=g % 4):
                    self.barrier_from(gg, en, "e2bst0", "e2bst1")
                    return self.inc(gg.collective_compute(
                        "AllReduce", ALU.add, replica_groups=self.rg,
                        ins=[self.e2bi[j].opt()], outs=[self.e2bo[j].opt()]),
                        self.sem("cc"), 1)
                self.tk_e2b[g % 4] = self.G(coll)
        self.full_barrier()
        self.end_scope(mark)

    def ph_x2y3(self):
        mark = self.scope()
        sld, sdve, sst = (self.sem(x) for x in ("x2ld", "dve", "x2st"))
        x1t = [self.sbuf(f"zx{f}", [128, 512], F32) for f in range(NCH)]
        sa = [self.sbuf(f"za{f}", [128, 512], F32) for f in range(NCH)]
        sb = [self.sbuf(f"zb{f}", [128, 512], F32) for f in range(NCH)]
        x2t = [self.sbuf(f"zc{f}", [128, 512], F32) for f in range(NCH)]
        xb = [self.sbuf(f"zd{f}", [128, 512], BF16) for f in range(NCH)]
        t1 = x1t
        p = self.mk_ln_pools("x2")
        war = {}
        for g in range(4):
            scol = 641 + 512 * g

            def ld(y, en, g=g):
                self.wait(y, en, self.tk_e2b[g], war.get("in"), war.get("x2t"))
                i = None
                for f in range(NCH):
                    r = slice(128 * f, 128 * f + 128)
                    i = self.inc(y.dma_start(out=x1t[f][:],
                                             in_=self.d_x1[r, 512 * g:512 * g + 512]), sld, 16)
                    i = self.inc(y.dma_start(out=sa[f][:],
                                             in_=self.e2bo[g][r, 0:512]), sld, 16)
                    i = self.inc(y.dma_start(out=sb[f][:],
                                             in_=self.e2bo[g][r, 512:1024]), sld, 16)
                return i
            tk_ld = self.Y(ld)

            def x2f(v, en, g=g):
                self.wait(v, en, tk_ld, war.get("x2t"), war.get("x2st"))
                i = None
                for f in range(NCH):
                    i = v.scalar_tensor_tensor(out=x2t[f][:], in0=sb[f][:],
                                               scalar=self.vecs[:, V_HV:V_HV + 1], in1=x1t[f][:],
                                               op0=ALU.mult, op1=ALU.add)
                self.lk(v, en, i)
                for f in range(NCH):
                    i = v.scalar_tensor_tensor(out=x2t[f][:], in0=sa[f][:],
                                               scalar=self.hvi[:, 0:1], in1=x2t[f][:],
                                               op0=ALU.mult, op1=ALU.add)
                self.lk(v, en, i)
                for f in range(NCH):
                    i = v.tensor_scalar_add(x2t[f][:], x2t[f][:],
                                            self.vecs[:, V_BO + f:V_BO + f + 1])
                self.lk(v, en, i)
                for f in range(NCH):
                    i = v.tensor_copy(xb[f][:], x2t[f][:])
                return self.inc(i, sdve)
            tk_x2 = self.V(x2f)
            war["in"] = tk_x2

            def st1(y, en, g=g):
                self.wait(y, en, tk_x2)
                i = None
                for f in range(NCH):
                    i = self.inc(y.dma_start(out=self.d_x2[128 * f:128 * f + 128,
                                                           512 * g:512 * g + 512],
                                             in_=x2t[f][:]), sst, 16)
                return i
            war["x2st"] = self.Y(st1)
            tk_mub, tk_rb = self.ln_stats(xb, 512, p, tk_x2)

            def app(v, en, scol=scol):
                self.wait(v, en, tk_rb)
                i = None
                for f in range(NCH):
                    i = v.tensor_sub(t1[f][:], x2t[f][:], p["mub"][:, 0:512])
                self.lk(v, en, i)
                for f in range(NCH):
                    i = v.tensor_mul(self.S[f][:, scol:scol + 512], t1[f][:], p["rbs"][:, 0:512])
                return self.inc(i, sdve)
            tk_y3 = self.V(app)
            war["x2t"] = tk_y3
            p["war"]["mub"] = tk_y3
        self.full_barrier()
        self.end_scope(mark)
        self.exchange_halo(self.e3_in, self.e3_out)

    def ph_out(self):
        mark = self.scope()
        sld, sdve, sst = (self.sem(x) for x in ("x2ld", "dve", "yst"))
        x2t = [self.sbuf(f"ox{f}", [128, 512], F32) for f in range(NCH)]
        ot = [self.sbuf(f"oo{f}", [128, 512], F32) for f in range(NCH)]
        self.full_barrier()
        war = {}
        for g in range(4):
            scol = 641 + 512 * g

            def ld(y, en, g=g):
                self.wait(y, en, war.get("x2t"))
                i = None
                for f in range(NCH):
                    i = self.inc(y.dma_start(out=x2t[f][:],
                                             in_=self.d_x2[128 * f:128 * f + 128,
                                                           512 * g:512 * g + 512]), sld, 16)
                return i
            tk_ld = self.Y(ld)

            def of(v, en, scol=scol):
                self.wait(v, en, tk_ld, war.get("ot"))
                i = None
                for f in range(NCH):
                    i = v.scalar_tensor_tensor(out=ot[f][:], in0=self.S[f][:, scol:scol + 512],
                                               scalar=0.5, in1=x2t[f][:],
                                               op0=ALU.mult, op1=ALU.add)
                return self.inc(i, sdve)
            tk_o = self.V(of)
            war["x2t"] = tk_o

            def st(y, en, g=g):
                self.wait(y, en, tk_o)
                i = None
                for f in range(NCH):
                    i = self.inc(y.dma_start(out=self.d_y[128 * f:128 * f + 128,
                                                          512 * g:512 * g + 512],
                                             in_=ot[f][:]), sst, 16)
                return i
            war["ot"] = self.Y(st)
        self.end_scope(mark)

    def ph_dump(self):
        sst = self.sem("dbgst")
        self.full_barrier()

        def dp(y, en):
            i = None
            for f in range(NCH):
                i = self.inc(y.dma_start(out=self.d_dbg[f], in_=self.S[f][:]), sst, 16)
            return i
        self.Y(dp)

    def ph_fin(self):
        self.full_barrier()



def build_program():
    return MK().build()


def kernel(**inputs):
    nc = build_program()
    in_maps = [core_inputs(inputs, c) for c in range(8)]
    res = run_bass_kernel_spmd(nc, in_maps, core_ids=list(range(8)))
    out = np.zeros((B, SEQ, D), NF32)
    for c in range(8):
        b, par = c // 2, c % 2
        y = np.asarray(res.results[c]["y"], NF32)
        out[b, par * OWN:par * OWN + OWN, :] = y.T
    if DUMP:
        kernel.dumps = [{f: np.asarray(res.results[c][f"dbg{f}"], NF32) for f in range(NCH)}
                        for c in range(8)]
        kernel.vdram = [np.asarray(res.results[c]["vdram"]) for c in range(8)]
        kernel.cxdram = [np.asarray(res.results[c]["cxdram"]) for c in range(8)]
        kernel.raw = res
    return out


if __name__ == "__main__":
    mkk = MK()
    nc = mkk.build()
    n = sum(len(b.instructions) for fn in nc.m.functions for b in fn.blocks)
    print(f"built ok: stage={STAGE} instructions={n}")


# version 12
# speedup vs baseline: 1.0662x; 1.0251x over previous
import os
import numpy as np
import ml_dtypes
import concourse.bass as bass
import concourse.mybir as mybir
from concourse.bass_utils import run_bass_kernel_spmd

F32 = mybir.dt.float32
BF16 = mybir.dt.bfloat16
AF = mybir.ActivationFunctionType
ALU = mybir.AluOpType

NF32 = np.float32
NBF = ml_dtypes.bfloat16

B, SEQ, D = 4, 4096, 1024
HALO, OWN = 640, 2048
W = HALO + OWN
PW = 1 + W
NCH = 8
FH, DH, WIN = 8, 64, 512
HID, HIDP = 2730, 2816
NHC = HIDP // 128
BLK = 32
NEG = np.float32(-1e30)
QT0 = 512
NQT = 17
NKC = 21
NTI = 6
STAGE = int(os.environ.get("MK_STAGE", "99"))
DUMP = os.environ.get("MK_DUMP", "0") == "1"
STOP = os.environ.get("MK_STOP", "")
NSIDE = int(os.environ.get("MK_NSIDE", "2"))
NTILE = int(os.environ.get("MK_NTILE", str(NTI)))


def _bfarr(x):
    return np.ascontiguousarray(np.asarray(x, NF32).astype(NBF))


def rotary_tables(par):
    t = (np.arange(W, dtype=np.float64) + (par * OWN - HALO))
    half = np.arange(0, DH, 2, dtype=np.float64)
    inv = 1.0 / 10000.0 ** (half / DH)
    fr = np.concatenate([t[None, :] * inv[:, None]] * 2, 0)
    base = (half + 0.4 * DH) / (1.4 * DH)
    power = (t - SEQ // 2) / float(WIN // 2)
    sc = np.concatenate([base[:, None] ** power[None, :]] * 2, 0)
    cos, sin = np.cos(fr), np.sin(fr)
    csq, ssq = np.asarray(cos * sc, NF32), np.asarray(sin * sc, NF32)
    csk, ssk = np.asarray(cos / sc, NF32), np.asarray(sin / sc, NF32)
    pk = lambda a: np.concatenate([a, a], 0)
    return tuple(_bfarr(pk(a)) for a in (csq, ssq, csk, ssk))


def _allowed(par, qa, ka):
    qg = par * OWN - HALO + qa
    kg = par * OWN - HALO + ka
    return (qg - kg >= 0) & (qg - kg <= WIN) & (kg >= 0)


def maskq_blob(par):
    out = np.zeros((128, 6 * 640), NF32)
    for s in range(6):
        t = s if s < 5 else 8
        q0 = QT0 + 128 * t
        qa = q0 + np.arange(128)[:, None]
        ka = (q0 - 512) + np.arange(640)[None, :]
        out[:, 640 * s:640 * (s + 1)] = np.where(_allowed(par, qa, ka), 0.0, NEG)
    return _bfarr(out)


def kspan(c):
    return max(512, 128 * c), min(W, 128 * c + 640)


def maskk_blob(par):
    pieces, offs, pos = [], {}, 0
    for c in range(5):
        lo, hi = kspan(c)
        ka = 128 * c + np.arange(128)[:, None]
        qa = np.arange(lo, hi)[None, :]
        pieces.append(np.where(_allowed(par, qa, ka), 0.0, NEG))
        offs[c] = (pos, hi - lo)
        pos += hi - lo
    ka = np.arange(128)[:, None]
    qa = np.arange(640)[None, :]
    pieces.append(np.where((qa - ka >= 0) & (qa - ka <= WIN), 0.0, NEG))
    offs["gen"] = (pos, 640)
    pos += 640
    return _bfarr(np.concatenate(pieces, 1)), offs


MK_OFFS = maskk_blob(0)[1]


def maskk_slice(c):
    if c < 5:
        return MK_OFFS[c]
    lo, hi = kspan(c)
    g0, _ = MK_OFFS["gen"]
    return g0 + (lo - 128 * c), hi - lo


def maskv_blob():
    gv = np.arange(128)[:, None]
    i = np.arange(SEQ)[None, :]
    return _bfarr(np.where(gv > (i // BLK), 0.0, NEG))


def a_kaug_rows():
    j = np.arange(SEQ)[None, :]
    g = np.arange(16)[:, None]
    return _bfarr(np.where(((j % 512) // BLK) == g, NEG, 0.0))


def a_qaug_rows():
    i = np.arange(512)[None, :]
    g = np.arange(16)[:, None]
    return _bfarr((g > (i // BLK)).astype(NF32))


def blkind_blob():
    out = np.zeros((128, 32 * 128), NF32)
    t = np.arange(128)[:, None]
    g = np.arange(128)[None, :]
    for tc in range(32):
        out[:, 128 * tc:128 * (tc + 1)] = (g == (4 * tc + t // 32)).astype(NF32)
    return _bfarr(out)


CBF_R2 = 0
CBF_ID = 128
CBF_HS33 = 256
CBF_SEL33 = 289
CBF_ONES1 = 417
CBF_ONESROW = 418
CBF_ONES64 = 546
CBF_W = 674


def const_bf():
    R2 = np.zeros((128, 128), NF32)
    for hh in range(2):
        for d in range(32):
            R2[64 * hh + d, 64 * hh + d + 32] = -1.0
            R2[64 * hh + d + 32, 64 * hh + d] = 1.0
    R2T = R2.T.copy()
    ident = np.eye(128, dtype=NF32)
    hs33 = np.zeros((128, 33), NF32)
    hs33[0:64, 0] = 1.0
    hs33[64:128, 32] = 1.0
    sel33 = np.zeros((128, 128), NF32)
    sel33[0, 0:64] = 1.0
    sel33[32, 64:128] = 1.0
    ones1 = np.ones((128, 1), NF32)
    onesrow = np.zeros((128, 128), NF32)
    onesrow[0, :] = 1.0
    ones64 = np.zeros((128, 128), NF32)
    ones64[64, :] = 1.0
    return _bfarr(np.concatenate([R2T, ident, hs33, sel33, ones1, onesrow, ones64], 1))


CF_ID = 0
CF_NEG8 = 128
CF_ONESROW = 256
CF_EPS = 384
CF_W = 385


def const_f32():
    ident = np.eye(128, dtype=NF32)
    neg8 = (-8.0 * np.eye(128)).astype(NF32)
    onesrow = np.zeros((128, 128), NF32)
    onesrow[0, :] = 1.0
    onesrow[64, :] = 1.0
    eps = np.full((128, 1), 1e-5, NF32)
    return np.ascontiguousarray(np.concatenate([ident, neg8, onesrow, eps], 1))


VL = 70
V_QS, V_KS, V_LG, V_LB, V_B2, V_B1A, V_B1G = 0, 1, 2, 10, 18, 26, 48
V_BO, V_BQ, V_BK, V_BV, V_HV = 140, 148, 152, 156, 160
VW = 161


def vec_blob(inputs, par):
    cols = []

    def pp(v, width):
        return np.asarray(v, NF32).reshape(width, 128).T

    qs = np.asarray(inputs["L0_ffn_q_scale"], NF32)
    ks = np.asarray(inputs["L0_ffn_k_scale"], NF32)
    lg = np.asarray(inputs["L0_ffn_ln_g"], NF32)
    lb = np.asarray(inputs["L0_ffn_ln_b"], NF32)
    b2 = np.asarray(inputs["L0_ffn_b2"], NF32)
    b1 = np.asarray(inputs["L0_ffn_b1"], NF32)
    for l in range(2):
        cols.append(np.concatenate([qs[l], qs[l]]).reshape(128, 1))
        cols.append(np.concatenate([ks[l], ks[l]]).reshape(128, 1))
        cols.append(pp(lg[l], 8))
        cols.append(pp(lb[l], 8))
        cols.append(pp(b2[l], 8))
        b1a = np.zeros(HIDP, NF32)
        b1g = np.zeros(HIDP, NF32)
        b1a[:HID] = b1[l, :HID]
        b1g[:HID] = b1[l, HID:]
        cols.append(pp(b1a, NHC))
        cols.append(pp(b1g, NHC))
    hsl = slice(par * 512, par * 512 + 512)
    cols.append(pp(np.asarray(inputs["L0_attn_bo"], NF32), 8))
    cols.append(pp(np.asarray(inputs["L0_attn_bq"], NF32)[hsl], 4))
    cols.append(pp(np.asarray(inputs["L0_attn_bk"], NF32)[hsl], 4))
    cols.append(pp(np.asarray(inputs["L0_attn_bv"], NF32)[hsl], 4))
    cols.append(np.full((128, 1), 1.0 if par == 1 else 0.0, NF32))
    return np.ascontiguousarray(np.concatenate(cols, 1))


def _b16(a):
    return np.ascontiguousarray(np.asarray(a, NF32).astype(ml_dtypes.bfloat16))


def core_inputs(inputs, core):
    b, par = core // 2, core % 2
    x0 = np.asarray(inputs["L0_attn_hidden_states"], NF32)
    xb = np.ascontiguousarray(x0[b].T)
    x0w = np.zeros((D, PW), NF32)
    lo = par * OWN - HALO
    s0 = max(0, lo)
    x0w[:, 1 + (s0 - lo):] = xb[:, s0:par * OWN + OWN]
    w1 = np.asarray(inputs["L0_ffn_w1"], NF32)
    w1a = np.zeros((2, D, HIDP), NF32)
    w1g = np.zeros((2, D, HIDP), NF32)
    w1a[:, :, :HID] = w1[:, :, :HID]
    w1g[:, :, :HID] = w1[:, :, HID:]
    w2 = np.asarray(inputs["L0_ffn_w2"], NF32)
    w2p = np.zeros((2, HIDP, D), NF32)
    w2p[:, :HID, :] = w2
    hsl = slice(par * 512, par * 512 + 512)
    csq, ssq, csk, ssk = rotary_tables(par)
    return {
        "x0w": x0w,
        "wqkv": _b16(np.asarray(inputs["L0_ffn_wqkv"], NF32)),
        "wo": _b16(np.asarray(inputs["L0_ffn_wo"], NF32)),
        "w1a": _b16(w1a),
        "w1g": _b16(w1g),
        "w2p": _b16(w2p),
        "wqT": _b16(np.asarray(inputs["L0_attn_Wq"], NF32).T[:, hsl]),
        "wkT": _b16(np.asarray(inputs["L0_attn_Wk"], NF32).T[:, hsl]),
        "wvT": _b16(np.asarray(inputs["L0_attn_Wv"], NF32).T[:, hsl]),
        "woT": _b16(np.asarray(inputs["L0_attn_Wo"], NF32).T[hsl, :]),
        "bv512": _b16(np.asarray(inputs["L0_attn_bv"], NF32)[hsl].reshape(1, 512)),
        "csq": csq, "ssq": ssq, "csk": csk, "ssk": ssk,
        "maskq": maskq_blob(par), "maskk": maskk_blob(par)[0],
        "maskv": maskv_blob(), "kaugc": a_kaug_rows(), "qaugc": a_qaug_rows(),
        "blkind": blkind_blob(), "cbf": const_bf(), "cf32": const_f32(),
        "vecs": vec_blob(inputs, par),
    }


class MK:
    def __init__(self):
        self.nc = bass.Bass("TRN2", target_bir_lowering=False)
        self.ctx = []
        self.sems = {}
        self.waited = {}
        self.namecount = {}

    def enter(self, cm):
        v = cm.__enter__()
        self.ctx.append(cm)
        return v

    ARENA = 212736

    def _uniq(self, name):
        n = self.namecount.get(name, 0)
        self.namecount[name] = n + 1
        return name if n == 0 else f"{name}_{n}"

    def init_arena(self):
        self.arena_h = self.enter(self.nc.sbuf_tensor("arena", [128, self.ARENA], mybir.dt.uint8))
        self.arena_base = self.nc.lookup_mloc(self.arena_h).addr
        self.aptr = 0
        self.apeak = 0

    def sbuf(self, name, shape, dt):
        per = 1
        for s in shape[1:]:
            per *= s
        per *= mybir.dt.size(dt)
        addr = (self.aptr + 31) & ~31
        assert addr + per <= self.ARENA, f"arena overflow at {name}: {addr+per} > {self.ARENA}"
        self.aptr = addr + per
        self.apeak = max(self.apeak, self.aptr)
        return self.nc.alloc_sbuf_tensor_at(name, shape, dt,
                                            offset=self.arena_base + addr)

    def psum(self, name, shape, dt=F32):
        return self.enter(self.nc.psum_tensor(self._uniq(name), shape, dt))

    def scope(self):
        return (len(self.ctx), self.aptr)

    def end_scope(self, mark):
        n, ap = mark
        while len(self.ctx) > n:
            self.ctx.pop().__exit__(None, None, None)
        self.aptr = ap

    def sem(self, name):
        if name not in self.sems:
            h = self.enter(self.nc.semaphore(name))
            self.sems[name] = [h, 0, name]
        return self.sems[name]

    def inc(self, instr, s, k=1):
        instr.then_inc(s[0], k)
        s[1] += k
        return (s, s[1])

    def wait(self, eng, engname, *tickets):
        for tk in tickets:
            if tk is None:
                continue
            s, v = tk
            key = (engname, s[2])
            if self.waited.get(key, -1) < v:
                eng.wait_ge(s[0], v)
                self.waited[key] = v

    def lk(self, e, en, instr):
        tk = self.inc(instr, self.sem(en))
        self.wait(e, en, tk)

    def barrier_from(self, eng, engname, *semnames):
        for sn in semnames:
            s = self.sem(sn)
            if s[1] > 0:
                self.wait(eng, engname, (s, s[1]))

    def _run(self, sec, fn, en):
        out = []
        sec(lambda e: out.append(fn(e, en)))
        return out[0]

    def T(self, fn):
        return self._run(self.blk.tensor, fn, "pe")

    def V(self, fn):
        return self._run(self.blk.vector, fn, "dve")

    def C(self, fn):
        return self._run(self.blk.scalar, fn, "act")

    def G(self, fn):
        return self._run(self.blk.gpsimd, fn, "pool")

    def Y(self, fn):
        return self._run(self.blk.sync, fn, "sp")

    def full_barrier(self):
        def mk(e, en):
            self.barrier_from(e, en, *list(self.sems.keys()))
        self.T(mk)
        self.V(mk)
        self.C(mk)
        self.G(mk)
        self.Y(mk)

    def build(self):
        nc = self.nc
        ap = lambda t: t.ap()
        self.d_x0w = ap(nc.dram_tensor("x0w", [D, PW], F32, kind="ExternalInput"))
        self.d_wqkv = ap(nc.dram_tensor("wqkv", [2, D, 1536], BF16, kind="ExternalInput"))
        self.d_wo = ap(nc.dram_tensor("wo", [2, 512, D], BF16, kind="ExternalInput"))
        self.d_w1a = ap(nc.dram_tensor("w1a", [2, D, HIDP], BF16, kind="ExternalInput"))
        self.d_w1g = ap(nc.dram_tensor("w1g", [2, D, HIDP], BF16, kind="ExternalInput"))
        self.d_w2p = ap(nc.dram_tensor("w2p", [2, HIDP, D], BF16, kind="ExternalInput"))
        self.d_wqT = ap(nc.dram_tensor("wqT", [D, 512], BF16, kind="ExternalInput"))
        self.d_wkT = ap(nc.dram_tensor("wkT", [D, 512], BF16, kind="ExternalInput"))
        self.d_wvT = ap(nc.dram_tensor("wvT", [D, 512], BF16, kind="ExternalInput"))
        self.d_woT = ap(nc.dram_tensor("woT", [512, D], BF16, kind="ExternalInput"))
        self.d_bv512 = ap(nc.dram_tensor("bv512", [1, 512], BF16, kind="ExternalInput"))
        for n in ("csq", "ssq", "csk", "ssk"):
            setattr(self, "d_" + n, ap(nc.dram_tensor(n, [128, W], BF16, kind="ExternalInput")))
        self.d_maskq = ap(nc.dram_tensor("maskq", [128, 6 * 640], BF16, kind="ExternalInput"))
        mkw = maskk_blob(0)[0].shape[1]
        self.d_maskk = ap(nc.dram_tensor("maskk", [128, mkw], BF16, kind="ExternalInput"))
        self.d_maskv = ap(nc.dram_tensor("maskv", [128, SEQ], BF16, kind="ExternalInput"))
        self.d_kaugc = ap(nc.dram_tensor("kaugc", [16, SEQ], BF16, kind="ExternalInput"))
        self.d_qaugc = ap(nc.dram_tensor("qaugc", [16, 512], BF16, kind="ExternalInput"))
        self.d_blkind = ap(nc.dram_tensor("blkind", [128, 32 * 128], BF16, kind="ExternalInput"))
        self.d_cbf = ap(nc.dram_tensor("cbf", [128, CBF_W], BF16, kind="ExternalInput"))
        self.d_cf32 = ap(nc.dram_tensor("cf32", [128, CF_W], F32, kind="ExternalInput"))
        self.d_vecs = ap(nc.dram_tensor("vecs", [128, VW], F32, kind="ExternalInput"))
        self.d_y = ap(nc.dram_tensor("y", [D, OWN], F32, kind="ExternalOutput"))
        if DUMP:
            self.d_dbg = [ap(nc.dram_tensor(f"dbg{f}", [128, PW], F32, kind="ExternalOutput"))
                          for f in range(NCH)]
        self.d_x1 = ap(nc.dram_tensor("x1d", [D, OWN], F32, kind="Internal"))
        self.d_x2 = ap(nc.dram_tensor("x2d", [D, OWN], F32, kind="Internal"))
        self.e1_in = ap(nc.dram_tensor("e1_in", [D, 514], F32, kind="Internal"))
        self.e1_out = ap(nc.dram_tensor("e1_out", [2 * D, 514], F32, kind="Internal"))
        self.e2a_in = ap(nc.dram_tensor("e2a_in", [D, OWN], BF16, kind="Internal"))
        self.e2a_out = ap(nc.dram_tensor("e2a_out", [2 * D, OWN], BF16, kind="Internal"))
        self.e2bi = [ap(nc.dram_tensor(f"e2bi{j}", [D, 1024], F32, kind="Internal"))
                     for j in range(4)]
        self.e2bo = [ap(nc.dram_tensor(f"e2bo{j}", [D, 1024], F32, kind="Internal"))
                     for j in range(4)]
        ik = "ExternalOutput" if DUMP else "Internal"
        self.vdram = ap(nc.dram_tensor("vdram", [NKC * 128, 512], BF16, kind=ik))
        self.cxdram = ap(nc.dram_tensor("cxdram", [512, W - 512], BF16, kind=ik))
        self.e3_in = ap(nc.dram_tensor("e3_in", [D, 514], F32, kind="Internal"))
        self.e3_out = ap(nc.dram_tensor("e3_out", [2 * D, 514], F32, kind="Internal"))
        self.rg = [[0, 1], [2, 3], [4, 5], [6, 7]]

        self.init_arena()
        self.cbf = self.sbuf("cbf_sb", [128, CBF_W], BF16)
        self.cf32 = self.sbuf("cf32_sb", [128, CF_W], F32)
        self.vecs = self.sbuf("vecs_sb", [128, VW], F32)
        self.qs8 = self.sbuf("qs8", [128, 2], F32)
        self.bq8 = self.sbuf("bq8", [128, 4], F32)
        self.hvi = self.sbuf("hvi", [128, 1], F32)
        self.mq = self.sbuf("mq_sb", [128, 6 * 640], BF16)
        self.mkb = self.sbuf("mk_sb", [128, mkw], BF16)
        for sn in ("pe", "dve", "act", "pool", "cc",
                   "cld", "xld", "wvld", "vst0", "vst1", "vhld", "wqld", "tbld",
                   "cxst0", "cxst1", "cxld0", "cxld1", "wold", "w1ld0", "w1ld1",
                   "w2ld0", "w2ld1", "exst", "exld",
                   "x1st", "e2ast", "ald", "awld", "akld", "aqld", "bkld", "qmst",
                   "e2bst0", "e2bst1", "x2ld", "x2st", "yst", "dbgst", "dbgst2"):
            self.sem(sn)

        with self.nc.Block() as blk:
            self.blk = blk
            self.ph_consts()
            mark_sa = self.scope()
            self.S = [self.sbuf(f"Sa{f}", [128, PW], F32) for f in range(NCH)]
            self.s_init(0)
            self.ph_y1()
            if STAGE >= 2:
                self.f_call(first=True)
            if STAGE >= 5:
                self.ph_x1y2()
            if DUMP and STAGE < 6:
                self.ph_dump()
            if STAGE >= 6:
                self.full_barrier()
                self.end_scope(mark_sa)
                self.ph_A()
                mark_sb = self.scope()
                self.S = [self.sbuf(f"Sb{f}", [128, PW], F32) for f in range(NCH)]
                self.s_init(641)
                self.ph_x2y3()
                if STAGE >= 7:
                    self.f_call(first=False)
                    self.ph_out()
                if DUMP:
                    self.ph_dump()
                self.full_barrier()
                self.end_scope(mark_sb)
            elif STAGE >= 7 or not DUMP:
                self.ph_out()
            self.ph_fin()
        while self.ctx:
            self.ctx.pop().__exit__(None, None, None)
        self.arena_peak = self.apeak
        return self.nc

    def ph_consts(self):
        sld, sdve = self.sem("cld"), self.sem("dve")

        def loads(y, en):
            tk = None
            for dst, src in ((self.cbf, self.d_cbf), (self.cf32, self.d_cf32),
                             (self.vecs, self.d_vecs), (self.mq, self.d_maskq),
                             (self.mkb, self.d_maskk)):
                tk = self.inc(y.dma_start(out=dst[:], in_=src), sld, 16)
            return tk
        tk = self.Y(loads)

        def prep(v, en):
            self.wait(v, en, tk)
            for l in range(2):
                v.tensor_scalar_mul(self.qs8[:, l:l + 1],
                                    self.vecs[:, VL * l + V_QS:VL * l + V_QS + 1], 8.0)
            v.tensor_scalar_mul(self.bq8[:], self.vecs[:, V_BQ:V_BQ + 4], 0.125)
            i = v.tensor_scalar(out=self.hvi[:], in0=self.vecs[:, V_HV:V_HV + 1],
                                scalar1=-1.0, scalar2=1.0, op0=ALU.mult, op1=ALU.add)
            return self.inc(i, sdve)
        self.V(prep)

    def s_init(self, upto):
        def z(v, en):
            i = None
            for f in range(NCH):
                i = v.memset(self.S[f][:, 0:max(1, upto)], 0.0)
            return self.inc(i, self.sem("dve"))
        self.V(z)

    def mk_ln_pools(self, tag):
        return {
            "sq": [self.sbuf(f"{tag}sq{f}", [128, 512], BF16) for f in range(NCH)],
            "mu": self.sbuf(f"{tag}mu", [1, 512], F32),
            "ex2": self.sbuf(f"{tag}e2", [1, 512], F32),
            "var": self.sbuf(f"{tag}va", [1, 512], F32),
            "sd": self.sbuf(f"{tag}sd", [1, 512], F32),
            "rstd": self.sbuf(f"{tag}rs", [1, 512], F32),
            "rbs": self.sbuf(f"{tag}rb", [128, 512], F32),
            "sx": self.psum(f"{tag}px", [1, 512]),
            "sq2": self.psum(f"{tag}pq", [1, 512]),
            "mub": self.psum(f"{tag}pm", [128, 512]),
            "rbp": self.psum(f"{tag}pr", [128, 512]),
            "war": {},
        }

    def ln_stats(self, xb_tiles, Tt, p, tk_in):
        spe, sdve, sact = self.sem("pe"), self.sem("dve"), self.sem("act")
        spool = self.sem("pool")

        def sqf(g, en):
            self.wait(g, en, tk_in, p["war"].get("sq"))
            i = None
            for f in range(NCH):
                i = g.tensor_mul(p["sq"][f][:, 0:Tt], xb_tiles[f][:, 0:Tt],
                                 xb_tiles[f][:, 0:Tt])
            return self.inc(i, spool)
        tk_sq = self.G(sqf)

        def mm(t, en):
            self.wait(t, en, tk_sq, p["war"].get("sx"))
            for f in range(NCH):
                t.matmul(p["sx"][0:1, 0:Tt], self.cbf[:, CBF_ONES1:CBF_ONES1 + 1],
                         xb_tiles[f][:, 0:Tt], start=(f == 0), stop=(f == NCH - 1))
            i = None
            for f in range(NCH):
                i = t.matmul(p["sq2"][0:1, 0:Tt], self.cbf[:, CBF_ONES1:CBF_ONES1 + 1],
                             p["sq"][f][:, 0:Tt], start=(f == 0), stop=(f == NCH - 1))
            return self.inc(i, spe)
        tk_mm = self.T(mm)
        p["war"]["sq"] = tk_mm

        def strow(v, en):
            self.wait(v, en, tk_mm)
            i = v.tensor_scalar_mul(p["mu"][:, 0:Tt], p["sx"][0:1, 0:Tt], 1.0 / D)
            self.lk(v, en, i)
            v.tensor_scalar_mul(p["ex2"][:, 0:Tt], p["sq2"][0:1, 0:Tt], 1.0 / D)
            i = v.tensor_mul(p["var"][:, 0:Tt], p["mu"][:, 0:Tt], p["mu"][:, 0:Tt])
            self.lk(v, en, i)
            i = v.tensor_sub(p["var"][:, 0:Tt], p["ex2"][:, 0:Tt], p["var"][:, 0:Tt])
            return self.inc(i, sdve)
        tk_st = self.V(strow)
        p["war"]["sx"] = tk_st

        def sdf(c, en):
            self.wait(c, en, tk_st)
            i = c.activation(out=p["sd"][:, 0:Tt], in_=p["var"][:, 0:Tt], func=AF.Sqrt,
                             bias=self.cf32[0:1, CF_EPS:CF_EPS + 1])
            return self.inc(i, sact)
        tk_sd = self.C(sdf)

        def rs(v, en):
            self.wait(v, en, tk_sd)
            i = v.reciprocal(p["rstd"][:, 0:Tt], p["sd"][:, 0:Tt])
            return self.inc(i, sdve)
        tk_rs = self.V(rs)

        def bc(t, en):
            self.wait(t, en, tk_rs, p["war"].get("mub"), p["war"].get("rbp"))
            t.matmul(p["mub"][:, 0:Tt], self.cf32[0:1, CF_ONESROW:CF_ONESROW + 128],
                     p["mu"][:, 0:Tt], start=True, stop=True)
            i = t.matmul(p["rbp"][:, 0:Tt], self.cf32[0:1, CF_ONESROW:CF_ONESROW + 128],
                         p["rstd"][:, 0:Tt], start=True, stop=True)
            return self.inc(i, spe)
        tk_bc = self.T(bc)

        def rb(v, en):
            self.wait(v, en, tk_bc)
            i = v.tensor_copy(p["rbs"][:, 0:Tt], p["rbp"][:, 0:Tt])
            return self.inc(i, sdve)
        tk_rb = self.V(rb)
        p["war"]["rbp"] = tk_rb
        return tk_bc, tk_rb

    def ph_y1(self):
        mark = self.scope()
        sld, sdve = self.sem("xld"), self.sem("dve")
        xt = [self.sbuf(f"p1x{f}", [128, 512], F32) for f in range(NCH)]
        xb = [self.sbuf(f"p1b{f}", [128, 512], BF16) for f in range(NCH)]
        t1 = [self.sbuf(f"p1t{f}", [128, 512], F32) for f in range(NCH)]
        p = self.mk_ln_pools("p1")
        war_x = None
        for ti in range(NTI):
            a0 = 1 + 512 * ti
            Tt = min(512, PW - a0)

            def ld(y, en):
                self.wait(y, en, war_x)
                i = None
                for f in range(NCH):
                    i = self.inc(y.dma_start(out=xt[f][:, 0:Tt],
                                             in_=self.d_x0w[128 * f:128 * f + 128, a0:a0 + Tt]),
                                 sld, 16)
                return i
            tk_ld = self.Y(ld)

            def cast(v, en):
                self.wait(v, en, tk_ld)
                i = None
                for f in range(NCH):
                    i = v.tensor_copy(xb[f][:, 0:Tt], xt[f][:, 0:Tt])
                return self.inc(i, sdve)
            tk_cast = self.V(cast)
            tk_mub, tk_rb = self.ln_stats(xb, Tt, p, tk_cast)

            def app(v, en):
                self.wait(v, en, tk_rb)
                i = None
                for f in range(NCH):
                    i = v.tensor_sub(t1[f][:, 0:Tt], xt[f][:, 0:Tt], p["mub"][:, 0:Tt])
                self.lk(v, en, i)
                for f in range(NCH):
                    i = v.tensor_mul(self.S[f][:, a0:a0 + Tt], t1[f][:, 0:Tt], p["rbs"][:, 0:Tt])
                return self.inc(i, sdve)
            tk_ap = self.V(app)
            war_x = tk_ap
            p["war"]["mub"] = tk_ap
        self.full_barrier()
        self.end_scope(mark)

    def cbar(self):
        self.full_barrier()

    def f_call(self, first):
        maxil = 2
        if first and (STAGE == 2 or STOP):
            maxil = 1
        for l in range(maxil):
            self.f_il(l)
            if l == 0 and not STOP:
                self.exchange_halo(self.e1_in, self.e1_out)

    def exchange_halo(self, din, dout):
        mark = self.scope()
        sst, scc, sld, sdve = (self.sem(x) for x in ("exst", "cc", "exld", "dve"))
        hr = [self.sbuf(f"hr{f}", [128, 514], F32) for f in range(NCH)]

        def snd(y, en):
            self.barrier_from(y, en, "dve")
            i = None
            for f in range(NCH):
                i = self.inc(y.dma_start(out=din[128 * f:128 * f + 128, :],
                                         in_=self.S[f][:, 2175:2689]), sst, 16)
            return i
        tk = self.Y(snd)

        def coll(g, en):
            self.wait(g, en, tk)
            return self.inc(g.collective_compute(
                "AllGather", ALU.bypass, replica_groups=self.rg,
                ins=[din.opt()], outs=[dout.opt()]), scc, 1)
        tkc = self.G(coll)

        def rcv(y, en):
            self.wait(y, en, tkc)
            i = None
            for f in range(NCH):
                i = self.inc(y.dma_start(out=hr[f][:], in_=dout[128 * f:128 * f + 128, :]),
                             sld, 16)
            return i
        tkr = self.Y(rcv)

        def wr(v, en):
            self.wait(v, en, tkr)
            i = None
            for f in range(NCH):
                i = v.tensor_scalar_mul(self.S[f][:, 127:641], hr[f][:],
                                        self.vecs[:, V_HV:V_HV + 1])
            return self.inc(i, sdve)
        self.V(wr)
        self.full_barrier()
        self.end_scope(mark)

    def f_il(self, l):
        spe, sdve, sact, spool = (self.sem(x) for x in ("pe", "dve", "act", "pool"))
        mark_il = self.scope()
        qaug = [self.sbuf(f"qaug{h}", [65, W], BF16) for h in range(2)]
        kaug = [self.sbuf(f"kaug{h}", [65, W], BF16) for h in range(2)]

        mark_sb = self.scope()
        Sb = [self.sbuf(f"Sh{f}", [128, PW], BF16) for f in range(NCH)]

        def cast(v, en):
            self.barrier_from(v, en, *list(self.sems.keys()))
            for f in range(NCH):
                v.tensor_copy(Sb[f][:], self.S[f][:])
            v.memset(kaug[0][64:65, :], 1.0)
            v.memset(kaug[1][64:65, :], 1.0)
            v.memset(qaug[0][64:65, :], 0.0)
            i = v.memset(qaug[1][64:65, :], 0.0)
            return self.inc(i, sdve)
        tk_cast = self.V(cast)

        def xsh(f, a0, Tt):
            off = 1 if f < 4 else 0
            return Sb[f][:, off + a0:off + a0 + Tt]

        mark_v = self.scope()
        wv = [self.sbuf(f"wv{f}", [128, 512], BF16) for f in range(NCH)]

        def ldwv(g, en):
            i = None
            for f in range(NCH):
                i = self.inc(g.dma_start(out=wv[f][:],
                                         in_=self.d_wqkv[l, 128 * f:128 * f + 128, 1024:1536]),
                             self.sem("wvld"), 16)
            return i
        tk_wv = self.Y(ldwv)
        vstg = [self.sbuf(f"vst{i}", [128, 512], BF16) for i in range(2)]
        vps = [self.psum(f"vps{i}", [128, 512]) for i in range(2)]
        warv = [None, None]
        tk_vst = None
        for ts in range(NKC):
            a0, bi = 128 * ts, ts % 2

            def vmm(t, en, a0=a0, bi=bi):
                self.wait(t, en, tk_cast, tk_wv)
                if warv[bi] is not None:
                    self.wait(t, en, warv[bi])
                i = None
                for f in range(NCH):
                    i = t.matmul(vps[bi][:], xsh(f, a0, 128), wv[f][:],
                                 start=(f == 0), stop=(f == NCH - 1))
                return self.inc(i, spe)
            tkv = self.T(vmm)

            def vev(v, en, bi=bi):
                self.wait(v, en, tkv, warv[bi])
                return self.inc(v.tensor_copy(vstg[bi][:], vps[bi][:]), sdve)
            tke = self.V(vev)

            def vst(y, en, bi=bi, ts=ts):
                self.wait(y, en, tke)
                return self.inc(y.dma_start(out=self.vdram[128 * ts:128 * ts + 128, :],
                                            in_=vstg[bi][:]), self.sem(f"vst{bi}"), 16)
            warv[bi] = self.Y(vst)
        tk_vblob = (warv[0], warv[1])
        self.cbar()
        self.end_scope(mark_v)

        nhp = 1 if STOP == "i" else 4
        tcs = self.sbuf("tcs", [128, W], BF16)
        tss = self.sbuf("tss", [128, W], BF16)
        wqkq = self.sbuf("wqkq", [128, 1024], BF16)
        wqkk = self.sbuf("wqkk", [128, 1024], BF16)
        ilwar = {}
        for hp in range(nhp):
            mark_hp = self.scope()

            def ldw(g, en, hp=hp):
                self.wait(g, en, ilwar.get("wqk"))
                self.inc(g.dma_start(
                    out=wqkq[:].rearrange("p (c n) -> p c n", c=8),
                    in_=self.d_wqkv[l].rearrange("(c p) n -> p c n", p=128)[:, :, 128 * hp:128 * hp + 128]),
                    self.sem("wqld"), 16)
                return self.inc(g.dma_start(
                    out=wqkk[:].rearrange("p (c n) -> p c n", c=8),
                    in_=self.d_wqkv[l].rearrange("(c p) n -> p c n", p=128)[:, :, 512 + 128 * hp:512 + 128 * hp + 128]),
                    self.sem("wqld"), 16)
            tk_wqk = self.Y(ldw)

            mark_i = self.scope()
            pps = [self.psum(f"ipp{i}", [128, 512]) for i in range(2)]
            nps = [self.psum(f"inp{i}", [33, 512]) for i in range(2)]
            bps = [self.psum(f"ibp{i}", [128, 512]) for i in range(2)]
            rps = [self.psum(f"irp{i}", [128, 512]) for i in range(2)]
            sqs = [self.sbuf(f"isq{i}", [128, 512], BF16) for i in range(2)]
            nrm = [self.sbuf(f"inr{i}", [33, 512], F32) for i in range(2)]
            nrr = [self.sbuf(f"inrr{i}", [33, 512], BF16) for i in range(2)]
            bcb = [self.sbuf(f"ibc{i}", [128, 512], F32) for i in range(2)]
            qn = [self.sbuf(f"iqn{i}", [128, 512], BF16) for i in range(2)]
            rawb = [self.sbuf(f"irw{i}", [128, 512], F32) for i in range(2)]
            qcs = [self.sbuf(f"iqc{i}", [128, 512], BF16) for i in range(2)]
            qss = [self.sbuf(f"iqs{i}", [128, 512], BF16) for i in range(2)]
            war = {}
            self.cbar()
            tk10 = None
            for side in range(NSIDE):
                wsb = wqkq if side == 0 else wqkk
                dst = qaug if side == 0 else kaug

                def ldt(g, en, side=side):
                    self.wait(g, en, ilwar.get("tabs"))
                    srcs = (self.d_csq, self.d_ssq) if side == 0 else (self.d_csk, self.d_ssk)
                    self.inc(g.dma_start(out=tcs[:], in_=srcs[0]), self.sem("tbld"), 16)
                    return self.inc(g.dma_start(out=tss[:], in_=srcs[1]), self.sem("tbld"), 16)
                tk_tab = self.Y(ldt)

                for ti in range(NTILE):
                    a0 = 512 * ti
                    Tt = min(512, W - a0)
                    bi = ti % 2

                    def qmm(t, en, a0=a0, Tt=Tt, bi=bi, wsb=wsb):
                        self.wait(t, en, tk_cast, tk_wqk, war.get(("pp", bi)))
                        i = None
                        for f in range(NCH):
                            i = t.matmul(pps[bi][:, 0:Tt], wsb[:, 128 * f:128 * f + 128],
                                         xsh(f, a0, Tt), start=(f == 0),
                                         stop=(f == NCH - 1))
                        return self.inc(i, spe)
                    tk1 = self.T(qmm)

                    def rawev(v, en, Tt=Tt, bi=bi):
                        self.wait(v, en, tk1, war.get(("raw", bi)))
                        return self.inc(v.tensor_copy(rawb[bi][:, 0:Tt],
                                                      pps[bi][:, 0:Tt]), sdve)
                    tk1b = self.V(rawev)
                    war[("pp", bi)] = tk1b

                    def sqf(g_, en, Tt=Tt, bi=bi):
                        self.wait(g_, en, tk1b, war.get(("sqs", bi)))
                        return self.inc(g_.tensor_mul(sqs[bi][:, 0:Tt], rawb[bi][:, 0:Tt],
                                                      rawb[bi][:, 0:Tt]), spool)
                    tk2 = self.G(sqf)

                    def nmm(t, en, Tt=Tt, bi=bi):
                        self.wait(t, en, tk2, war.get(("np", bi)))
                        return self.inc(t.matmul(nps[bi][0:33, 0:Tt],
                                                 self.cbf[:, CBF_HS33:CBF_HS33 + 33],
                                                 sqs[bi][:, 0:Tt], start=True, stop=True), spe)
                    tk3 = self.T(nmm)
                    war[("sqs", bi)] = tk3

                    def sqrtf(c, en, Tt=Tt, bi=bi):
                        self.wait(c, en, tk3, war.get(("nrm", bi)))
                        return self.inc(c.activation(out=nrm[bi][:, 0:Tt], in_=nps[bi][0:33, 0:Tt],
                                                     func=AF.Sqrt), sact)
                    tk4 = self.C(sqrtf)
                    war[("np", bi)] = tk4

                    def recf(v, en, Tt=Tt, bi=bi):
                        self.wait(v, en, tk4, war.get(("nrr", bi)))
                        self.lk(v, en, v.tensor_scalar_max(nrm[bi][:, 0:Tt], nrm[bi][:, 0:Tt], 1e-12))
                        with self.nc.allow_low_precision(reason="bf16 recip-norm feeds bf16 matmul broadcast; modeled in mk_math"):
                            i = v.reciprocal(nrr[bi][:, 0:Tt], nrm[bi][:, 0:Tt])
                        return self.inc(i, sdve)
                    tk5 = self.V(recf)
                    war[("nrm", bi)] = tk5

                    def bcf(t, en, Tt=Tt, bi=bi):
                        self.wait(t, en, tk5, war.get(("bp", bi)))
                        return self.inc(t.matmul(bps[bi][:, 0:Tt],
                                                 self.cbf[0:33, CBF_SEL33:CBF_SEL33 + 128],
                                                 nrr[bi][:, 0:Tt], start=True, stop=True), spe)
                    tk6 = self.T(bcf)
                    war[("nrr", bi)] = tk6

                    def bcc(v, en, Tt=Tt, bi=bi):
                        self.wait(v, en, tk6, war.get(("bcb", bi)))
                        return self.inc(v.tensor_copy(bcb[bi][:, 0:Tt], bps[bi][:, 0:Tt]),
                                        sdve)
                    tk7 = self.V(bcc)
                    war[("bp", bi)] = tk7

                    def stf(v, en, Tt=Tt, bi=bi, side=side):
                        self.wait(v, en, tk7, war.get(("qn", bi)))
                        sc = self.qs8[:, l:l + 1] if side == 0 else \
                            self.vecs[:, VL * l + V_KS:VL * l + V_KS + 1]
                        i = v.scalar_tensor_tensor(out=qn[bi][:, 0:Tt], in0=rawb[bi][:, 0:Tt],
                                                   scalar=sc, in1=bcb[bi][:, 0:Tt],
                                                   op0=ALU.mult, op1=ALU.mult)
                        return self.inc(i, sdve)
                    tk8a = self.V(stf)
                    war[("raw", bi)] = tk8a
                    war[("bcb", bi)] = tk8a

                    def csf(g_, en, Tt=Tt, bi=bi, a0=a0):
                        self.wait(g_, en, tk8a, tk_tab, war.get(("qn", bi)))
                        g_.tensor_mul(qcs[bi][:, 0:Tt], qn[bi][:, 0:Tt], tcs[:, a0:a0 + Tt])
                        return self.inc(g_.tensor_mul(qss[bi][:, 0:Tt], qn[bi][:, 0:Tt],
                                                      tss[:, a0:a0 + Tt]), spool)
                    tk8 = self.G(csf)
                    ilwar["tabs"] = tk8

                    def rotf(t, en, Tt=Tt, bi=bi):
                        self.wait(t, en, tk8, war.get(("rp", bi)))
                        return self.inc(t.matmul(rps[bi][:, 0:Tt], self.cbf[:, CBF_R2:CBF_R2 + 128],
                                                 qss[bi][:, 0:Tt], start=True, stop=True), spe)
                    tk9 = self.T(rotf)

                    def evf(v, en, Tt=Tt, bi=bi, a0=a0, dst=dst):
                        self.wait(v, en, tk9)
                        self.lk(v, en, v.tensor_add(dst[0][0:64, a0:a0 + Tt],
                                                    rps[bi][0:64, 0:Tt],
                                                    qcs[bi][0:64, 0:Tt]))
                        return self.inc(v.tensor_add(dst[1][0:64, a0:a0 + Tt],
                                                     rps[bi][64:128, 0:Tt],
                                                     qcs[bi][64:128, 0:Tt]), sdve)
                    tk10 = self.V(evf)
                    war[("rp", bi)] = tk10
                    war[("qn", bi)] = tk10
                war[("aug", side)] = tk10
                ilwar["wqk"] = tk1
            self.cbar()
            self.end_scope(mark_i)
            if STOP != "i":
                self.f_attn_hp(l, hp, qaug, kaug,
                               war[("aug", 0)], war[("aug", 1)], tk_vblob)
            self.end_scope(mark_hp)
        self.cbar()
        if STOP == "i":
            self.full_barrier()

            def dmpa(g, en):
                i = None
                tensors = (qaug[0], qaug[1], kaug[0], kaug[1])[:2 * NSIDE]
                wd = 512 * NTILE
                for k, t in enumerate(tensors):
                    i = self.inc(g.dma_start(out=self.d_dbg[k][0:65, 0:wd], in_=t[0:65, 0:wd]),
                                 self.sem("dbgst2"), 16)
                return i
            self.G(dmpa)
            self.full_barrier()
            return
        self.end_scope(mark_sb)

        mark_wo = self.scope()
        wosb = [self.sbuf(f"wosb{c}", [128, 1024], BF16) for c in range(4)]

        def ldwo(g, en):
            i = None
            for c in range(4):
                i = self.inc(g.dma_start(out=wosb[c][:],
                                         in_=self.d_wo[l, 128 * c:128 * c + 128, :]),
                             self.sem("wold"), 16)
            return i
        tk_wo = self.Y(ldwo)
        wop = [self.psum(f"wop{i}", [128, 512]) for i in range(2)]
        cxt = [[self.sbuf(f"cxt{k}_{i}", [128, 512], BF16) for k in range(4)] for i in range(2)]
        warw = [None, None]
        warcx = [None, None]
        groups5 = [(512, 640)] + [(640 + 512 * g, 1152 + 512 * g) for g in range(4)]
        self.cbar()
        for gi, (glo, ghi) in enumerate(groups5):
            Tg = ghi - glo
            cbi = gi % 2

            def cxld(y, en, glo=glo, ghi=ghi, Tg=Tg, cbi=cbi):
                self.wait(y, en, warcx[cbi])
                self.barrier_from(y, en, "cxst0", "cxst1")
                i = None
                for kc in range(4):
                    i = self.inc(y.dma_start(out=cxt[cbi][kc][:, 0:Tg],
                                             in_=self.cxdram[128 * kc:128 * kc + 128,
                                                             glo - 512:ghi - 512]),
                                 self.sem(f"cxld{cbi}"), 16)
                return i
            tk_cx = self.Y(cxld)
            for f in range(NCH):
                bi = (gi * NCH + f) % 2

                def womm(t, en, f=f, bi=bi, cbi=cbi, Tg=Tg):
                    self.wait(t, en, tk_wo, tk_cx, warw[bi])
                    i = None
                    for kc in range(4):
                        i = t.matmul(wop[bi][:, 0:Tg], wosb[kc][:, 128 * f:128 * f + 128],
                                     cxt[cbi][kc][:, 0:Tg],
                                     start=(kc == 0), stop=(kc == 3))
                    return self.inc(i, spe)
                tkw = self.T(womm)
                if f == NCH - 1:
                    warcx[cbi] = tkw

                def woev(v, en, f=f, bi=bi, glo=glo, ghi=ghi, Tg=Tg):
                    self.wait(v, en, tkw)
                    i = v.scalar_tensor_tensor(out=self.S[f][:, glo + 1:ghi + 1],
                                               in0=wop[bi][:, 0:Tg], scalar=1.0,
                                               in1=self.S[f][:, glo + 1:ghi + 1],
                                               op0=ALU.mult, op1=ALU.add)
                    return self.inc(i, sdve)
                warw[bi] = self.V(woev)

        def hz(v, en):
            i = None
            for f in range(NCH):
                i = v.tensor_scalar_mul(self.S[f][:, 640:641], self.S[f][:, 640:641],
                                        self.vecs[:, V_HV:V_HV + 1])
            return self.inc(i, sdve)
        self.V(hz)
        self.cbar()
        self.end_scope(mark_wo)

        if STOP != "ii":
            self.f_mlp(l)
        self.end_scope(mark_il)

    def f_attn_hp(self, l, hp, qaug, kaug, tk_qaug, tk_kaug, tk_vblob):
        spe, sdve, sact, sld, spool = (self.sem(x) for x in
                                       ("pe", "dve", "act", "vhld", "pool"))
        sst = [self.sem("cxst0"), self.sem("cxst1")]
        groups5 = [(512, 640)] + [(640 + 512 * g, 1152 + 512 * g) for g in range(4)]
        for hl in range(2):
            hg = 2 * hp + hl
            mark_head = self.scope()
            vh = self.sbuf("vh", [128, NKC * 65], BF16)

            def vini(v, en):
                self.barrier_from(v, en, "pe")
                return self.inc(v.memset(vh[:], 1.0), sdve)
            tk_vh0 = self.V(vini)

            def vhl(y, en, hg=hg):
                self.wait(y, en, *tk_vblob)
                self.wait(y, en, tk_vh0)
                i = y.dma_start(
                    out=vh[:].rearrange("p (c e) -> p c e", c=NKC)[:, :, 0:64],
                    in_=self.vdram.rearrange("(c p) n -> p c n", p=128)[:, :, 64 * hg:64 * hg + 64])
                return self.inc(i, sld, 16)
            tk_vh = self.Y(vhl)

            mark_q = self.scope()
            pq = [self.psum(f"pq{i}", [128, 640]) for i in range(3)]
            tpp = self.psum("tpq", [NQT, 128])
            tpb = self.sbuf("tpb", [NQT, 128], F32)
            smq = [self.sbuf(f"smq{i}", [128, 640], F32) for i in range(3)]
            e8 = [self.sbuf(f"e8x{i}", [128, 640], BF16) for i in range(2)]
            s8a = self.sbuf("s8a", [128, NQT], F32)
            lcol = self.sbuf("lcq", [128, NQT], F32)
            war = {}
            self.cbar()
            tk3 = None
            msk_tk = {}
            for t in range(NQT):
                bi = t % 3
                mi = t if t < 5 else 5

                def qmm(tt, en, t=t, bi=bi):
                    self.wait(tt, en, tk_qaug, tk_kaug, war.get(("pq", bi)))
                    a = 128 * t
                    tt.matmul(pq[bi][:, 0:512], qaug[hl][0:64, 512 + a:640 + a],
                              kaug[hl][0:64, a:a + 512], start=True, stop=True)
                    i = tt.matmul(pq[bi][:, 512:640], qaug[hl][0:64, 512 + a:640 + a],
                                  kaug[hl][0:64, a + 512:a + 640], start=True, stop=True)
                    return self.inc(i, spe)
                tk1 = self.T(qmm)

                def mskf(v, en, bi=bi, mi=mi):
                    self.wait(v, en, tk1, war.get(("smq", bi)))
                    i = v.tensor_tensor(smq[bi][:], pq[bi][:],
                                        self.mq[:, 640 * mi:640 * mi + 640], ALU.add)
                    return self.inc(i, sdve)
                msk_tk[t] = self.V(mskf)
                war[("pq", bi)] = msk_tk[t]

                if t % 2 == 1 or t == NQT - 1:
                    ts = [t - 1, t] if t % 2 == 1 else [t]

                    def expf(c, en, ts=tuple(ts)):
                        self.wait(c, en, war.get("e8"))
                        for tj in ts:
                            self.wait(c, en, msk_tk[tj], war.get(("smq", tj % 3)))
                        i = None
                        for tj in ts:
                            i = c.activation(out=e8[tj % 2][:], in_=smq[tj % 3][:],
                                             func=AF.Exp, scale=0.125,
                                             accum_out=s8a[:, tj:tj + 1])
                        return self.inc(i, sact)
                    tk3 = self.C(expf)
                    war["e8"] = tk3
                    for tj in ts:
                        war[("smq", tj % 3)] = tk3

            def maxf(v, en):
                self.wait(v, en, tk3)
                i = v.tensor_scalar_max(s8a[:], s8a[:], 1e-35)
                return self.inc(i, sdve)
            tk4 = self.V(maxf)

            def lnf(c, en):
                self.wait(c, en, tk4)
                i = c.activation(out=lcol[:], in_=s8a[:], func=AF.Ln)
                return self.inc(i, sact)
            tk5 = self.C(lnf)

            def trf(tt, en):
                self.wait(tt, en, tk5)
                i = tt.transpose(tpp[:], lcol[:], self.cf32[:, CF_ID:CF_ID + 128])
                return self.inc(i, spe)
            tk6 = self.T(trf)

            def tcp(v, en):
                self.wait(v, en, tk6)
                return self.inc(v.tensor_scalar_mul(tpb[:], tpp[:], -8.0), sdve)
            tk6b = self.V(tcp)

            def mcp(g, en, hl=hl):
                self.wait(g, en, tk6b)
                i = g.dma_start(out=qaug[hl][64:65, 512:512 + 128 * NQT],
                                in_=tpb[0:NQT, 0:128])
                return self.inc(i, self.sem("qmst"), 16)
            tk_mrow = self.G(mcp)
            self.cbar()
            self.end_scope(mark_q)

            mark_k = self.scope()
            sk = [self.psum(f"sk{i}", [128, 640]) for i in range(3)]
            gctx = [self.psum(f"gx{i}", [65, 512]) for i in range(2)]
            zbp = gctx[1]
            smk = [self.sbuf(f"smk{i}", [128, 640], F32) for i in range(3)]
            probs = [self.sbuf(f"pb{i}", [128, 640], BF16) for i in range(2)]
            cstg = self.sbuf("cstg", [65, 2176], F32)
            ctev = self.sbuf("ctevh", [64, 2176], BF16)
            zrh = cstg[64:65, :]
            stg_tk = {}
            kmsk_tk = {}
            exp_tk = {}
            war = {}
            self.cbar()
            gfirst = {gi: True for gi in range(5)}
            ghw = {gi: 0 for gi in range(5)}
            for c in range(NKC):
                qlo, qhi = kspan(c)
                spanw = qhi - qlo
                bi = c % 3
                pi = c % 2
                mo, mw = maskk_slice(c)
                assert mw == spanw

                def kmm(tt, en, c=c, bi=bi, qlo=qlo, qhi=qhi):
                    self.wait(tt, en, tk_mrow, tk_kaug, war.get(("sk", bi)))
                    i = None
                    off = 0
                    while off < qhi - qlo:
                        wpc = min(512, qhi - qlo - off)
                        i = tt.matmul(sk[bi][:, off:off + wpc],
                                      kaug[hl][0:65, 128 * c:128 * c + 128],
                                      qaug[hl][0:65, qlo + off:qlo + off + wpc],
                                      start=True, stop=True)
                        off += wpc
                    return self.inc(i, spe)
                tk1 = self.T(kmm)

                def mskf(v, en, bi=bi, mo=mo, mw=mw):
                    self.wait(v, en, tk1, war.get(("smk", bi)))
                    i = v.tensor_tensor(smk[bi][:, 0:mw], sk[bi][:, 0:mw],
                                        self.mkb[:, mo:mo + mw], ALU.add)
                    return self.inc(i, sdve)
                kmsk_tk[c] = self.V(mskf)
                war[("sk", bi)] = kmsk_tk[c]

                if c % 2 == 1 or c == NKC - 1:
                    cs_ = [c - 1, c] if c % 2 == 1 else [c]

                    def expf(cc, en, cs_=tuple(cs_)):
                        for cj in cs_:
                            self.wait(cc, en, kmsk_tk[cj], war.get(("pb", cj % 2)))
                        i = None
                        for cj in cs_:
                            mwj = kspan(cj)[1] - kspan(cj)[0]
                            i = cc.activation(out=probs[cj % 2][:, 0:mwj],
                                              in_=smk[cj % 3][:, 0:mwj], func=AF.Exp)
                        return self.inc(i, sact)
                    tk3 = self.C(expf)
                    for cj in cs_:
                        war[("smk", cj % 3)] = tk3
                        exp_tk[cj] = tk3

                    for cj in cs_:
                        qloj, qhij = kspan(cj)
                        pij = cj % 2
                        for gi, (glo, ghi) in enumerate(groups5):
                            a, bnd = max(qloj, glo), min(qhij, ghi)
                            if a >= bnd:
                                continue
                            gslot = gi % 2
                            clast = (ghi - 1) // 128
                            lo_g, hi_g = a - glo, bnd - glo
                            vhw = ghw[gi]
                            parts = []
                            if lo_g < vhw:
                                parts.append((lo_g, min(hi_g, vhw)))
                            if vhw < hi_g:
                                parts.append((max(lo_g, vhw), hi_g))
                            ghw[gi] = max(vhw, hi_g)

                            def cmm(tt, en, c=cj, gi=gi, gslot=gslot, glo=glo,
                                    qlo=qloj, pi=pij, clast=clast,
                                    parts=tuple(parts), first=gfirst[gi]):
                                self.wait(tt, en, exp_tk[c], tk_vh)
                                if first:
                                    self.wait(tt, en, war.get(("gx", gslot)))
                                i = None
                                for n, (plo, phi) in enumerate(parts):
                                    i = tt.matmul(gctx[gslot][0:65, plo:phi],
                                                  vh[:, 65 * c:65 * c + 65],
                                                  probs[pi][:, plo + glo - qlo:phi + glo - qlo],
                                                  start=(first and n == 0),
                                                  stop=(c == clast and n == len(parts) - 1),
                                                  skip_group_check=True)
                                gfirst[gi] = False
                                return self.inc(i, spe)
                            tk_ctx = self.T(cmm)
                            war[("pb", pij)] = tk_ctx
                            if cj == clast:
                                Tg = ghi - glo

                                def stg(v, en, gslot=gslot, glo=glo, Tg=Tg, tk_ctx=tk_ctx):
                                    self.wait(v, en, tk_ctx)
                                    i = v.tensor_copy(cstg[0:65, glo - 512:glo - 512 + Tg],
                                                      gctx[gslot][0:65, 0:Tg])
                                    return self.inc(i, sdve)
                                stg_tk[gi] = self.V(stg)
                                war[("gx", gslot)] = stg_tk[gi]

            def zmax(v, en):
                self.wait(v, en, *stg_tk.values())
                self.lk(v, en, v.tensor_scalar_max(zrh, zrh, 1e-35))
                return self.inc(v.reciprocal(zrh, zrh), sdve)
            tkz = self.V(zmax)
            tke = None
            for gi, (glo, ghi) in enumerate(groups5):
                Tg = ghi - glo

                def zb(tt, en, glo=glo, Tg=Tg):
                    self.wait(tt, en, tkz, war.get("zbp"))
                    i = tt.matmul(zbp[0:64, 0:Tg],
                                  self.cf32[64:65, CF_ONESROW:CF_ONESROW + 64],
                                  cstg[64:65, glo - 512:glo - 512 + Tg],
                                  start=True, stop=True)
                    return self.inc(i, spe)
                tkb = self.T(zb)

                def ctv(v, en, glo=glo, Tg=Tg):
                    self.wait(v, en, tkb)
                    i = v.tensor_tensor(ctev[0:64, glo - 512:glo - 512 + Tg],
                                        cstg[0:64, glo - 512:glo - 512 + Tg],
                                        zbp[0:64, 0:Tg], ALU.mult)
                    return self.inc(i, sdve)
                tke = self.V(ctv)
                war["zbp"] = tke

            def cst(y, en, hg=hg):
                self.wait(y, en, tke)
                return self.inc(y.dma_start(
                    out=self.cxdram[64 * hg:64 * hg + 64, :],
                    in_=ctev[0:64, :]), sst[0], 16)
            self.Y(cst)
            self.cbar()
            self.end_scope(mark_k)
            self.end_scope(mark_head)

    def f_mlp(self, l):
        spe, sdve, sact, spool = (self.sem(x) for x in ("pe", "dve", "act", "pool"))
        mark_il3 = self.scope()
        xh2 = [self.sbuf(f"xh2_{f}", [128, OWN], BF16) for f in range(NCH)]
        mark_a = self.scope()
        p = self.mk_ln_pools("m")
        xmb = [self.sbuf(f"mx{f}", [128, 512], BF16) for f in range(NCH)]
        t1 = [self.sbuf(f"mt{f}", [128, 512], F32) for f in range(NCH)]
        self.cbar()
        war_xh = None
        for g in range(4):
            scol = 641 + 512 * g

            def cast(v, en, scol=scol):
                self.wait(v, en, war_xh)
                i = None
                for f in range(NCH):
                    src = self.S[f][:, scol:scol + 512] if f < 4 else \
                        self.S[f][:, scol - 1:scol - 1 + 512]
                    i = v.tensor_copy(xmb[f][:], src)
                return self.inc(i, sdve)
            tk_cast = self.V(cast)
            tk_mub, tk_rb = self.ln_stats(xmb, 512, p, tk_cast)

            def app(v, en, scol=scol):
                self.wait(v, en, tk_rb)
                i = None
                for f in range(NCH):
                    srcv = self.S[f][:, scol:scol + 512] if f < 4 else \
                        self.S[f][:, scol - 1:scol - 1 + 512]
                    i = v.tensor_sub(t1[f][:], srcv, p["mub"][:, 0:512])
                self.lk(v, en, i)
                for f in range(NCH):
                    i = v.scalar_tensor_tensor(
                        out=t1[f][:], in0=t1[f][:],
                        scalar=self.vecs[:, VL * l + V_LG + f:VL * l + V_LG + f + 1],
                        in1=p["rbs"][:, 0:512], op0=ALU.mult, op1=ALU.mult)
                return self.inc(i, sdve)
            tk_ap = self.V(app)
            p["war"]["mub"] = tk_ap

            def xw(v, en, g=g):
                self.wait(v, en, tk_ap)
                i = None
                for f in range(NCH):
                    i = v.tensor_scalar_add(
                        xh2[f][:, 512 * g:512 * g + 512], t1[f][:],
                        self.vecs[:, VL * l + V_LB + f:VL * l + V_LB + f + 1])
                return self.inc(i, sdve)
            war_xh = self.V(xw)
        tk_xh = war_xh
        self.cbar()
        self.end_scope(mark_a)

        mark_b = self.scope()
        ublob = self.sbuf("ublob", [128, NHC * 512], BF16)
        w1s = [[self.sbuf(f"w1s{a}{i}", [128, 1024], BF16) for i in range(2)] for a in range(2)]
        w2s = [self.sbuf(f"w2s{i}", [128, NHC * 128], BF16) for i in range(2)]
        aps = [self.psum(f"aps{i}", [128, 512]) for i in range(2)]
        gps = [self.psum(f"gps{i}", [128, 512]) for i in range(2)]
        w2p = [self.psum(f"w2ps{i}", [128, 512]) for i in range(2)]
        gl = [self.sbuf(f"gl{i}", [128, 512], BF16) for i in range(2)]
        war = {}
        self.cbar()
        for g in range(4):
            scol = 641 + 512 * g
            for hc in range(NHC):
                bi = hc % 2

                def ldw(gg, en, hc=hc, bi=bi):
                    self.wait(gg, en, war.get(("w1", bi)))
                    swt_ = self.sem(f"w1ld{bi}")
                    self.inc(gg.dma_start(
                        out=w1s[0][bi][:].rearrange("p (c n) -> p c n", c=8),
                        in_=self.d_w1a[l].rearrange("(c p) n -> p c n", p=128)[:, :, 128 * hc:128 * hc + 128]),
                        swt_, 16)
                    return self.inc(gg.dma_start(
                        out=w1s[1][bi][:].rearrange("p (c n) -> p c n", c=8),
                        in_=self.d_w1g[l].rearrange("(c p) n -> p c n", p=128)[:, :, 128 * hc:128 * hc + 128]),
                        swt_, 16)
                tk_w = self.Y(ldw)

                def amm(tt, en, g=g, bi=bi):
                    self.wait(tt, en, tk_w, tk_xh, war.get(("aps", bi)), war.get(("gps", bi)))
                    for f in range(NCH):
                        tt.matmul(aps[bi][:], w1s[0][bi][:, 128 * f:128 * f + 128],
                                  xh2[f][:, 512 * g:512 * g + 512],
                                  start=(f == 0), stop=(f == NCH - 1))
                    i = None
                    for f in range(NCH):
                        i = tt.matmul(gps[bi][:], w1s[1][bi][:, 128 * f:128 * f + 128],
                                      xh2[f][:, 512 * g:512 * g + 512],
                                      start=(f == 0), stop=(f == NCH - 1))
                    return self.inc(i, spe)
                tk_mm = self.T(amm)
                war[("w1", bi)] = tk_mm

                def gel(cc, en, hc=hc, bi=bi):
                    self.wait(cc, en, tk_mm, war.get(("gl", bi)))
                    i = cc.activation(out=gl[bi][:], in_=gps[bi][:], func=AF.Gelu,
                                      bias=self.vecs[:, VL * l + V_B1G + hc:VL * l + V_B1G + hc + 1])
                    return self.inc(i, sact)
                tk_g = self.C(gel)
                war[("gps", bi)] = tk_g

                def uf(v, en, hc=hc, bi=bi):
                    self.wait(v, en, tk_g, war.get(("ub", hc)))
                    i = v.scalar_tensor_tensor(
                        out=ublob[:, 512 * hc:512 * hc + 512], in0=aps[bi][:],
                        scalar=self.vecs[:, VL * l + V_B1A + hc:VL * l + V_B1A + hc + 1],
                        in1=gl[bi][:], op0=ALU.add, op1=ALU.mult)
                    return self.inc(i, sdve)
                tk_u = self.V(uf)
                war[("aps", bi)] = tk_u
                war[("gl", bi)] = tk_u
                war[("u", hc)] = tk_u
            tk_ulast = tk_u
            for f in range(NCH):
                bi = f % 2

                def ldw2(gg, en, f=f, bi=bi):
                    self.wait(gg, en, war.get(("w2", bi)))
                    swt_ = self.sem(f"w2ld{bi}")
                    return self.inc(gg.dma_start(
                        out=w2s[bi][:].rearrange("p (c n) -> p c n", c=NHC),
                        in_=self.d_w2p[l].rearrange("(c p) n -> p c n", p=128)[:, :, 128 * f:128 * f + 128]),
                        swt_, 16)
                tk_w2 = self.Y(ldw2)

                def wmm(tt, en, f=f, bi=bi):
                    self.wait(tt, en, tk_w2, tk_ulast, war.get(("w2p", bi)))
                    i = None
                    for hc in range(NHC):
                        i = tt.matmul(w2p[bi][:], w2s[bi][:, 128 * hc:128 * hc + 128],
                                      ublob[:, 512 * hc:512 * hc + 512],
                                      start=(hc == 0), stop=(hc == NHC - 1))
                    return self.inc(i, spe)
                tk_m = self.T(wmm)
                war[("w2", bi)] = tk_m

                def ev(v, en, f=f, bi=bi, scol=scol):
                    self.wait(v, en, tk_m)
                    i = v.scalar_tensor_tensor(
                        out=self.S[f][:, scol:scol + 512], in0=w2p[bi][:],
                        scalar=self.vecs[:, VL * l + V_B2 + f:VL * l + V_B2 + f + 1],
                        in1=self.S[f][:, scol:scol + 512], op0=ALU.add, op1=ALU.add)
                    return self.inc(i, sdve)
                war[("w2p", bi)] = self.V(ev)
            for hc in range(NHC):
                war[("ub", hc)] = war[("w2", (NCH - 1) % 2)]
        self.cbar()
        self.end_scope(mark_b)
        self.end_scope(mark_il3)

    def ph_x1y2(self):
        mark = self.scope()
        sld, sdve, sact = (self.sem(x) for x in ("xld", "dve", "act"))
        xt = [self.sbuf(f"xx{f}", [128, 512], F32) for f in range(NCH)]
        x1t = [self.sbuf(f"x1t{f}", [128, 512], F32) for f in range(NCH)]
        xb = [self.sbuf(f"xxb{f}", [128, 512], BF16) for f in range(NCH)]
        y2b = [self.sbuf(f"y2b{f}", [128, 512], BF16) for f in range(NCH)]
        t1 = [self.sbuf(f"xt1{f}", [128, 512], F32) for f in range(NCH)]
        p = self.mk_ln_pools("x1")
        self.full_barrier()
        war = {}
        for g in range(4):
            scol = 641 + 512 * g

            def ld(y, en, g=g, scol=scol):
                self.wait(y, en, war.get("xt"))
                i = None
                for f in range(NCH):
                    i = self.inc(y.dma_start(out=xt[f][:],
                                             in_=self.d_x0w[128 * f:128 * f + 128, scol:scol + 512]),
                                 sld, 16)
                return i
            tk_ld = self.Y(ld)

            def x1f(v, en, scol=scol):
                self.wait(v, en, tk_ld, war.get("x1t"), war.get("x1st"))
                i = None
                for f in range(NCH):
                    i = v.scalar_tensor_tensor(out=x1t[f][:], in0=self.S[f][:, scol:scol + 512],
                                               scalar=0.5, in1=xt[f][:],
                                               op0=ALU.mult, op1=ALU.add)
                self.lk(v, en, i)
                for f in range(NCH):
                    i = v.tensor_copy(xb[f][:], x1t[f][:])
                return self.inc(i, sdve)
            tk_x1 = self.V(x1f)
            war["xt"] = tk_x1

            def st1(y, en, g=g):
                self.wait(y, en, tk_x1)
                i = None
                for f in range(NCH):
                    i = self.inc(y.dma_start(out=self.d_x1[128 * f:128 * f + 128,
                                                          512 * g:512 * g + 512],
                                             in_=x1t[f][:]), self.sem("x1st"), 16)
                return i
            war["x1st"] = self.Y(st1)
            tk_mub, tk_rb = self.ln_stats(xb, 512, p, tk_x1)

            def app(v, en):
                self.wait(v, en, tk_rb, war.get("y2b"))
                i = None
                for f in range(NCH):
                    i = v.tensor_sub(t1[f][:], x1t[f][:], p["mub"][:, 0:512])
                self.lk(v, en, i)
                for f in range(NCH):
                    i = v.tensor_mul(y2b[f][:], t1[f][:], p["rbs"][:, 0:512])
                return self.inc(i, sdve)
            tk_y2 = self.V(app)
            war["x1t"] = tk_y2
            p["war"]["mub"] = tk_y2

            def st2(y, en, g=g):
                self.wait(y, en, tk_y2)
                i = None
                for f in range(NCH):
                    i = self.inc(y.dma_start(out=self.e2a_in[128 * f:128 * f + 128,
                                                             512 * g:512 * g + 512],
                                             in_=y2b[f][:]), self.sem("e2ast"), 16)
                return i
            war["y2b"] = self.Y(st2)

        def coll(gg, en):
            self.barrier_from(gg, en, "e2ast")
            return self.inc(gg.collective_compute(
                "AllGather", ALU.bypass, replica_groups=self.rg,
                ins=[self.e2a_in.opt()], outs=[self.e2a_out.opt()]), self.sem("cc"), 1)
        self.tk_e2a = self.G(coll)
        self.full_barrier()
        self.end_scope(mark)

    def ph_A(self):
        spe, sdve, sact, swt, sld, spool = (self.sem(x) for x in
                                            ("pe", "dve", "act", "awld", "ald", "pool"))
        sst = [self.sem("e2bst0"), self.sem("e2bst1")]
        mark = self.scope()
        y2sb = [self.sbuf(f"ya{f}", [128, SEQ], BF16) for f in range(NCH)]
        mvb = self.sbuf("mv_sb", [128, SEQ], BF16)
        bvr = self.sbuf("bvr", [1, 512], BF16)
        vblA = self.sbuf("vblA", [128, 32 * 520], BF16)
        bkV = self.sbuf("bkV", [128, 8 * 65], BF16)
        ctxA = [self.sbuf(f"cA{c}", [128, SEQ], BF16) for c in range(4)]

        def ldy(y, en):
            self.wait(y, en, self.tk_e2a)
            i = None
            for f in range(NCH):
                i = self.inc(y.dma_start(out=y2sb[f][:, 0:OWN],
                                         in_=self.e2a_out[128 * f:128 * f + 128, :]), sld, 16)
                i = self.inc(y.dma_start(out=y2sb[f][:, OWN:SEQ],
                                         in_=self.e2a_out[D + 128 * f:D + 128 * f + 128, :]),
                             sld, 16)
            i = self.inc(y.dma_start(out=mvb[:], in_=self.d_maskv), sld, 16)
            return i
        tk_y2 = self.Y(ldy)

        def ldbv(g, en):
            return self.inc(g.dma_start(out=bvr[:], in_=self.d_bv512), self.sem("awld"), 16)
        tk_bv = self.Y(ldbv)

        mark_v = self.scope()
        bki = self.sbuf("bki", [128, 32 * 128], BF16)
        wvsb = [self.sbuf(f"wva{f}", [128, 512], BF16) for f in range(NCH)]

        def ldbki(y, en):
            return self.inc(y.dma_start(out=bki[:], in_=self.d_blkind), self.sem("bkld"), 16)
        tk_bki = self.Y(ldbki)

        def ldwv(g, en):
            i = None
            for f in range(NCH):
                i = self.inc(g.dma_start(out=wvsb[f][:],
                                         in_=self.d_wvT[128 * f:128 * f + 128, :]), swt, 16)
            return i
        tk_wv = self.Y(ldwv)

        def vinit(v, en):
            return self.inc(v.memset(vblA[:], 1.0), sdve)
        tk_vi = self.V(vinit)
        vps = [self.psum(f"avp{i}", [128, 512]) for i in range(2)]
        warv = [None, None]
        for tc in range(32):
            bi = tc % 2

            def vmm(t, en, tc=tc, bi=bi):
                self.wait(t, en, tk_y2, tk_wv, tk_bv, warv[bi])
                for f in range(NCH):
                    t.matmul(vps[bi][:], y2sb[f][:, 128 * tc:128 * tc + 128], wvsb[f][:],
                             start=(f == 0), stop=False)
                i = t.matmul(vps[bi][:], self.cbf[0:1, CBF_ONESROW:CBF_ONESROW + 128],
                             bvr[:], start=False, stop=True)
                return self.inc(i, spe)
            tkv = self.T(vmm)

            def vev(v, en, tc=tc, bi=bi):
                self.wait(v, en, tkv, tk_vi)
                dst = vblA.rearrange("p (c h e) -> p c h e", c=32, h=FH)[:, tc, :, 0:64]
                src = vps[bi][:].rearrange("p (h e) -> p h e", h=FH)
                return self.inc(v.tensor_copy(dst, src), sdve)
            warv[bi] = self.V(vev)
        tk_vbl = warv[31 % 2]
        bkp = [self.psum(f"bkp{i}", [128, 65]) for i in range(2)]
        warb = [None, None]
        for ha in range(FH):
            bi = ha % 2

            def bmm(t, en, ha=ha, bi=bi):
                self.wait(t, en, tk_vbl, tk_bki, warb[bi])
                i = None
                for tc in range(32):
                    i = t.matmul(bkp[bi][:], bki[:, 128 * tc:128 * tc + 128],
                                 vblA[:, 520 * tc + 65 * ha:520 * tc + 65 * ha + 65],
                                 start=(tc == 0), stop=(tc == 31))
                return self.inc(i, spe)
            tkb = self.T(bmm)

            def bev(v, en, ha=ha, bi=bi):
                self.wait(v, en, tkb)
                return self.inc(v.tensor_copy(bkV[:, 65 * ha:65 * ha + 65],
                                              bkp[bi][:]), sdve)
            warb[bi] = self.V(bev)
        self.cbar()
        self.end_scope(mark_v)

        awar = {}
        for hp in range(4):
            mark_hp = self.scope()
            kaugA = [self.sbuf(f"kA{i}", [81, SEQ], BF16) for i in range(2)]
            kmx2 = self.sbuf("kmx2", [33, 1], F32)
            kred = self.sbuf("kred", [33, 1], F32)
            kmax = self.sbuf("kmax", [33, 1], F32)
            wksb = self.sbuf("wksb", [128, 1024], BF16)
            wqsb = self.sbuf("wqsb", [128, 1024], BF16)
            qaugA = [self.sbuf(f"qA{i}", [81, 512], BF16) for i in range(2)]
            qnorm = self.sbuf("qnorm", [33, 512], F32)
            ksq = [self.sbuf(f"ksq{i}", [128, 512], BF16) for i in range(2)]
            smv = self.sbuf("smv", [128, 512], F32)
            vpb = self.sbuf("vpb", [128, 512], BF16)
            probsA = [self.sbuf(f"pA{i}", [128, 512], BF16) for i in range(3)]
            z2 = self.sbuf("az2", [1, 512], F32)
            zr = self.sbuf("azr", [1, 512], F32)
            zbs = self.sbuf("azbs", [64, 512], F32)

            def ldk(g, en, hp=hp):
                self.wait(g, en, awar.get("hp"))
                i = self.inc(g.dma_start(
                    out=wksb[:].rearrange("p (c n) -> p c n", c=8),
                    in_=self.d_wkT.rearrange("(c p) n -> p c n", p=128)[:, :, 128 * hp:128 * hp + 128]),
                    swt, 16)
                i = self.inc(g.dma_start(
                    out=wqsb[:].rearrange("p (c n) -> p c n", c=8),
                    in_=self.d_wqT.rearrange("(c p) n -> p c n", p=128)[:, :, 128 * hp:128 * hp + 128]),
                    swt, 16)
                return i
            tk_wk = self.Y(ldk)

            def kinit(y, en):
                self.wait(y, en, awar.get("hp"))
                i = None
                for hl in range(2):
                    i = self.inc(y.dma_start(out=kaugA[hl][65:81, :], in_=self.d_kaugc),
                                 self.sem("akld"), 16)
                return i
            tk_kc = self.Y(kinit)

            def kini2(v, en):
                i = None
                for hl in range(2):
                    i = v.memset(kaugA[hl][64:65, :], 1.0)
                return self.inc(i, sdve)
            tk_k1 = self.V(kini2)

            def qinit(y, en):
                self.wait(y, en, awar.get("hp"))
                i = None
                for hl in range(2):
                    i = self.inc(y.dma_start(out=qaugA[hl][65:81, :], in_=self.d_qaugc),
                                 self.sem("aqld"), 16)
                return i
            tk_qc = self.Y(qinit)

            mark_kb = self.scope()
            kps = [self.psum(f"akp{i}", [128, 512]) for i in range(2)]
            nkp = [self.psum(f"ank{i}", [33, 512]) for i in range(2)]
            self.cbar()
            war = {}
            for g8 in range(8):
                bi = g8 % 2

                def kmm(t, en, g8=g8, bi=bi):
                    self.wait(t, en, tk_y2, tk_wk, war.get(("kps", bi)))
                    i = None
                    for f in range(NCH):
                        i = t.matmul(kps[bi][:], wksb[:, 128 * f:128 * f + 128],
                                     y2sb[f][:, 512 * g8:512 * g8 + 512],
                                     start=(f == 0), stop=(f == NCH - 1))
                    return self.inc(i, spe)
                tk1 = self.T(kmm)

                def kev(v, en, g8=g8, bi=bi, hp=hp):
                    self.wait(v, en, tk1, tk_k1, war.get(("ksq", bi)))
                    i = None
                    for hl in range(2):
                        i = v.tensor_scalar_add(
                            kaugA[hl][0:64, 512 * g8:512 * g8 + 512],
                            kps[bi][64 * hl:64 * hl + 64, :],
                            self.vecs[64 * hl:64 * hl + 64, V_BK + hp:V_BK + hp + 1])
                    return self.inc(i, sdve)
                tk2a = self.V(kev)

                def ksqf(g_, en, g8=g8, bi=bi):
                    self.wait(g_, en, tk2a, war.get(("ksq", bi)))
                    i = None
                    for hl in range(2):
                        i = g_.tensor_mul(ksq[bi][64 * hl:64 * hl + 64, :],
                                          kaugA[hl][0:64, 512 * g8:512 * g8 + 512],
                                          kaugA[hl][0:64, 512 * g8:512 * g8 + 512])
                    return self.inc(i, spool)
                tk2 = self.G(ksqf)

                def nmm(t, en, bi=bi):
                    self.wait(t, en, tk2, war.get(("nkp", bi)))
                    return self.inc(t.matmul(nkp[bi][0:33, :],
                                             self.cbf[:, CBF_HS33:CBF_HS33 + 33],
                                             ksq[bi][:], start=True, stop=True), spe)
                tk3 = self.T(nmm)
                war[("kps", bi)] = tk2
                war[("ksq", bi)] = tk3

                def nred(v, en, g8=g8, bi=bi):
                    self.wait(v, en, tk3, war.get("kred"))
                    self.lk(v, en, v.reduce_max(kred[:], nkp[bi][0:33, :],
                                                axis=mybir.AxisListType.X))
                    if g8 == 0:
                        i = v.tensor_copy(kmx2[:], kred[:])
                    else:
                        i = v.tensor_max(kmx2[:], kmx2[:], kred[:])
                    return self.inc(i, sdve)
                war[("nkp", bi)] = self.V(nred)
                war["kred"] = war[("nkp", bi)]

            def kmx(c, en):
                self.wait(c, en, war["kred"])
                return self.inc(c.activation(out=kmax[:], in_=kmx2[:],
                                             func=AF.Sqrt, scale=1.0), sact)
            tk_sq1 = self.C(kmx)

            def kneg(v, en):
                self.wait(v, en, tk_sq1)
                return self.inc(v.tensor_scalar_mul(kmax[:], kmax[:], -1.0), sdve)
            tk_kmx = self.V(kneg)
            self.cbar()
            self.end_scope(mark_kb)

            mark_qa = self.scope()
            qps = [self.psum(f"aqp{i}", [128, 512]) for i in range(2)]
            nzp = self.psum("anzp", [128, 512])
            nqp = nzp
            skp = [self.psum(f"ask{i}", [128, 512]) for i in range(3)]
            gcx = [self.psum(f"agx{i}", [65, 512]) for i in range(2)]
            zbp = nzp
            qsq = [self.sbuf(f"qsq{i}", [128, 512], BF16) for i in range(2)]
            self.cbar()
            war = {}
            for g in range(8):
                bi = g % 2

                def qmm(t, en, g=g, bi=bi):
                    self.wait(t, en, tk_y2, tk_wk, war.get(("qps", bi)))
                    i = None
                    for f in range(NCH):
                        i = t.matmul(qps[bi][:], wqsb[:, 128 * f:128 * f + 128],
                                     y2sb[f][:, 512 * g:512 * g + 512],
                                     start=(f == 0), stop=(f == NCH - 1))
                    return self.inc(i, spe)
                tk1 = self.T(qmm)

                def qev(v, en, g=g, bi=bi, hp=hp):
                    self.wait(v, en, tk1, tk_qc, war.get("qaug"), war.get("qsqf"))
                    i = None
                    for hl in range(2):
                        i = v.tensor_scalar(
                            out=qaugA[hl][0:64, :],
                            in0=qps[bi][64 * hl:64 * hl + 64, :],
                            scalar1=0.125, op0=ALU.mult,
                            scalar2=self.bq8[64 * hl:64 * hl + 64, hp:hp + 1],
                            op1=ALU.add)
                    return self.inc(i, sdve)
                tk2b = self.V(qev)

                def qsqf(g_, en, bi=bi):
                    self.wait(g_, en, tk2b, war.get(("qsq", bi)))
                    i = None
                    for hl in range(2):
                        i = g_.tensor_mul(qsq[bi][64 * hl:64 * hl + 64, :],
                                          qaugA[hl][0:64, :], qaugA[hl][0:64, :])
                    return self.inc(i, spool)
                tk2 = self.G(qsqf)
                war["qsqf"] = tk2

                def nmm(t, en, bi=bi):
                    self.wait(t, en, tk2, war.get("nqp"))
                    return self.inc(t.matmul(nqp[0:33, :], self.cbf[:, CBF_HS33:CBF_HS33 + 33],
                                             qsq[bi][:], start=True, stop=True), spe)
                tk3 = self.T(nmm)
                war[("qps", bi)] = tk2
                war[("qsq", bi)] = tk3

                def sqf(c, en):
                    self.wait(c, en, tk3)
                    return self.inc(c.activation(out=qnorm[:], in_=nqp[0:33, :],
                                                 func=AF.Sqrt), sact)
                tk4 = self.C(sqf)
                war["nqp"] = tk4

                def mrow(v, en, bi=bi):
                    self.wait(v, en, tk4, tk_kmx)
                    i = None
                    for hl in range(2):
                        i = v.tensor_scalar_mul(qaugA[hl][64:65, :],
                                                qnorm[32 * hl:32 * hl + 1, :],
                                                kmax[32 * hl:32 * hl + 1, 0:1])
                    return self.inc(i, sdve)
                tk_mr = self.V(mrow)

                last_aug_reader = []
                for hl in range(2):
                    ha = 2 * hp + hl
                    gfirst = True
                    tk_ctx = None
                    smm_tk = {}
                    exp_tkA = {}
                    for ck in range(4 * g + 4):
                        sbi = ck % 3
                        diag = (ck >= 4 * g)
                        rows = 81 if diag else 65

                        def smm(t, en, ck=ck, sbi=sbi, hl=hl, rows=rows):
                            self.wait(t, en, tk_mr, tk_k1, war.get(("skp", sbi)))
                            return self.inc(t.matmul(
                                skp[sbi][:], kaugA[hl][0:rows, 128 * ck:128 * ck + 128],
                                qaugA[hl][0:rows, :], start=True, stop=True), spe)
                        smm_tk[ck] = self.T(smm)

                        nck = 4 * g + 4
                        if (ck + 1) % 3 != 0 and ck != nck - 1:
                            continue
                        cks = tuple(range(3 * (ck // 3), ck + 1))

                        def pexp(c, en, cks=cks):
                            for cj in cks:
                                self.wait(c, en, smm_tk[cj], war.get(("pA", cj % 3)))
                            i = None
                            for cj in cks:
                                i = c.activation(out=probsA[cj % 3][:],
                                                 in_=skp[cj % 3][:], func=AF.Exp)
                            return self.inc(i, sact)
                        tk_p = self.C(pexp)
                        for cj in cks:
                            war[("skp", cj % 3)] = tk_p
                            exp_tkA[cj] = tk_p

                        for cj in cks:
                            def cmm(t, en, ck=cj, pi=cj % 3, ha=ha,
                                    gfirst=gfirst, hl=hl):
                                self.wait(t, en, exp_tkA[ck], tk_vbl)
                                if gfirst:
                                    self.wait(t, en, war.get(("gcx", hl)))
                                return self.inc(t.matmul(
                                    gcx[hl][0:65, :],
                                    vblA[:, 520 * ck + 65 * ha:520 * ck + 65 * ha + 65],
                                    probsA[pi][:], start=gfirst, stop=False,
                                    skip_group_check=True), spe)
                            tk_ctx = self.T(cmm)
                            war[("pA", cj % 3)] = tk_ctx
                            gfirst = False
                    vsl = (4 * g + 4) % 3

                    def vmm(t, en, g=g, hl=hl, vsl=vsl):
                        self.wait(t, en, tk_mr, war.get(("skp", vsl)))
                        return self.inc(t.matmul(
                            skp[vsl][:], self.cbf[64:65, CBF_ONES64:CBF_ONES64 + 128],
                            qaugA[hl][64:65, :], start=True, stop=True), spe)
                    tk_v1 = self.T(vmm)

                    def vmsk(v, en, g=g, vsl=vsl):
                        self.wait(v, en, tk_v1, war.get("smv"))
                        return self.inc(v.tensor_tensor(
                            smv[:], skp[vsl][:],
                            mvb[:, 512 * g:512 * g + 512], ALU.add), sdve)
                    tk_v2 = self.V(vmsk)
                    war[("skp", vsl)] = tk_v2

                    def vexp(c, en):
                        self.wait(c, en, tk_v2, war.get("vpb"))
                        return self.inc(c.activation(out=vpb[:], in_=smv[:], func=AF.Exp), sact)
                    tk_v3 = self.C(vexp)
                    war["smv"] = tk_v3

                    def vcm(t, en, ha=ha, hl=hl):
                        self.wait(t, en, tk_v3)
                        return self.inc(t.matmul(
                            gcx[hl][0:65, :], bkV[:, 65 * ha:65 * ha + 65], vpb[:],
                            start=False, stop=True, skip_group_check=True), spe)
                    tk_v4 = self.T(vcm)
                    war["vpb"] = tk_v4

                    def zf(v, en, hl=hl):
                        self.wait(v, en, tk_v4, war.get("zch"))
                        self.lk(v, en, v.tensor_scalar_max(z2[:], gcx[hl][64:65, :], 1e-35))
                        return self.inc(v.reciprocal(zr[:], z2[:]), sdve)
                    tkz = self.V(zf)

                    def zb(t, en):
                        self.wait(t, en, tkz, war.get("zbp"))
                        return self.inc(t.matmul(zbp[0:64, :],
                                                 self.cf32[0:1, CF_ONESROW:CF_ONESROW + 64],
                                                 zr[0:1, :], start=True, stop=True), spe)
                    tkb2 = self.T(zb)
                    war["zch"] = tkb2

                    def zc(v, en):
                        self.wait(v, en, tkb2, war.get("zbs"))
                        return self.inc(v.tensor_copy(zbs[:], zbp[0:64, :]), sdve)
                    tkc2 = self.V(zc)
                    war["zbp"] = tkc2
                    war["nqp"] = tkc2

                    def cev(v, en, ha=ha, g=g, hl=hl):
                        self.wait(v, en, tkc2)
                        i = v.tensor_tensor(
                            ctxA[ha // 2][(ha % 2) * 64:(ha % 2) * 64 + 64,
                                          512 * g:512 * g + 512],
                            gcx[hl][0:64, :], zbs[:], ALU.mult)
                        return self.inc(i, sdve)
                    tke = self.V(cev)
                    war[("gcx", hl)] = tke
                    war["zbs"] = tke
                    last_aug_reader.append(tk_v4)
                war["qaug"] = last_aug_reader[-1]
            awar["hp"] = war["qaug"]
            self.cbar()
            self.end_scope(mark_qa)
            self.end_scope(mark_hp)

        mark_wo = self.scope()
        wotsb = [self.sbuf(f"wot{c}", [128, 1024], BF16) for c in range(4)]
        pot = [self.sbuf(f"pot{i}", [128, 512], F32) for i in range(2)]
        wop = [self.psum(f"awo{i}", [128, 512]) for i in range(2)]

        def ldwo(g, en):
            i = None
            for c in range(4):
                i = self.inc(g.dma_start(out=wotsb[c][:],
                                         in_=self.d_woT[128 * c:128 * c + 128, :]), swt, 16)
            return i
        tk_wo = self.Y(ldwo)
        self.cbar()
        war = {}
        self.tk_e2b = [None] * 4
        for gn, g in enumerate([0, 4, 1, 5, 2, 6, 3, 7]):
            for f in range(NCH):
                bi = (gn * NCH + f) % 2

                def womm(t, en, g=g, f=f, bi=bi):
                    self.wait(t, en, tk_wo, war.get(("wop", bi)))
                    self.barrier_from(t, en, "dve")
                    i = None
                    for kc in range(4):
                        i = t.matmul(wop[bi][:], wotsb[kc][:, 128 * f:128 * f + 128],
                                     ctxA[kc][:, 512 * g:512 * g + 512],
                                     start=(kc == 0), stop=(kc == 3))
                    return self.inc(i, spe)
                tkw = self.T(womm)

                def wev(v, en, bi=bi):
                    self.wait(v, en, tkw, war.get(("pot", bi)))
                    return self.inc(v.tensor_copy(pot[bi][:], wop[bi][:]), sdve)
                tkv = self.V(wev)
                war[("wop", bi)] = tkv

                def wst(y, en, g=g, f=f, bi=bi):
                    self.wait(y, en, tkv)
                    return self.inc(y.dma_start(
                        out=self.e2bi[g % 4][128 * f:128 * f + 128,
                                            (g // 4) * 512:(g // 4) * 512 + 512],
                        in_=pot[bi][:]), sst[bi], 16)
                war[("pot", bi)] = self.Y(wst)
            if g >= 4:
                def coll(gg, en, j=g % 4):
                    self.barrier_from(gg, en, "e2bst0", "e2bst1")
                    return self.inc(gg.collective_compute(
                        "AllReduce", ALU.add, replica_groups=self.rg,
                        ins=[self.e2bi[j].opt()], outs=[self.e2bo[j].opt()]),
                        self.sem("cc"), 1)
                self.tk_e2b[g % 4] = self.G(coll)
        self.full_barrier()
        self.end_scope(mark)

    def ph_x2y3(self):
        mark = self.scope()
        sld, sdve, sst = (self.sem(x) for x in ("x2ld", "dve", "x2st"))
        x1t = [self.sbuf(f"zx{f}", [128, 512], F32) for f in range(NCH)]
        sa = [self.sbuf(f"za{f}", [128, 512], F32) for f in range(NCH)]
        sb = [self.sbuf(f"zb{f}", [128, 512], F32) for f in range(NCH)]
        x2t = [self.sbuf(f"zc{f}", [128, 512], F32) for f in range(NCH)]
        xb = [self.sbuf(f"zd{f}", [128, 512], BF16) for f in range(NCH)]
        t1 = x1t
        p = self.mk_ln_pools("x2")
        war = {}
        for g in range(4):
            scol = 641 + 512 * g

            def ld(y, en, g=g):
                self.wait(y, en, self.tk_e2b[g], war.get("in"), war.get("x2t"))
                i = None
                for f in range(NCH):
                    r = slice(128 * f, 128 * f + 128)
                    i = self.inc(y.dma_start(out=x1t[f][:],
                                             in_=self.d_x1[r, 512 * g:512 * g + 512]), sld, 16)
                    i = self.inc(y.dma_start(out=sa[f][:],
                                             in_=self.e2bo[g][r, 0:512]), sld, 16)
                    i = self.inc(y.dma_start(out=sb[f][:],
                                             in_=self.e2bo[g][r, 512:1024]), sld, 16)
                return i
            tk_ld = self.Y(ld)

            def x2f(v, en, g=g):
                self.wait(v, en, tk_ld, war.get("x2t"), war.get("x2st"))
                i = None
                for f in range(NCH):
                    i = v.scalar_tensor_tensor(out=x2t[f][:], in0=sb[f][:],
                                               scalar=self.vecs[:, V_HV:V_HV + 1], in1=x1t[f][:],
                                               op0=ALU.mult, op1=ALU.add)
                self.lk(v, en, i)
                for f in range(NCH):
                    i = v.scalar_tensor_tensor(out=x2t[f][:], in0=sa[f][:],
                                               scalar=self.hvi[:, 0:1], in1=x2t[f][:],
                                               op0=ALU.mult, op1=ALU.add)
                self.lk(v, en, i)
                for f in range(NCH):
                    i = v.tensor_scalar_add(x2t[f][:], x2t[f][:],
                                            self.vecs[:, V_BO + f:V_BO + f + 1])
                self.lk(v, en, i)
                for f in range(NCH):
                    i = v.tensor_copy(xb[f][:], x2t[f][:])
                return self.inc(i, sdve)
            tk_x2 = self.V(x2f)
            war["in"] = tk_x2

            def st1(y, en, g=g):
                self.wait(y, en, tk_x2)
                i = None
                for f in range(NCH):
                    i = self.inc(y.dma_start(out=self.d_x2[128 * f:128 * f + 128,
                                                           512 * g:512 * g + 512],
                                             in_=x2t[f][:]), sst, 16)
                return i
            war["x2st"] = self.Y(st1)
            tk_mub, tk_rb = self.ln_stats(xb, 512, p, tk_x2)

            def app(v, en, scol=scol):
                self.wait(v, en, tk_rb)
                i = None
                for f in range(NCH):
                    i = v.tensor_sub(t1[f][:], x2t[f][:], p["mub"][:, 0:512])
                self.lk(v, en, i)
                for f in range(NCH):
                    i = v.tensor_mul(self.S[f][:, scol:scol + 512], t1[f][:], p["rbs"][:, 0:512])
                return self.inc(i, sdve)
            tk_y3 = self.V(app)
            war["x2t"] = tk_y3
            p["war"]["mub"] = tk_y3
        self.full_barrier()
        self.end_scope(mark)
        self.exchange_halo(self.e3_in, self.e3_out)

    def ph_out(self):
        mark = self.scope()
        sld, sdve, sst = (self.sem(x) for x in ("x2ld", "dve", "yst"))
        x2t = [self.sbuf(f"ox{f}", [128, 512], F32) for f in range(NCH)]
        ot = [self.sbuf(f"oo{f}", [128, 512], F32) for f in range(NCH)]
        self.full_barrier()
        war = {}
        for g in range(4):
            scol = 641 + 512 * g

            def ld(y, en, g=g):
                self.wait(y, en, war.get("x2t"))
                i = None
                for f in range(NCH):
                    i = self.inc(y.dma_start(out=x2t[f][:],
                                             in_=self.d_x2[128 * f:128 * f + 128,
                                                           512 * g:512 * g + 512]), sld, 16)
                return i
            tk_ld = self.Y(ld)

            def of(v, en, scol=scol):
                self.wait(v, en, tk_ld, war.get("ot"))
                i = None
                for f in range(NCH):
                    i = v.scalar_tensor_tensor(out=ot[f][:], in0=self.S[f][:, scol:scol + 512],
                                               scalar=0.5, in1=x2t[f][:],
                                               op0=ALU.mult, op1=ALU.add)
                return self.inc(i, sdve)
            tk_o = self.V(of)
            war["x2t"] = tk_o

            def st(y, en, g=g):
                self.wait(y, en, tk_o)
                i = None
                for f in range(NCH):
                    i = self.inc(y.dma_start(out=self.d_y[128 * f:128 * f + 128,
                                                          512 * g:512 * g + 512],
                                             in_=ot[f][:]), sst, 16)
                return i
            war["ot"] = self.Y(st)
        self.end_scope(mark)

    def ph_dump(self):
        sst = self.sem("dbgst")
        self.full_barrier()

        def dp(y, en):
            i = None
            for f in range(NCH):
                i = self.inc(y.dma_start(out=self.d_dbg[f], in_=self.S[f][:]), sst, 16)
            return i
        self.Y(dp)

    def ph_fin(self):
        self.full_barrier()



def build_program():
    return MK().build()


def kernel(**inputs):
    nc = build_program()
    in_maps = [core_inputs(inputs, c) for c in range(8)]
    res = run_bass_kernel_spmd(nc, in_maps, core_ids=list(range(8)))
    out = np.zeros((B, SEQ, D), NF32)
    for c in range(8):
        b, par = c // 2, c % 2
        y = np.asarray(res.results[c]["y"], NF32)
        out[b, par * OWN:par * OWN + OWN, :] = y.T
    if DUMP:
        kernel.dumps = [{f: np.asarray(res.results[c][f"dbg{f}"], NF32) for f in range(NCH)}
                        for c in range(8)]
        kernel.vdram = [np.asarray(res.results[c]["vdram"]) for c in range(8)]
        kernel.cxdram = [np.asarray(res.results[c]["cxdram"]) for c in range(8)]
        kernel.raw = res
    return out


if __name__ == "__main__":
    mkk = MK()
    nc = mkk.build()
    n = sum(len(b.instructions) for fn in nc.m.functions for b in fn.blocks)
    print(f"built ok: stage={STAGE} instructions={n}")
